# version 14
# baseline (speedup 1.0000x reference)
import contextlib
import numpy as np

import concourse.bass as bass
import concourse.mybir as mybir
from concourse.bass_utils import run_bass_kernel_spmd

F32 = mybir.dt.float32
BF16 = mybir.dt.bfloat16
FP16 = mybir.dt.float16
Alu = mybir.AluOpType
Act = mybir.ActivationFunctionType
Ax = mybir.AxisListType

ENGINES = ("tensor", "vector", "scalar", "gpsimd", "sync")
QUEUES = {"qS": "sync", "qA": "scalar", "qG": "gpsimd"}
DMA_POOL = 10

S, D, DI, H = 2048, 1024, 2816, 8
TT = 128
NTT = S // TT
DC = D // 128
IC = DI // 128
DHQK, DHV = 64, 128
LN_EPS = 1e-5
ATT_EPS = 1e-6
CAP = 15.0
LN8 = float(np.log(8.0))


class Buf:
    __slots__ = ("w", "rs", "name")

    def __init__(self, name=""):
        self.w = None
        self.rs = []
        self.name = name


class B:

    def __init__(self, nc, stack):
        self.nc = nc
        self.stack = stack
        self.sems = {}
        self.counts = {}
        self.dma_sems = {}
        self.dma_rr = {}
        for name in ("T", "V", "Sx", "G"):
            self._mk(name)
        for q in QUEUES:
            self.dma_sems[q] = []
            self.dma_rr[q] = 0
            for k in range(DMA_POOL):
                n = f"{q}_{k}"
                self._mk(n)
                self.dma_sems[q].append(n)
        self.engine_sem = {"tensor": "T", "vector": "V", "scalar": "Sx", "gpsimd": "G"}
        self.streams = {e: [] for e in ENGINES}
        self.is_dma_sem = {s for q in QUEUES for s in self.dma_sems[q]}

    def _mk(self, name):
        self.sems[name] = self.stack.enter_context(self.nc.semaphore(name))
        self.counts[name] = 0

    def _fix(self, tok):
        s, v = tok
        if s in self.is_dma_sem:
            return (s, self.counts[s])
        return tok

    def op(self, eng, fn, reads=(), writes=(), extra=(), sem=None, amt=1, track=True):
        waits = []
        for b in reads:
            if b.w is not None:
                waits.append(self._fix(b.w))
        for b in writes:
            for t_ in b.rs:
                waits.append(self._fix(t_))
            if b.w is not None:
                waits.append(self._fix(b.w))
        waits.extend(self._fix(t_) for t_ in extra)
        tok = None
        if track:
            s = sem if sem is not None else self.engine_sem[eng]
            self.counts[s] += amt
            tok = (s, self.counts[s])
        self.streams[eng].append((fn, tuple(waits), (tok[0], amt) if tok else None))
        if tok:
            for b in reads:
                b.rs.append(tok)
            for b in writes:
                b.w = tok
                b.rs = []
        return tok

    def t(self, fn, reads=(), writes=(), extra=(), track=True):
        return self.op("tensor", fn, reads, writes, extra, track=track)

    def v(self, fn, reads=(), writes=(), extra=()):
        return self.op("vector", fn, reads, writes, extra)

    def s(self, fn, reads=(), writes=(), extra=()):
        return self.op("scalar", fn, reads, writes, extra)

    def g(self, fn, reads=(), writes=(), extra=()):
        return self.op("gpsimd", fn, reads, writes, extra)

    def vg(self, k, fn, reads=(), writes=(), extra=()):
        return self.v(fn, reads, writes, extra)

    def dma(self, queue, fn, reads=(), writes=(), extra=()):
        eng = QUEUES[queue]
        idx = self.dma_rr[queue]
        self.dma_rr[queue] = (idx + 1) % DMA_POOL
        sem = self.dma_sems[queue][idx]
        return self.op(eng, fn, reads, writes, extra, sem=sem, amt=16)

    def flush_phase(self, name=None):
        for q, eng in QUEUES.items():
            toks = [(s, self.counts[s]) for s in self.dma_sems[q] if self.counts[s] > 0]
            if toks:
                self.streams[eng].append((None, tuple(toks), None))
        streams = {e: list(s) for e, s in self.streams.items()}
        self.streams = {e: [] for e in ENGINES}
        nonempty = [e for e in ENGINES if streams[e]]
        if not nonempty:
            return
        with self.nc.Block(name) as block:
            for eng in nonempty:
                def emitter(e, _eng=eng):
                    lw = {}
                    for fn, waits, inc in streams[_eng]:
                        for (sname, v_) in waits:
                            if lw.get(sname, 0) < v_:
                                e.wait_ge(self.sems[sname], v_)
                                lw[sname] = v_
                        if fn is None:
                            continue
                        inst = fn(e)
                        if inc is not None:
                            inst.then_inc(self.sems[inc[0]], inc[1])
                getattr(block, eng)(emitter)


def bcast(full_view, small_view):
    _, b2 = bass.broadcast_tensor_aps(full_view, small_view)
    return b2


def emit_quantize(bld, nc, x_ap, x_buf, out_ap, out_buf, scr, nblk):
    bm, bm_b = scr["bm"], scr["bm_b"]
    rs, rs_b = scr["rs"], scr["rs_b"]
    xn, xn_b = scr["xn"], scr["xn_b"]
    axn, axn_b = scr["axn"], scr["axn_b"]
    nA, nA_b = scr["nA"], scr["nA_b"]
    nB, nB_b = scr["nB"], scr["nB_b"]
    nC, nC_b = scr["nC"], scr["nC_b"]
    n = nblk * 16
    x3 = x_ap.rearrange("p (b s) -> p b s", s=16)
    bm3 = bm[:, 0:nblk].rearrange("p (b o) -> p b o", o=1)
    bld.v(lambda e: e.tensor_reduce(bm3, x3, Ax.X, Alu.max, apply_absolute_value=True),
          reads=[x_buf], writes=[bm_b])
    bld.v(lambda e: e.tensor_scalar(bm[:, 0:nblk], bm[:, 0:nblk], 1.0 / 6.0, 1e-12, Alu.mult, Alu.max),
          reads=[bm_b], writes=[bm_b])
    bld.v(lambda e: e.reciprocal(rs[:, 0:nblk], bm[:, 0:nblk]), reads=[bm_b], writes=[rs_b])
    xn3 = xn[:, 0:n].rearrange("p (b s) -> p b s", s=16)
    rs3 = rs[:, 0:nblk].rearrange("p (b o) -> p b o", o=1)
    bld.v(lambda e: e.tensor_tensor(xn3, x3, bcast(x3, rs3), Alu.mult),
          reads=[x_buf, rs_b], writes=[xn_b])
    bld.v(lambda e: e.tensor_tensor(axn[:, 0:n], xn[:, 0:n], xn[:, 0:n], Alu.mult),
          reads=[xn_b], writes=[axn_b])
    def sgn_t(dst, dst_b, c):
        cap_, cb_ = bld._csts[c]
        bld.s(lambda e, cap_=cap_: e.activation(dst[:, 0:n], axn[:, 0:n], Act.Sign, bias=cap_),
              reads=[axn_b, cb_], writes=[dst_b])
    def add_t(d, d_b, s_, s_b):
        bld.v(lambda e: e.tensor_tensor(d[:, 0:n], d[:, 0:n], s_[:, 0:n], Alu.add),
              reads=[d_b, s_b], writes=[d_b])
    sgn_t(nA, nA_b, 0.0625)
    sgn_t(nB, nB_b, 0.5625)
    add_t(nA, nA_b, nB, nB_b)
    sgn_t(nB, nB_b, 1.5625)
    sgn_t(nC, nC_b, 3.0625)
    add_t(nB, nB_b, nC, nC_b)
    add_t(nA, nA_b, nB, nB_b)
    bld.v(lambda e: e.tensor_single_scalar(nA[:, 0:n], nA[:, 0:n], 0.25, Alu.mult),
          reads=[nA_b], writes=[nA_b])
    sgn_t(nB, nB_b, 6.25)
    sgn_t(nC, nC_b, 12.25)
    add_t(nB, nB_b, nC, nC_b)
    bld.v(lambda e: e.scalar_tensor_tensor(nA[:, 0:n], nB[:, 0:n], 0.5, nA[:, 0:n], Alu.mult, Alu.add),
          reads=[nA_b, nB_b], writes=[nA_b])
    sgn_t(nC, nC_b, 25.0)
    bld.v(lambda e: e.tensor_single_scalar(nC[:, 0:n], nC[:, 0:n], 3.0, Alu.add),
          reads=[nC_b], writes=[nC_b])
    add_t(nA, nA_b, nC, nC_b)
    bld.s(lambda e: e.sign(xn[:, 0:n], xn[:, 0:n]), reads=[xn_b], writes=[xn_b])
    bld.g(lambda e: e.tensor_tensor(nA[:, 0:n], nA[:, 0:n], xn[:, 0:n], Alu.mult),
          reads=[nA_b, xn_b], writes=[nA_b])
    o3 = out_ap.rearrange("p (b s) -> p b s", s=16)
    nA3 = nA[:, 0:n].rearrange("p (b s) -> p b s", s=16)
    bld.g(lambda e: e.tensor_tensor(o3, nA3, bcast(nA3, bm3), Alu.mult),
          reads=[nA_b, bm_b], writes=[out_buf])


def build_core_program():
    nc = bass.Bass("TRN2", target_bir_lowering=False)
    x_in = nc.dram_tensor("x", [S, D], F32, kind="ExternalInput")
    wq_d = nc.dram_tensor("wq", [D, 512], F32, kind="ExternalInput")
    wk_d = nc.dram_tensor("wk", [D, 512], F32, kind="ExternalInput")
    wv_d = nc.dram_tensor("wv", [D, D], F32, kind="ExternalInput")
    wo_d = nc.dram_tensor("wo", [D, D], F32, kind="ExternalInput")
    wif_d = nc.dram_tensor("wif", [D, 16], F32, kind="ExternalInput")
    bif_d = nc.dram_tensor("bif", [16, 1], F32, kind="ExternalInput")
    onw_d = nc.dram_tensor("onw", [128, D], F32, kind="ExternalInput")
    wdn_d = nc.dram_tensor("wdn", [D, D], F32, kind="ExternalInput")
    wg_d = nc.dram_tensor("wg", [DI, D], F32, kind="ExternalInput")
    wu_d = nc.dram_tensor("wu", [DI, D], F32, kind="ExternalInput")
    wd_d = nc.dram_tensor("wd", [D, DI], F32, kind="ExternalInput")
    out_d = nc.dram_tensor("out", [S, D], F32, kind="ExternalOutput")
    xs1 = nc.dram_tensor("xs1", [S, D], F32)
    xs2 = nc.dram_tensor("xs2", [S, D], F32)
    wgc = nc.dram_tensor("wgc", [D, DI], BF16)
    wuc = nc.dram_tensor("wuc", [D, DI], BF16)
    wdc = nc.dram_tensor("wdc", [DI, D], BF16)

    with contextlib.ExitStack() as gstack:
        bld = B(nc, gstack)
        sb = lambda name, shape, dt, st=gstack, **kw: st.enter_context(nc.sbuf_tensor(name, shape, dt, **kw))
        psum = lambda name, shape, dt, st=gstack: st.enter_context(nc.psum_tensor(name, shape, dt))

        cst = sb("cst", [128, 9], F32)
        cst_b = Buf("cst")
        bld.g(lambda e: e.memset(cst[:, 0:1], LN_EPS), writes=[cst_b])
        bld.g(lambda e: e.memset(cst[:, 1:2], ATT_EPS), writes=[cst_b])
        _thr = (0.0625, 0.5625, 1.5625, 3.0625, 6.25, 12.25, 25.0)
        bld._csts = {}
        for _k, _c in enumerate(_thr):
            bld.g(lambda e, _k=_k, _c=_c: e.memset(cst[:, 2 + _k:3 + _k], -_c), writes=[cst_b])
            bld._csts[_c] = (cst[:, 2 + _k:3 + _k], cst_b)

        def ffn_phase(tag, x_src, x_dst, wgqT, wuqT, wdqT, w_bufs, stack):
            wg_b, wu_b, wd_b = w_bufs
            xt = [sb(f"{tag}xt{i}", [TT, D], F32, stack) for i in range(3)]
            xt_b = [Buf() for _ in range(3)]
            xln = [sb(f"{tag}xln{i}", [TT, D], F32, stack) for i in range(2)]
            xln_b = [Buf() for _ in range(2)]
            xq = [sb(f"{tag}xq{i}", [TT, D], BF16, stack) for i in range(2)]
            xq_b = [Buf() for _ in range(2)]
            xqT = [sb(f"{tag}xqT{i}", [128, DC, TT], BF16, stack, is_transpose_dest=True) for i in range(2)]
            xqT_hb = [[Buf(), Buf()] for _ in range(2)]
            hq = [sb(f"{tag}hq{i}", [TT, DI], BF16, stack) for i in range(2)]
            hq_b = [Buf() for _ in range(2)]
            hqT = [sb(f"{tag}hqT{i}", [128, IC, TT], BF16, stack, is_transpose_dest=True) for i in range(2)]
            hqT_b = [Buf() for _ in range(2)]
            hf = [sb(f"{tag}hf{i}", [TT, 512], F32, stack) for i in range(2)]
            hf_b = [Buf() for _ in range(2)]
            st_t = sb(f"{tag}st", [TT, 16], F32, stack)
            st_b = Buf()
            scr2 = []
            for si in range(2):
                scr2.append({
                    "bm": sb(f"{tag}bm{si}", [TT, 32], F32, stack), "bm_b": Buf(),
                    "rs": sb(f"{tag}rs{si}", [TT, 32], F32, stack), "rs_b": Buf(),
                    "xn": sb(f"{tag}xn{si}", [TT, 512], F32, stack), "xn_b": Buf(),
                    "axn": sb(f"{tag}axn{si}", [TT, 512], F32, stack), "axn_b": Buf(),
                    "nA": sb(f"{tag}nA{si}", [TT, 512], BF16, stack), "nA_b": Buf(),
                    "nB": sb(f"{tag}nB{si}", [TT, 512], BF16, stack), "nB_b": Buf(),
                    "nC": sb(f"{tag}nC{si}", [TT, 512], BF16, stack), "nC_b": Buf(),
                })
            ps_g = [psum(f"{tag}psg{k}", [TT, 512], F32, stack) for k in range(3)]
            ps_g_b = [Buf() for _ in range(3)]
            ps_u = [psum(f"{tag}psu{k}", [TT, 512], F32, stack) for k in range(3)]
            ps_u_b = [Buf() for _ in range(3)]
            ps_d = [psum(f"{tag}psd{i}", [TT, 512], F32, stack) for i in range(2)]
            ps_d_b = [Buf() for _ in range(2)]

            ichunks = [(0, 512), (512, 512), (1024, 512), (1536, 512), (2048, 512), (2560, 256)]
            pend = []

            def emit_down(item):
                dti, dhqT, dhqT_b, dxt, dxt_b, dxn, dxn_b = item
                for j in range(IC):
                    for half in range(2):
                        pd, pd_b = ps_d[half], ps_d_b[half]
                        bld.t(lambda e, j=j, half=half, a=dhqT, w=wdqT, pd=pd: e.matmul(
                                pd[:], a[:, j, :], w[:, j, half * 512:(half + 1) * 512],
                                start=(j == 0), stop=(j == IC - 1)),
                              reads=[dhqT_b, wd_b], writes=[pd_b], track=(j == IC - 1))
                for half in range(2):
                    pd, pd_b = ps_d[half], ps_d_b[half]
                    bld.v(lambda e, half=half, x=dxt, o=dxn, pd=pd: e.scalar_tensor_tensor(
                            o[:, half * 512:(half + 1) * 512], pd[:], 0.5, x[:, half * 512:(half + 1) * 512],
                            Alu.mult, Alu.add),
                          reads=[pd_b, dxt_b], writes=[dxn_b])
                bld.dma("qS", lambda e, dti=dti, o=dxn: e.dma_start(x_dst.ap()[dti * TT:(dti + 1) * TT, :], o[:]),
                        reads=[dxn_b])

            def xside(ti):
                xti, xti_b = xt[ti % 3], xt_b[ti % 3]
                xlni, xlni_b = xln[ti % 2], xln_b[ti % 2]
                xqi, xqi_b = xq[ti % 2], xq_b[ti % 2]
                xqTi, xqTi_hb = xqT[ti % 2], xqT_hb[ti % 2]
                bld.dma("qG", lambda e, ti=ti, x=xti: e.dma_start(x[:], x_src.ap()[ti * TT:(ti + 1) * TT, :]),
                        writes=[xti_b])
                bld.v(lambda e, x=xti, s=st_t: e.bn_stats(s[:, 0:6], x[:, 0:512]),
                      reads=[xti_b], writes=[st_b])
                bld.v(lambda e, x=xti, s=st_t: e.bn_stats(s[:, 6:12], x[:, 512:1024]),
                      reads=[xti_b], writes=[st_b])
                bld.v(lambda e, s=st_t: e.bn_aggr(s[:, 12:14], s[:, 0:12]),
                      reads=[st_b], writes=[st_b])
                bld.s(lambda e, s=st_t: e.activation(s[:, 14:15], s[:, 13:14], Act.Sqrt, bias=cst[:, 0:1]),
                      reads=[st_b, cst_b], writes=[st_b])
                bld.v(lambda e, s=st_t: e.reciprocal(s[:, 15:16], s[:, 14:15]),
                      reads=[st_b], writes=[st_b])
                bld.v(lambda e, x=xti, o=xlni, s=st_t: e.tensor_scalar(o[:], x[:], s[:, 12:13], s[:, 15:16], Alu.subtract, Alu.mult),
                      reads=[xti_b, st_b], writes=[xlni_b])
                for hh in range(2):
                    emit_quantize(bld, nc, xlni[:, hh * 512:(hh + 1) * 512], xlni_b,
                                  xqi[:, hh * 512:(hh + 1) * 512], xqi_b, scr2[hh], 32)
                    bld.dma("qS" if (ti + hh) % 2 == 0 else "qA",
                            lambda e, hh=hh, a=xqi, b=xqTi: e.dma_start(
                                b[:, hh * 4:(hh + 1) * 4, :],
                                a[:, hh * 512:(hh + 1) * 512].rearrange("p (j q) -> p j q", q=128), transpose=True),
                            reads=[xqi_b], writes=[xqT_hb[ti % 2][hh]])

            xside(0)
            for ti in range(NTT):
                xti, xti_b = xt[ti % 3], xt_b[ti % 3]
                xlni, xlni_b = xln[ti % 2], xln_b[ti % 2]
                xqTi, xqTi_hb = xqT[ti % 2], xqT_hb[ti % 2]
                hqi, hqi_b = hq[ti % 2], hq_b[ti % 2]
                hqTi, hqTi_b = hqT[ti % 2], hqT_b[ti % 2]
                if ti + 1 < NTT:
                    xside(ti + 1)
                for grp in range(2):
                    cis = [grp * 3, grp * 3 + 1, grp * 3 + 2]
                    for j in range(DC):
                        for k, ci in enumerate(cis):
                            i0, iw = ichunks[ci]
                            bld.t(lambda e, j=j, i0=i0, iw=iw, k=k, a=xqTi, w=wgqT: e.matmul(
                                    ps_g[k][:, 0:iw], a[:, j, :], w[:, j, i0:i0 + iw],
                                    start=(j == 0), stop=(j == DC - 1)),
                                  reads=[xqTi_hb[j // 4], wg_b], writes=[ps_g_b[k]], track=(j == DC - 1))
                            bld.t(lambda e, j=j, i0=i0, iw=iw, k=k, a=xqTi, w=wuqT: e.matmul(
                                    ps_u[k][:, 0:iw], a[:, j, :], w[:, j, i0:i0 + iw],
                                    start=(j == 0), stop=(j == DC - 1)),
                                  reads=[xqT_hb[ti % 2][j // 4], wu_b], writes=[ps_u_b[k]], track=(j == DC - 1))
                    for k, ci in enumerate(cis):
                        i0, iw = ichunks[ci]
                        hfc, hfc_b = hf[ci % 2], hf_b[ci % 2]
                        bld.s(lambda e, iw=iw, k=k, o=hfc: e.activation(o[:, 0:iw], ps_g[k][:, 0:iw], Act.Silu),
                              reads=[ps_g_b[k]], writes=[hfc_b])
                        bld.v(lambda e, iw=iw, k=k, o=hfc: e.tensor_tensor(o[:, 0:iw], o[:, 0:iw], ps_u[k][:, 0:iw], Alu.mult),
                              reads=[hfc_b, ps_u_b[k]], writes=[hfc_b])
                        emit_quantize(bld, nc, hfc[:, 0:iw], hfc_b, hqi[:, i0:i0 + iw], hqi_b, scr2[ci % 2], iw // 16)
                    if grp == 0 and pend:
                        emit_down(pend.pop(0))
                bld.dma("qA" if ti % 2 == 0 else "qS",
                        lambda e, a=hqi, b=hqTi: e.dma_start(
                            b[:], a[:].rearrange("p (j q) -> p j q", q=128), transpose=True),
                        reads=[hqi_b], writes=[hqTi_b])
                pend.append((ti, hqTi, hqTi_b, xti, xti_b, xlni, xlni_b))
                if len(pend) > 1:
                    emit_down(pend.pop(0))
            while pend:
                emit_down(pend.pop(0))
            bld.flush_phase(tag)

        with contextlib.ExitStack() as wstack:
            wgqT = sb("wgqT", [128, DC, DI], BF16, wstack, is_transpose_dest=True)
            wuqT = sb("wuqT", [128, DC, DI], BF16, wstack, is_transpose_dest=True)
            wdqT = sb("wdqT", [128, IC, D], BF16, wstack, is_transpose_dest=True)
            wgqT_b, wuqT_b, wdqT_b = Buf("wgqT"), Buf("wuqT"), Buf("wdqT")

            wsstack = wstack.enter_context(contextlib.ExitStack())
            wchunk = [sb("wch0", [128, DI], F32, wsstack), sb("wch1", [128, D], F32, wsstack)]
            wchunk_b = [Buf() for _ in range(2)]
            wqb = [sb("wqb0", [128, DI], BF16, wsstack), sb("wqb1", [128, D], BF16, wsstack)]
            wqb_b = [Buf() for _ in range(2)]
            wscr2 = []
            for si, w_ in ((0, D), (1, D)):
                wscr2.append({
                    "bm": sb(f"wbm{si}", [128, 64], F32, wsstack), "bm_b": Buf(),
                    "rs": sb(f"wrs{si}", [128, 64], F32, wsstack), "rs_b": Buf(),
                    "xn": sb(f"wxn{si}", [128, w_], F32, wsstack), "xn_b": Buf(),
                    "axn": sb(f"waxn{si}", [128, w_], F32, wsstack), "axn_b": Buf(),
                    "nA": sb(f"wnA{si}", [128, w_], BF16, wsstack), "nA_b": Buf(),
                    "nB": sb(f"wnB{si}", [128, w_], BF16, wsstack), "nB_b": Buf(),
                    "nC": sb(f"wnC{si}", [128, w_], BF16, wsstack), "nC_b": Buf(),
                })
            for wi, (wsrc, wdstT, wdstT_b) in enumerate(((wg_d, wgqT, wgqT_b), (wu_d, wuqT, wuqT_b))):
                for c in range(IC):
                    wc, wc_b = wchunk[c % 2], wchunk_b[c % 2]
                    wq_, wq_b_ = wqb[c % 2], wqb_b[c % 2]
                    bld.dma("qS", lambda e, c=c, wsrc=wsrc, wc=wc: e.dma_start(
                            wc[:, 0:D], wsrc.ap()[c * 128:(c + 1) * 128, :]), writes=[wc_b])
                    emit_quantize(bld, nc, wc[:, 0:D], wc_b, wq_[:, 0:D], wq_b_, wscr2[c % 2], D // 16)
                    bld.dma("qS" if c % 2 == 0 else "qA",
                            lambda e, c=c, a=wq_, b=wdstT: e.dma_start(
                                b[:, :, c * 128:(c + 1) * 128], a[:, 0:D].rearrange("p (j q) -> p j q", q=128), transpose=True),
                            reads=[wq_b_], writes=[wdstT_b])
            for c in range(DC):
                wc, wc_b = wchunk[0], wchunk_b[0]
                wq_, wq_b_ = wqb[0], wqb_b[0]
                bld.dma("qS", lambda e, c=c, wc=wc: e.dma_start(
                        wc[:], wd_d.ap()[c * 128:(c + 1) * 128, :]), writes=[wc_b])
                for wci, (wi0, wiw) in enumerate(((0, 1024), (1024, 1024), (2048, 768))):
                    emit_quantize(bld, nc, wc[:, wi0:wi0 + wiw], wc_b, wq_[:, wi0:wi0 + wiw], wq_b_,
                                  wscr2[wci % 2], wiw // 16)
                bld.dma("qS" if c % 2 == 0 else "qA",
                        lambda e, c=c, a=wq_: e.dma_start(
                            wdqT[:, :, c * 128:(c + 1) * 128], a[:].rearrange("p (j q) -> p j q", q=128), transpose=True),
                        reads=[wq_b_], writes=[wdqT_b])
            bld.dma("qS", lambda e: e.dma_start(wgc.ap().rearrange("(j p) i -> p j i", p=128), wgqT[:]),
                    reads=[wgqT_b])
            bld.dma("qS", lambda e: e.dma_start(wuc.ap().rearrange("(j p) i -> p j i", p=128), wuqT[:]),
                    reads=[wuqT_b])
            bld.dma("qS", lambda e: e.dma_start(wdc.ap().rearrange("(j p) i -> p j i", p=128), wdqT[:]),
                    reads=[wdqT_b])
            bld.flush_phase("wprep")
            wsstack.close()

            with contextlib.ExitStack() as fstack:
                ffn_phase("f1", x_in, xs1, wgqT, wuqT, wdqT, (wgqT_b, wuqT_b, wdqT_b), fstack)

        with contextlib.ExitStack() as astack:
            idf = sb("idf", [128, 128], F32, astack)
            idf_b = Buf()
            maskt = sb("maskt", [TT, TT], BF16, astack)
            maskt_b = Buf()
            bld.g(lambda e: e.memset(idf[:], 0.0), writes=[idf_b])
            bld.g(lambda e: e.affine_select(out=idf[:], in_=idf[:], compare_op=Alu.not_equal,
                                            fill=1.0, base=0, pattern=[[-1, 128]], channel_multiplier=1),
                  reads=[idf_b], writes=[idf_b])
            bld.g(lambda e: e.memset(maskt[:], 1.0), writes=[maskt_b])
            bld.g(lambda e: e.affine_select(out=maskt[:], in_=maskt[:], compare_op=Alu.is_ge,
                                            fill=0.0, base=0, pattern=[[1, TT]], channel_multiplier=-1),
                  reads=[maskt_b], writes=[maskt_b])

            QT = sb("QT", [128, 4, S], BF16, astack)
            QT_b = Buf("QT")
            KT = sb("KT", [128, 4, S], BF16, astack)
            KT_b = Buf("KT")
            VH = sb("VH", [TT, NTT, H, 132], BF16, astack)
            VH_b = Buf("VH")
            so = sb("so", [TT, NTT, D], BF16, astack)
            so_b = Buf("so")
            onw = sb("onw_sb", [128, D], F32, astack)
            onw_b = Buf()
            ecol_t = sb("ecol_t", [128, NTT, 128], F32, astack)
            ecol_t_b = Buf()
            erow_t = sb("erow_t", [128, 128], F32, astack)
            erow_t_b = Buf()
            emb_t = sb("emb_t", [128, 128], F32, astack)
            emb_t_b = Buf()
            bld.dma("qS", lambda e: e.dma_start(onw[:], onw_d.ap()), writes=[onw_b])
            bld.g(lambda e: e.memset(VH[:, :, :, 128:129], 1.0), writes=[VH_b])
            with contextlib.ExitStack() as a12stack:
                ifsb = sb("ifsb", [TT, NTT, 16], F32, a12stack)
                ifsb_b = Buf()
                ir_row = sb("ir_row", [8, S], F32, a12stack)
                ir_b = Buf()
                fr_row = sb("fr_row", [8, S], F32, a12stack)
                fr_b = Buf()
                bi_s = sb("bi_s", [8, 2], F32, a12stack)
                bi_b = Buf()
                bf_s = sb("bf_s", [8, 2], F32, a12stack)
                bf_b = Buf()
                bld.dma("qS", lambda e: e.dma_start(bi_s[:, 0:1], bif_d.ap()[0:8, :]), writes=[bi_b])
                bld.dma("qS", lambda e: e.dma_start(bf_s[:, 0:1], bif_d.ap()[8:16, :]), writes=[bf_b])
                bld.v(lambda e: e.tensor_single_scalar(bi_s[:, 1:2], bi_s[:, 0:1], 1.0 / CAP, Alu.mult),
                      reads=[bi_b], writes=[bi_b])
                bld.v(lambda e: e.tensor_single_scalar(bf_s[:, 1:2], bf_s[:, 0:1], 1.0 / CAP, Alu.mult),
                      reads=[bf_b], writes=[bf_b])

                with contextlib.ExitStack() as pstack:
                    wqs = sb("wqs", [128, DC, 512], BF16, pstack)
                    wqs_b = Buf()
                    wks = sb("wks", [128, DC, 512], BF16, pstack)
                    wks_b = Buf()
                    wvs = sb("wvs", [128, DC, D], BF16, pstack)
                    wvs_b = Buf()
                    wos = sb("wos", [128, DC, D], BF16, pstack)
                    wos_b = Buf()
                    wifs = sb("wifs", [128, DC, 16], BF16, pstack)
                    wifs_b = Buf()
                    xT2t = [sb(f"xT2t{i}", [128, DC, TT], BF16, pstack, is_transpose_dest=True) for i in range(2)]
                    xT2t_b = [Buf() for _ in range(2)]
                    xt = [sb(f"a1xt{i}", [TT, D], F32, pstack) for i in range(2)]
                    xt_b = [Buf() for _ in range(2)]
                    xln = [sb(f"a1xln{i}", [TT, D], BF16, pstack) for i in range(2)]
                    xln_b = [Buf() for _ in range(2)]
                    st_t = sb("a1st", [TT, 16], F32, pstack)
                    st_b = Buf()
                    ps_p = [psum(f"a1ps{i}", [128, 512], F32, pstack) for i in range(2)]
                    ps_p_b = [Buf() for _ in range(2)]
                    ps_v = [psum(f"a1psv{i}", [128, 512], F32, pstack) for i in range(2)]
                    ps_v_b = [Buf() for _ in range(2)]
                    ps_if = psum("a1psif", [128, 16], F32, pstack)
                    ps_if_b = Buf()
                    ps_t = psum("a1pst", [16, 128], F32, pstack)
                    ps_t_b = Buf()
                    bld.dma("qG", lambda e: e.dma_start(wqs[:], wq_d.ap().rearrange("(j p) q -> p j q", p=128)), writes=[wqs_b])
                    bld.dma("qG", lambda e: e.dma_start(wks[:], wk_d.ap().rearrange("(j p) q -> p j q", p=128)), writes=[wks_b])
                    bld.dma("qG", lambda e: e.dma_start(wvs[:], wv_d.ap().rearrange("(j p) q -> p j q", p=128)), writes=[wvs_b])
                    bld.dma("qG", lambda e: e.dma_start(wos[:], wo_d.ap().rearrange("(j p) q -> p j q", p=128)), writes=[wos_b])
                    bld.dma("qG", lambda e: e.dma_start(wifs[:], wif_d.ap().rearrange("(j p) q -> p j q", p=128)), writes=[wifs_b])
                    def a1_xside(ti):
                        xti, xti_b = xt[ti % 2], xt_b[ti % 2]
                        xlni, xlni_b = xln[ti % 2], xln_b[ti % 2]
                        xTt, xTt_b = xT2t[ti % 2], xT2t_b[ti % 2]
                        bld.dma("qG", lambda e, ti=ti, x=xti: e.dma_start(x[:], xs1.ap()[ti * TT:(ti + 1) * TT, :]),
                                writes=[xti_b])
                        bld.v(lambda e, x=xti, s=st_t: e.bn_stats(s[:, 0:6], x[:, 0:512]), reads=[xti_b], writes=[st_b])
                        bld.v(lambda e, x=xti, s=st_t: e.bn_stats(s[:, 6:12], x[:, 512:1024]), reads=[xti_b], writes=[st_b])
                        bld.v(lambda e, s=st_t: e.bn_aggr(s[:, 12:14], s[:, 0:12]), reads=[st_b], writes=[st_b])
                        bld.s(lambda e, s=st_t: e.activation(s[:, 14:15], s[:, 13:14], Act.Sqrt, bias=cst[:, 0:1]),
                              reads=[st_b, cst_b], writes=[st_b])
                        bld.v(lambda e, s=st_t: e.reciprocal(s[:, 15:16], s[:, 14:15]), reads=[st_b], writes=[st_b])
                        bld.v(lambda e, x=xti, o=xlni, s=st_t: e.tensor_scalar(o[:], x[:], s[:, 12:13], s[:, 15:16], Alu.subtract, Alu.mult),
                              reads=[xti_b, st_b], writes=[xlni_b])
                        bld.dma("qS" if ti % 2 == 0 else "qA",
                                lambda e, a=xlni, b=xTt: e.dma_start(
                                    b[:], a[:].rearrange("p (j q) -> p j q", q=128), transpose=True),
                                reads=[xlni_b], writes=[xTt_b])

                    a1_xside(0)
                    for ti in range(NTT):
                        xTt, xTt_b = xT2t[ti % 2], xT2t_b[ti % 2]
                        if ti + 1 < NTT:
                            a1_xside(ti + 1)
                        for gq in range(4):
                            pp, pp_b = ps_p[gq % 2], ps_p_b[gq % 2]
                            for j in range(DC):
                                bld.t(lambda e, j=j, gq=gq, pp=pp, a=xTt: e.matmul(
                                        pp[:, 0:TT], wqs[:, j, gq * 128:(gq + 1) * 128], a[:, j, :],
                                        start=(j == 0), stop=(j == DC - 1)),
                                      reads=[wqs_b, xTt_b], writes=[pp_b], track=(j == DC - 1))
                            bld.v(lambda e, gq=gq, ti=ti, pp=pp: e.tensor_copy(QT[:, gq, ti * TT:(ti + 1) * TT], pp[:, 0:TT]),
                                  reads=[pp_b], writes=[QT_b])
                            pp, pp_b = ps_p[(gq + 1) % 2], ps_p_b[(gq + 1) % 2]
                            for j in range(DC):
                                bld.t(lambda e, j=j, gq=gq, pp=pp, a=xTt: e.matmul(
                                        pp[:, 0:TT], wks[:, j, gq * 128:(gq + 1) * 128], a[:, j, :],
                                        start=(j == 0), stop=(j == DC - 1)),
                                      reads=[wks_b, xTt_b], writes=[pp_b], track=(j == DC - 1))
                            bld.v(lambda e, gq=gq, ti=ti, pp=pp: e.tensor_copy(KT[:, gq, ti * TT:(ti + 1) * TT], pp[:, 0:TT]),
                                  reads=[pp_b], writes=[KT_b])
                        for j in range(DC):
                            for half in range(2):
                                bld.t(lambda e, j=j, half=half, pp=ps_v[half], a=xTt: e.matmul(
                                        pp[:], a[:, j, :], wvs[:, j, half * 512:(half + 1) * 512],
                                        start=(j == 0), stop=(j == DC - 1)),
                                      reads=[wvs_b, xTt_b], writes=[ps_v_b[half]], track=(j == DC - 1))
                                bld.t(lambda e, j=j, half=half, pp=ps_p[half], a=xTt: e.matmul(
                                        pp[:], a[:, j, :], wos[:, j, half * 512:(half + 1) * 512],
                                        start=(j == 0), stop=(j == DC - 1)),
                                      reads=[wos_b, xTt_b], writes=[ps_p_b[half]], track=(j == DC - 1))
                            bld.t(lambda e, j=j, a=xTt: e.matmul(
                                    ps_if[:], a[:, j, :], wifs[:, j, :],
                                    start=(j == 0), stop=(j == DC - 1)),
                                  reads=[wifs_b, xTt_b], writes=[ps_if_b], track=(j == DC - 1))
                        for half in range(2):
                            bld.v(lambda e, ti=ti, half=half, pp=ps_v[half]: e.tensor_copy(
                                    VH[:, ti, half * 4:(half + 1) * 4, 0:128], pp[:].rearrange("p (h v) -> p h v", v=128)),
                                  reads=[ps_v_b[half]], writes=[VH_b])
                            bld.s(lambda e, ti=ti, half=half, pp=ps_p[half]: e.activation(
                                    so[:, ti, half * 512:(half + 1) * 512], pp[:], Act.Sigmoid),
                                  reads=[ps_p_b[half]], writes=[so_b])
                        bld.v(lambda e, ti=ti: e.tensor_copy(ifsb[:, ti, :], ps_if[:]),
                              reads=[ps_if_b], writes=[ifsb_b])
                        bld.t(lambda e, ti=ti: e.transpose(ps_t[0:8, :], ifsb[:, ti, 0:8], idf[:]),
                              reads=[ifsb_b, idf_b], writes=[ps_t_b])
                        bld.v(lambda e, ti=ti: e.tensor_copy(ir_row[:, ti * TT:(ti + 1) * TT], ps_t[0:8, :]),
                              reads=[ps_t_b], writes=[ir_b])
                        bld.t(lambda e, ti=ti: e.transpose(ps_t[0:8, :], ifsb[:, ti, 8:16], idf[:]),
                              reads=[ifsb_b, idf_b], writes=[ps_t_b])
                        bld.v(lambda e, ti=ti: e.tensor_copy(fr_row[:, ti * TT:(ti + 1) * TT], ps_t[0:8, :]),
                              reads=[ps_t_b], writes=[fr_b])
                    bld.flush_phase("a1")
                gr_row = sb("gr_row", [8, S], F32, a12stack)
                gr_b = Buf()
                tmp_row = sb("tmp_row", [8, S], F32, a12stack)
                tmp_b = Buf()
                lfcr = sb("lfcr", [8, S], F32, a12stack)
                lfcr_b = Buf()
                Mr = sb("Mr", [8, S], F32, a12stack)
                Mr_b = Buf()
                M_loc = sb("M_loc", [128, TT], F32, a12stack)
                M_loc_b = Buf()
                nm_loc = sb("nm_loc", [128, TT], F32, a12stack)
                nm_loc_b = Buf()
                g_rep = sb("g_rep", [128, S], F32, a12stack)
                g_rep_b = Buf()
                PL = sb("PL", [128, 2], F32, a12stack)
                PL_b = Buf()
                ps_tr = psum("a2pst", [128, 128], F32, a12stack)
                ps_tr_b = Buf()
                bld.s(lambda e: e.activation(ir_row[:], ir_row[:], Act.Tanh, bias=bi_s[:, 1:2], scale=1.0 / CAP),
                      reads=[ir_b, bi_b], writes=[ir_b])
                bld.v(lambda e: e.tensor_single_scalar(ir_row[:], ir_row[:], CAP, Alu.mult),
                      reads=[ir_b], writes=[ir_b])
                bld.s(lambda e: e.activation(fr_row[:], fr_row[:], Act.Tanh, bias=bf_s[:, 1:2], scale=1.0 / CAP),
                      reads=[fr_b, bf_b], writes=[fr_b])
                bld.v(lambda e: e.tensor_single_scalar(fr_row[:], fr_row[:], CAP, Alu.mult),
                      reads=[fr_b], writes=[fr_b])
                bld.s(lambda e: e.activation(tmp_row[:], fr_row[:], Act.Abs),
                      reads=[fr_b], writes=[tmp_b])
                bld.s(lambda e: e.activation(tmp_row[:], tmp_row[:], Act.Exp, scale=-1.0),
                      reads=[tmp_b], writes=[tmp_b])
                bld.s(lambda e: e.activation(tmp_row[:], tmp_row[:], Act.Ln, bias=1.0),
                      reads=[tmp_b], writes=[tmp_b])
                bld.v(lambda e: e.scalar_tensor_tensor(tmp_row[:], fr_row[:], 0.0, tmp_row[:], Alu.min, Alu.subtract),
                      reads=[fr_b, tmp_b], writes=[tmp_b])
                bld.v(lambda e: e.tensor_tensor_scan(lfcr[:], tmp_row[:], tmp_row[:], 0.0, Alu.add, Alu.bypass),
                      reads=[tmp_b], writes=[lfcr_b])
                bld.v(lambda e: e.tensor_sub(gr_row[:], ir_row[:], lfcr[:]),
                      reads=[ir_b, lfcr_b], writes=[gr_b])
                bld.v(lambda e: e.tensor_tensor_scan(Mr[:], gr_row[:], gr_row[:], -1e30, Alu.max, Alu.bypass),
                      reads=[gr_b], writes=[Mr_b])
                bld.v(lambda e: e.scalar_tensor_tensor(tmp_row[:], lfcr[:], -1.0, Mr[:], Alu.mult, Alu.subtract),
                      reads=[lfcr_b, Mr_b], writes=[tmp_b])
                for i in range(NTT):
                    bld.dma("qS", lambda e, i=i: e.dma_start(M_loc[i * 8:(i + 1) * 8, :], Mr[:, i * TT:(i + 1) * TT]),
                            reads=[Mr_b], writes=[M_loc_b])
                    bld.dma("qA", lambda e, i=i: e.dma_start(nm_loc[i * 8:(i + 1) * 8, :], tmp_row[:, i * TT:(i + 1) * TT]),
                            reads=[tmp_b], writes=[nm_loc_b])
                    bld.dma("qG", lambda e, i=i: e.dma_start(g_rep[i * 8:(i + 1) * 8, :], gr_row[:]),
                            reads=[gr_b], writes=[g_rep_b])
                bld.v(lambda e: e.tensor_copy(PL[:, 0:1], M_loc[:, TT - 1:TT]), reads=[M_loc_b], writes=[PL_b])
                bld.v(lambda e: e.tensor_single_scalar(PL[:, 1:2], PL[:, 0:1], LN8, Alu.add), reads=[PL_b], writes=[PL_b])
                bld.v(lambda e: e.tensor_scalar(g_rep[:], g_rep[:], PL[:, 1:2], 0.0, Alu.subtract, Alu.min),
                      reads=[g_rep_b, PL_b], writes=[g_rep_b])
                bld.s(lambda e: e.activation(g_rep[:], g_rep[:], Act.Exp), reads=[g_rep_b], writes=[g_rep_b])
                for tau in range(NTT):
                    bld.t(lambda e, tau=tau: e.transpose(ps_tr[:], g_rep[:, tau * TT:(tau + 1) * TT], idf[:]),
                          reads=[g_rep_b, idf_b], writes=[ps_tr_b])
                    bld.v(lambda e, tau=tau: e.tensor_copy(ecol_t[:, tau, :], ps_tr[:]),
                          reads=[ps_tr_b], writes=[ecol_t_b])
                bld.v(lambda e: e.tensor_scalar(M_loc[:], M_loc[:], PL[:, 0:1], -85.0, Alu.subtract, Alu.max),
                      reads=[M_loc_b, PL_b], writes=[M_loc_b])
                bld.s(lambda e: e.activation(M_loc[:], M_loc[:], Act.Exp, scale=-1.0), reads=[M_loc_b], writes=[M_loc_b])
                bld.t(lambda e: e.transpose(ps_tr[:], M_loc[:], idf[:]), reads=[M_loc_b, idf_b], writes=[ps_tr_b])
                bld.v(lambda e: e.tensor_copy(erow_t[:], ps_tr[:]), reads=[ps_tr_b], writes=[erow_t_b])
                bld.s(lambda e: e.activation(nm_loc[:], nm_loc[:], Act.Exp), reads=[nm_loc_b], writes=[nm_loc_b])
                bld.t(lambda e: e.transpose(ps_tr[:], nm_loc[:], idf[:]), reads=[nm_loc_b, idf_b], writes=[ps_tr_b])
                bld.v(lambda e: e.tensor_copy(emb_t[:], ps_tr[:]), reads=[ps_tr_b], writes=[emb_t_b])
                bld.flush_phase("a2")

            with contextlib.ExitStack() as astack3:
                wdns = sb("wdns", [128, DC, D], BF16, astack3)
                wdns_b = Buf()
                bld.dma("qG", lambda e: e.dma_start(wdns[:], wdn_d.ap().rearrange("(j p) q -> p j q", p=128)),
                        writes=[wdns_b])
                ct = [sb(f"ct{i}", [TT, 512], BF16, astack3) for i in range(2)]
                ct_b = [Buf() for _ in range(2)]
                acat = sb("acat", [TT, 4, H, 132], F32, astack3)
                acat_b = Buf()
                nsm = sb("nsm", [TT, 8 * 8], F32, astack3)
                nsm_b = Buf()
                sqt = sb("sqt", [TT, H, 128], F32, astack3)
                sqt_b = Buf()
                hub = sb("hub", [TT, H, 128], F32, astack3)
                hub_b = Buf()
                wsr = sb("wsr", [TT, D], F32, astack3)
                wsr_b = Buf()
                hg = [sb(f"hg{i}", [TT, D], BF16, astack3) for i in range(2)]
                hg_b = [Buf() for _ in range(2)]
                hgT = [sb(f"hgT{i}", [128, DC, TT], BF16, astack3, is_transpose_dest=True) for i in range(2)]
                hgT_b = [Buf() for _ in range(2)]
                x1t = [sb(f"x1t{i}", [TT, D], F32, astack3) for i in range(2)]
                x1t_b = [Buf() for _ in range(2)]
                x2t = [sb(f"x2t{i}", [TT, D], F32, astack3) for i in range(2)]
                x2t_b = [Buf() for _ in range(2)]
                ps_sc = [psum(f"pssc{i}", [TT, 512], F32, astack3) for i in range(2)]
                ps_sc_b = [Buf() for _ in range(2)]
                ps_a = [psum(f"psa{i}", [TT, 132], F32, astack3) for i in range(4)]
                ps_a_b = [Buf() for _ in range(4)]
                ps_o = [psum(f"pso{i}", [TT, 512], F32, astack3) for i in range(2)]
                ps_o_b = [Buf() for _ in range(2)]

                for i4 in range(4):
                    for h in range(H):
                        gq, sub = h // 2, (h % 2) * 64

                        def emit_scores(tau, i4=i4, gq=gq, sub=sub):
                            cb = tau % 2
                            psc, psc_b = ps_sc[cb], ps_sc_b[cb]
                            bld.t(lambda e, i4=i4, tau=tau, gq=gq, sub=sub, psc=psc: e.matmul(
                                    psc[:], KT[sub:sub + 64, gq, tau * TT:(tau + 1) * TT],
                                    QT[sub:sub + 64, gq, i4 * 512:(i4 + 1) * 512], start=True, stop=True),
                                  reads=[KT_b, QT_b], writes=[psc_b])

                        emit_scores(0)
                        for tau in range(4 * i4 + 4):
                            cb = tau % 2
                            psc, psc_b = ps_sc[cb], ps_sc_b[cb]
                            if tau + 1 < 4 * i4 + 4:
                                emit_scores(tau + 1)
                            for il in range(4):
                                i = 4 * i4 + il
                                if tau > i:
                                    continue
                                col = i * 8 + h
                                if tau == i:
                                    bld.v(lambda e, tau=tau, col=col, il=il, cbuf=ct[cb], psc=psc: e.scalar_tensor_tensor(
                                            cbuf[:, il * TT:(il + 1) * TT], psc[:, il * TT:(il + 1) * TT],
                                            ecol_t[:, tau, col:col + 1], maskt[:], Alu.mult, Alu.mult),
                                          reads=[psc_b, ecol_t_b, maskt_b], writes=[ct_b[cb]])
                                elif il % 2 == 0:
                                    bld.s(lambda e, tau=tau, col=col, il=il, cbuf=ct[cb], psc=psc: e.activation(
                                            cbuf[:, il * TT:(il + 1) * TT], psc[:, il * TT:(il + 1) * TT],
                                            Act.Copy, scale=ecol_t[:, tau, col:col + 1]),
                                          reads=[psc_b, ecol_t_b], writes=[ct_b[cb]])
                                else:
                                    bld.v(lambda e, tau=tau, col=col, il=il, cbuf=ct[cb], psc=psc: e.tensor_scalar(
                                            cbuf[:, il * TT:(il + 1) * TT], psc[:, il * TT:(il + 1) * TT],
                                            ecol_t[:, tau, col:col + 1], None, Alu.mult),
                                          reads=[psc_b, ecol_t_b], writes=[ct_b[cb]])
                                bld.t(lambda e, i=i, tau=tau, h=h, il=il, cbuf=ct[cb], pa=ps_a[il]: e.matmul(
                                        pa[:, 0:129], cbuf[:, il * TT:(il + 1) * TT], VH[:, tau, h, 0:129],
                                        start=(tau == 0), stop=(tau == i)),
                                      reads=[ct_b[cb], VH_b], writes=[ps_a_b[il]], track=(tau == i))
                        for il in range(4):
                            col = (4 * i4 + il) * 8 + h
                            bld.v(lambda e, h=h, il=il, col=col, pa=ps_a[il]: e.tensor_scalar(
                                    acat[:, il, h, 0:129], pa[:, 0:129], erow_t[:, col:col + 1], None, Alu.mult),
                                  reads=[ps_a_b[il], erow_t_b], writes=[acat_b])
                    for il in range(4):
                        i = 4 * i4 + il
                        bld.dma("qG", lambda e, i=i, x=x1t[i % 2]: e.dma_start(x[:], xs1.ap()[i * TT:(i + 1) * TT, :]),
                                writes=[x1t_b[i % 2]])
                        c0 = i * 8
                        rsv = acat[:, il, :, 128:129]
                        bld.v(lambda e, rsv=rsv: e.tensor_scalar(nsm[:, 0:8], rsv, -1.0, None, Alu.mult),
                              reads=[acat_b], writes=[nsm_b])
                        bld.v(lambda e, rsv=rsv: e.tensor_tensor(nsm[:, 0:8].rearrange("p (h o) -> p h o", o=1), rsv,
                                                        nsm[:, 0:8].rearrange("p (h o) -> p h o", o=1), Alu.max),
                              reads=[acat_b, nsm_b], writes=[nsm_b])
                        bld.v(lambda e, c0=c0: e.tensor_tensor(nsm[:, 0:8], nsm[:, 0:8], emb_t[:, c0:c0 + 8], Alu.max),
                              reads=[nsm_b, emb_t_b], writes=[nsm_b])
                        bld.v(lambda e: e.tensor_single_scalar(nsm[:, 0:8], nsm[:, 0:8], ATT_EPS, Alu.add),
                              reads=[nsm_b], writes=[nsm_b])
                        bld.v(lambda e: e.reciprocal(nsm[:, 8:16], nsm[:, 0:8]), reads=[nsm_b], writes=[nsm_b])
                        av = acat[:, il, :, 0:128]
                        bld.v(lambda e, av=av: e.tensor_reduce(nsm[:, 16:24].rearrange("p (h o) -> p h o", o=1), av, Ax.X, Alu.add),
                              reads=[acat_b], writes=[nsm_b])
                        bld.s(lambda e, av=av: e.activation(sqt[:], av, Act.Square), reads=[acat_b], writes=[sqt_b])
                        bld.v(lambda e: e.tensor_reduce(nsm[:, 24:32].rearrange("p (h o) -> p h o", o=1), sqt[:], Ax.X, Alu.add),
                              reads=[sqt_b], writes=[nsm_b])
                        bld.v(lambda e: e.tensor_single_scalar(nsm[:, 16:24], nsm[:, 16:24], 1.0 / 128.0, Alu.mult),
                              reads=[nsm_b], writes=[nsm_b])
                        bld.v(lambda e: e.tensor_single_scalar(nsm[:, 24:32], nsm[:, 24:32], 1.0 / 128.0, Alu.mult),
                              reads=[nsm_b], writes=[nsm_b])
                        bld.v(lambda e: e.tensor_tensor(nsm[:, 32:40], nsm[:, 16:24], nsm[:, 16:24], Alu.mult),
                              reads=[nsm_b], writes=[nsm_b])
                        bld.v(lambda e: e.tensor_tensor(nsm[:, 24:32], nsm[:, 24:32], nsm[:, 32:40], Alu.subtract),
                              reads=[nsm_b], writes=[nsm_b])
                        bld.v(lambda e: e.tensor_tensor(nsm[:, 32:40], nsm[:, 8:16], nsm[:, 8:16], Alu.mult),
                              reads=[nsm_b], writes=[nsm_b])
                        bld.v(lambda e: e.tensor_tensor(nsm[:, 24:32], nsm[:, 24:32], nsm[:, 32:40], Alu.mult),
                              reads=[nsm_b], writes=[nsm_b])
                        bld.s(lambda e: e.activation(nsm[:, 32:40], nsm[:, 24:32], Act.Sqrt, bias=cst[:, 1:2]),
                              reads=[nsm_b, cst_b], writes=[nsm_b])
                        bld.v(lambda e: e.reciprocal(nsm[:, 40:48], nsm[:, 32:40]), reads=[nsm_b], writes=[nsm_b])
                        bld.v(lambda e: e.tensor_tensor(nsm[:, 40:48], nsm[:, 40:48], nsm[:, 8:16], Alu.mult),
                              reads=[nsm_b], writes=[nsm_b])
                        mu3 = nsm[:, 16:24].rearrange("p (h o) -> p h o", o=1)
                        bld.v(lambda e, av=av: e.tensor_tensor(hub[:], av, bcast(av, mu3), Alu.subtract),
                              reads=[acat_b, nsm_b], writes=[hub_b])
                        w3 = wsr[:].rearrange("p (h v) -> p h v", v=128)
                        rr3 = nsm[:, 40:48].rearrange("p (h o) -> p h o", o=1)
                        bld.g(lambda e, i=i: e.tensor_tensor(wsr[:], onw[:], so[:, i, :], Alu.mult),
                              reads=[onw_b, so_b], writes=[wsr_b])
                        bld.g(lambda e: e.tensor_tensor(w3, w3, bcast(w3, rr3), Alu.mult),
                              reads=[wsr_b, nsm_b], writes=[wsr_b])
                        hgi, hgi_b = hg[i % 2], hg_b[i % 2]
                        bld.v(lambda e, o=hgi: e.tensor_tensor(o[:].rearrange("p (h v) -> p h v", v=128), hub[:], w3, Alu.mult),
                              reads=[hub_b, wsr_b], writes=[hgi_b])
                        hgTi, hgTi_b = hgT[i % 2], hgT_b[i % 2]
                        bld.dma("qS" if i % 2 == 0 else "qA",
                                lambda e, a=hgi, b=hgTi: e.dma_start(
                                    b[:], a[:].rearrange("p (j q) -> p j q", q=128), transpose=True),
                                reads=[hgi_b], writes=[hgTi_b])
                        for j in range(DC):
                            for half in range(2):
                                po, po_b = ps_o[half], ps_o_b[half]
                                bld.t(lambda e, j=j, half=half, a=hgTi, po=po: e.matmul(
                                        po[:], a[:, j, :], wdns[:, j, half * 512:(half + 1) * 512],
                                        start=(j == 0), stop=(j == DC - 1)),
                                      reads=[hgTi_b, wdns_b], writes=[po_b], track=(j == DC - 1))
                        x2i, x2i_b = x2t[i % 2], x2t_b[i % 2]
                        for half in range(2):
                            po, po_b = ps_o[half], ps_o_b[half]
                            bld.v(lambda e, half=half, x=x1t[i % 2], o=x2i, po=po: e.tensor_tensor(
                                    o[:, half * 512:(half + 1) * 512], po[:], x[:, half * 512:(half + 1) * 512], Alu.add),
                                  reads=[po_b, x1t_b[i % 2]], writes=[x2i_b])
                        bld.dma("qS", lambda e, i=i, o=x2i: e.dma_start(xs2.ap()[i * TT:(i + 1) * TT, :], o[:]),
                                reads=[x2i_b])
                bld.flush_phase("a3")

        with contextlib.ExitStack() as w2stack:
            wgqT2 = sb("wgqT2", [128, DC, DI], BF16, w2stack)
            wuqT2 = sb("wuqT2", [128, DC, DI], BF16, w2stack)
            wdqT2 = sb("wdqT2", [128, IC, D], BF16, w2stack)
            wgqT2_b, wuqT2_b, wdqT2_b = Buf(), Buf(), Buf()
            bld.dma("qS", lambda e: e.dma_start(wgqT2[:], wgc.ap().rearrange("(j p) i -> p j i", p=128)), writes=[wgqT2_b])
            bld.dma("qA", lambda e: e.dma_start(wuqT2[:], wuc.ap().rearrange("(j p) i -> p j i", p=128)), writes=[wuqT2_b])
            bld.dma("qS", lambda e: e.dma_start(wdqT2[:], wdc.ap().rearrange("(j p) i -> p j i", p=128)), writes=[wdqT2_b])
            with contextlib.ExitStack() as fstack2:
                ffn_phase("f2", xs2, out_d, wgqT2, wuqT2, wdqT2, (wgqT2_b, wuqT2_b, wdqT2_b), fstack2)

    return nc


def kernel(L0_attn_x, L0_attn_Wq, L0_attn_Wk, L0_attn_Wv, L0_attn_Wo, L0_attn_Wi,
           L0_attn_bi, L0_attn_Wf, L0_attn_bf, L0_attn_outnorm_w, L0_attn_Wdown,
           L0_ffn_w_gate, L0_ffn_w_up, L0_ffn_w_down):
    f32 = lambda a: np.ascontiguousarray(np.asarray(a), dtype=np.float32)
    x = f32(L0_attn_x)
    wif = np.ascontiguousarray(np.concatenate([f32(L0_attn_Wi), f32(L0_attn_Wf)], axis=1))
    bif = np.ascontiguousarray(np.concatenate([f32(L0_attn_bi), f32(L0_attn_bf)]).reshape(16, 1))
    onw = np.ascontiguousarray(np.broadcast_to(f32(L0_attn_outnorm_w).reshape(1, D), (128, D)))
    base = {
        "wq": f32(L0_attn_Wq), "wk": f32(L0_attn_Wk), "wv": f32(L0_attn_Wv),
        "wo": f32(L0_attn_Wo), "wif": wif, "bif": bif, "onw": onw,
        "wdn": f32(L0_attn_Wdown), "wg": f32(L0_ffn_w_gate), "wu": f32(L0_ffn_w_up),
        "wd": f32(L0_ffn_w_down),
    }
    in_maps = [dict(base, x=np.ascontiguousarray(x[c])) for c in range(8)]
    nc = build_core_program()
    res = run_bass_kernel_spmd(nc, in_maps, core_ids=list(range(8)))
    return np.stack([np.asarray(r["out"], dtype=np.float32) for r in res.results], axis=0)


# version 15
# speedup vs baseline: 1.1198x; 1.1198x over previous
import contextlib
import numpy as np

import concourse.bass as bass
import concourse.mybir as mybir
from concourse.bass_utils import run_bass_kernel_spmd

F32 = mybir.dt.float32
BF16 = mybir.dt.bfloat16
FP16 = mybir.dt.float16
Alu = mybir.AluOpType
Act = mybir.ActivationFunctionType
Ax = mybir.AxisListType

ENGINES = ("tensor", "vector", "scalar", "gpsimd", "sync")
QUEUES = {"qS": "sync", "qA": "scalar", "qG": "gpsimd"}
DMA_POOL = 10

S, D, DI, H = 2048, 1024, 2816, 8
TT = 128
NTT = S // TT
DC = D // 128
IC = DI // 128
DHQK, DHV = 64, 128
LN_EPS = 1e-5
ATT_EPS = 1e-6
CAP = 15.0
LN8 = float(np.log(8.0))


class Buf:
    __slots__ = ("w", "rs", "name")

    def __init__(self, name=""):
        self.w = None
        self.rs = []
        self.name = name


class B:

    def __init__(self, nc, stack):
        self.nc = nc
        self.stack = stack
        self.sems = {}
        self.counts = {}
        self.dma_sems = {}
        self.dma_rr = {}
        for name in ("T", "V", "Sx", "G"):
            self._mk(name)
        for q in QUEUES:
            self.dma_sems[q] = []
            self.dma_rr[q] = 0
            for k in range(DMA_POOL):
                n = f"{q}_{k}"
                self._mk(n)
                self.dma_sems[q].append(n)
        self.engine_sem = {"tensor": "T", "vector": "V", "scalar": "Sx", "gpsimd": "G"}
        self.streams = {e: [] for e in ENGINES}
        self.is_dma_sem = {s for q in QUEUES for s in self.dma_sems[q]}

    def _mk(self, name):
        self.sems[name] = self.stack.enter_context(self.nc.semaphore(name))
        self.counts[name] = 0

    def _fix(self, tok):
        s, v = tok
        if s in self.is_dma_sem:
            return (s, self.counts[s])
        return tok

    def op(self, eng, fn, reads=(), writes=(), extra=(), sem=None, amt=1, track=True):
        waits = []
        for b in reads:
            if b.w is not None:
                waits.append(self._fix(b.w))
        for b in writes:
            for t_ in b.rs:
                waits.append(self._fix(t_))
            if b.w is not None:
                waits.append(self._fix(b.w))
        waits.extend(self._fix(t_) for t_ in extra)
        tok = None
        if track:
            s = sem if sem is not None else self.engine_sem[eng]
            self.counts[s] += amt
            tok = (s, self.counts[s])
        self.streams[eng].append((fn, tuple(waits), (tok[0], amt) if tok else None))
        if tok:
            for b in reads:
                b.rs.append(tok)
            for b in writes:
                b.w = tok
                b.rs = []
        return tok

    def t(self, fn, reads=(), writes=(), extra=(), track=True):
        return self.op("tensor", fn, reads, writes, extra, track=track)

    def v(self, fn, reads=(), writes=(), extra=()):
        return self.op("vector", fn, reads, writes, extra)

    def s(self, fn, reads=(), writes=(), extra=()):
        return self.op("scalar", fn, reads, writes, extra)

    def g(self, fn, reads=(), writes=(), extra=()):
        return self.op("gpsimd", fn, reads, writes, extra)

    def vg(self, k, fn, reads=(), writes=(), extra=()):
        return self.v(fn, reads, writes, extra)

    def dma(self, queue, fn, reads=(), writes=(), extra=()):
        eng = QUEUES[queue]
        idx = self.dma_rr[queue]
        self.dma_rr[queue] = (idx + 1) % DMA_POOL
        sem = self.dma_sems[queue][idx]
        return self.op(eng, fn, reads, writes, extra, sem=sem, amt=16)

    def flush_phase(self, name=None):
        for q, eng in QUEUES.items():
            toks = [(s, self.counts[s]) for s in self.dma_sems[q] if self.counts[s] > 0]
            if toks:
                self.streams[eng].append((None, tuple(toks), None))
        streams = {e: list(s) for e, s in self.streams.items()}
        self.streams = {e: [] for e in ENGINES}
        nonempty = [e for e in ENGINES if streams[e]]
        if not nonempty:
            return
        with self.nc.Block(name) as block:
            for eng in nonempty:
                def emitter(e, _eng=eng):
                    lw = {}
                    for fn, waits, inc in streams[_eng]:
                        for (sname, v_) in waits:
                            if lw.get(sname, 0) < v_:
                                e.wait_ge(self.sems[sname], v_)
                                lw[sname] = v_
                        if fn is None:
                            continue
                        inst = fn(e)
                        if inc is not None:
                            inst.then_inc(self.sems[inc[0]], inc[1])
                getattr(block, eng)(emitter)


def bcast(full_view, small_view):
    _, b2 = bass.broadcast_tensor_aps(full_view, small_view)
    return b2


def emit_quantize(bld, nc, x_ap, x_buf, out_ap, out_buf, scr, nblk, fin_v=False):
    bm, bm_b = scr["bm"], scr["bm_b"]
    rs, rs_b = scr["rs"], scr["rs_b"]
    xn, xn_b = scr["xn"], scr["xn_b"]
    axn, axn_b = scr["axn"], scr["axn_b"]
    nA, nA_b = scr["nA"], scr["nA_b"]
    nB, nB_b = scr["nB"], scr["nB_b"]
    n = nblk * 16
    x3 = x_ap.rearrange("p (b s) -> p b s", s=16)
    bm3 = bm[:, 0:nblk].rearrange("p (b o) -> p b o", o=1)
    bld.v(lambda e: e.tensor_reduce(bm3, x3, Ax.X, Alu.max, apply_absolute_value=True),
          reads=[x_buf], writes=[bm_b])
    bld.v(lambda e: e.tensor_scalar(bm[:, 0:nblk], bm[:, 0:nblk], 1.0 / 6.0, 1e-12, Alu.mult, Alu.max),
          reads=[bm_b], writes=[bm_b])
    bld.v(lambda e: e.reciprocal(rs[:, 0:nblk], bm[:, 0:nblk]), reads=[bm_b], writes=[rs_b])
    xn3 = xn[:, 0:n].rearrange("p (b s) -> p b s", s=16)
    rs3 = rs[:, 0:nblk].rearrange("p (b o) -> p b o", o=1)
    bld.v(lambda e: e.tensor_tensor(xn3, x3, bcast(x3, rs3), Alu.mult),
          reads=[x_buf, rs_b], writes=[xn_b])
    bld.v(lambda e: e.tensor_tensor(axn[:, 0:n], xn[:, 0:n], xn[:, 0:n], Alu.mult),
          reads=[xn_b], writes=[axn_b])
    bld.s(lambda e: e.sign(xn[:, 0:n], xn[:, 0:n]), reads=[xn_b], writes=[xn_b])
    bld.vg(0, lambda e: e.tensor_single_scalar(nA[:, 0:n], axn[:, 0:n], 0.0625, Alu.is_gt),
           reads=[axn_b], writes=[nA_b])
    for k, c in enumerate((0.5625, 1.5625, 3.0625)):
        bld.vg(k + 1, lambda e, c=c: e.scalar_tensor_tensor(nA[:, 0:n], axn[:, 0:n], c, nA[:, 0:n], Alu.is_gt, Alu.add),
               reads=[axn_b, nA_b], writes=[nA_b])
    bld.vg(0, lambda e: e.tensor_single_scalar(nB[:, 0:n], axn[:, 0:n], 6.25, Alu.is_gt),
           reads=[axn_b], writes=[nB_b])
    for k, c in enumerate((12.25, 25.0, 25.0)):
        bld.vg(k + 1, lambda e, c=c: e.scalar_tensor_tensor(nB[:, 0:n], axn[:, 0:n], c, nB[:, 0:n], Alu.is_gt, Alu.add),
               reads=[axn_b, nB_b], writes=[nB_b])
    bld.vg(0, lambda e: e.scalar_tensor_tensor(nA[:, 0:n], nA[:, 0:n], 0.5, nB[:, 0:n], Alu.mult, Alu.add),
           reads=[nA_b, nB_b], writes=[nA_b])
    fin = bld.v if fin_v else bld.g
    fin(lambda e: e.tensor_tensor(nA[:, 0:n], nA[:, 0:n], xn[:, 0:n], Alu.mult),
        reads=[nA_b, xn_b], writes=[nA_b])
    o3 = out_ap.rearrange("p (b s) -> p b s", s=16)
    nA3 = nA[:, 0:n].rearrange("p (b s) -> p b s", s=16)
    fin(lambda e: e.tensor_tensor(o3, nA3, bcast(nA3, bm3), Alu.mult),
        reads=[nA_b, bm_b], writes=[out_buf])


def build_core_program():
    nc = bass.Bass("TRN2", target_bir_lowering=False)
    x_in = nc.dram_tensor("x", [S, D], F32, kind="ExternalInput")
    wq_d = nc.dram_tensor("wq", [D, 512], F32, kind="ExternalInput")
    wk_d = nc.dram_tensor("wk", [D, 512], F32, kind="ExternalInput")
    wv_d = nc.dram_tensor("wv", [D, D], F32, kind="ExternalInput")
    wo_d = nc.dram_tensor("wo", [D, D], F32, kind="ExternalInput")
    wif_d = nc.dram_tensor("wif", [D, 16], F32, kind="ExternalInput")
    bif_d = nc.dram_tensor("bif", [16, 1], F32, kind="ExternalInput")
    onw_d = nc.dram_tensor("onw", [128, D], F32, kind="ExternalInput")
    wdn_d = nc.dram_tensor("wdn", [D, D], F32, kind="ExternalInput")
    wg_d = nc.dram_tensor("wg", [DI, D], F32, kind="ExternalInput")
    wu_d = nc.dram_tensor("wu", [DI, D], F32, kind="ExternalInput")
    wd_d = nc.dram_tensor("wd", [D, DI], F32, kind="ExternalInput")
    out_d = nc.dram_tensor("out", [S, D], F32, kind="ExternalOutput")
    xs1 = nc.dram_tensor("xs1", [S, D], F32)
    xs2 = nc.dram_tensor("xs2", [S, D], F32)
    wgc = nc.dram_tensor("wgc", [D, DI], BF16)
    wuc = nc.dram_tensor("wuc", [D, DI], BF16)
    wdc = nc.dram_tensor("wdc", [DI, D], BF16)

    with contextlib.ExitStack() as gstack:
        bld = B(nc, gstack)
        sb = lambda name, shape, dt, st=gstack, **kw: st.enter_context(nc.sbuf_tensor(name, shape, dt, **kw))
        psum = lambda name, shape, dt, st=gstack: st.enter_context(nc.psum_tensor(name, shape, dt))

        cst = sb("cst", [128, 2], F32)
        cst_b = Buf("cst")
        bld.g(lambda e: e.memset(cst[:, 0:1], LN_EPS), writes=[cst_b])
        bld.g(lambda e: e.memset(cst[:, 1:2], ATT_EPS), writes=[cst_b])

        def ffn_phase(tag, x_src, x_dst, wgqT, wuqT, wdqT, w_bufs, stack):
            wg_b, wu_b, wd_b = w_bufs
            xt = [sb(f"{tag}xt{i}", [TT, D], F32, stack) for i in range(3)]
            xt_b = [Buf() for _ in range(3)]
            xln = [sb(f"{tag}xln{i}", [TT, D], F32, stack) for i in range(2)]
            xln_b = [Buf() for _ in range(2)]
            xq = [sb(f"{tag}xq{i}", [TT, D], BF16, stack) for i in range(2)]
            xq_b = [Buf() for _ in range(2)]
            xqT = [sb(f"{tag}xqT{i}", [128, DC, TT], BF16, stack, is_transpose_dest=True) for i in range(2)]
            xqT_hb = [[Buf(), Buf()] for _ in range(2)]
            hq = [sb(f"{tag}hq{i}", [TT, DI], BF16, stack) for i in range(2)]
            hq_b = [Buf() for _ in range(2)]
            hqT = [sb(f"{tag}hqT{i}", [128, IC, TT], BF16, stack, is_transpose_dest=True) for i in range(2)]
            hqT_b = [Buf() for _ in range(2)]
            hf = [sb(f"{tag}hf{i}", [TT, 512], F32, stack) for i in range(2)]
            hf_b = [Buf() for _ in range(2)]
            st_t = sb(f"{tag}st", [TT, 16], F32, stack)
            st_b = Buf()
            scr2 = []
            for si in range(2):
                scr2.append({
                    "bm": sb(f"{tag}bm{si}", [TT, 32], F32, stack), "bm_b": Buf(),
                    "rs": sb(f"{tag}rs{si}", [TT, 32], F32, stack), "rs_b": Buf(),
                    "xn": sb(f"{tag}xn{si}", [TT, 512], F32, stack), "xn_b": Buf(),
                    "axn": sb(f"{tag}axn{si}", [TT, 512], F32, stack), "axn_b": Buf(),
                    "nA": sb(f"{tag}nA{si}", [TT, 512], BF16, stack), "nA_b": Buf(),
                    "nB": sb(f"{tag}nB{si}", [TT, 512], BF16, stack), "nB_b": Buf(),
                })
            ps_g = [psum(f"{tag}psg{k}", [TT, 512], F32, stack) for k in range(3)]
            ps_g_b = [Buf() for _ in range(3)]
            ps_u = [psum(f"{tag}psu{k}", [TT, 512], F32, stack) for k in range(3)]
            ps_u_b = [Buf() for _ in range(3)]
            ps_d = [psum(f"{tag}psd{i}", [TT, 512], F32, stack) for i in range(2)]
            ps_d_b = [Buf() for _ in range(2)]

            ichunks = [(0, 512), (512, 512), (1024, 512), (1536, 512), (2048, 512), (2560, 256)]
            pend = []

            def emit_down(item):
                dti, dhqT, dhqT_b, dxt, dxt_b, dxn, dxn_b = item
                for j in range(IC):
                    for half in range(2):
                        pd, pd_b = ps_d[half], ps_d_b[half]
                        bld.t(lambda e, j=j, half=half, a=dhqT, w=wdqT, pd=pd: e.matmul(
                                pd[:], a[:, j, :], w[:, j, half * 512:(half + 1) * 512],
                                start=(j == 0), stop=(j == IC - 1)),
                              reads=[dhqT_b, wd_b], writes=[pd_b], track=(j == IC - 1))
                for half in range(2):
                    pd, pd_b = ps_d[half], ps_d_b[half]
                    bld.v(lambda e, half=half, x=dxt, o=dxn, pd=pd: e.scalar_tensor_tensor(
                            o[:, half * 512:(half + 1) * 512], pd[:], 0.5, x[:, half * 512:(half + 1) * 512],
                            Alu.mult, Alu.add),
                          reads=[pd_b, dxt_b], writes=[dxn_b])
                bld.dma("qS", lambda e, dti=dti, o=dxn: e.dma_start(x_dst.ap()[dti * TT:(dti + 1) * TT, :], o[:]),
                        reads=[dxn_b])

            def xside(ti):
                xti, xti_b = xt[ti % 3], xt_b[ti % 3]
                xlni, xlni_b = xln[ti % 2], xln_b[ti % 2]
                xqi, xqi_b = xq[ti % 2], xq_b[ti % 2]
                xqTi, xqTi_hb = xqT[ti % 2], xqT_hb[ti % 2]
                bld.dma("qG", lambda e, ti=ti, x=xti: e.dma_start(x[:], x_src.ap()[ti * TT:(ti + 1) * TT, :]),
                        writes=[xti_b])
                bld.v(lambda e, x=xti, s=st_t: e.bn_stats(s[:, 0:6], x[:, 0:512]),
                      reads=[xti_b], writes=[st_b])
                bld.v(lambda e, x=xti, s=st_t: e.bn_stats(s[:, 6:12], x[:, 512:1024]),
                      reads=[xti_b], writes=[st_b])
                bld.v(lambda e, s=st_t: e.bn_aggr(s[:, 12:14], s[:, 0:12]),
                      reads=[st_b], writes=[st_b])
                bld.s(lambda e, s=st_t: e.activation(s[:, 14:15], s[:, 13:14], Act.Sqrt, bias=cst[:, 0:1]),
                      reads=[st_b, cst_b], writes=[st_b])
                bld.v(lambda e, s=st_t: e.reciprocal(s[:, 15:16], s[:, 14:15]),
                      reads=[st_b], writes=[st_b])
                bld.v(lambda e, x=xti, o=xlni, s=st_t: e.tensor_scalar(o[:], x[:], s[:, 12:13], s[:, 15:16], Alu.subtract, Alu.mult),
                      reads=[xti_b, st_b], writes=[xlni_b])
                for hh in range(2):
                    emit_quantize(bld, nc, xlni[:, hh * 512:(hh + 1) * 512], xlni_b,
                                  xqi[:, hh * 512:(hh + 1) * 512], xqi_b, scr2[hh], 32)
                    bld.dma("qS" if (ti + hh) % 2 == 0 else "qA",
                            lambda e, hh=hh, a=xqi, b=xqTi: e.dma_start(
                                b[:, hh * 4:(hh + 1) * 4, :],
                                a[:, hh * 512:(hh + 1) * 512].rearrange("p (j q) -> p j q", q=128), transpose=True),
                            reads=[xqi_b], writes=[xqT_hb[ti % 2][hh]])

            xside(0)
            for ti in range(NTT):
                xti, xti_b = xt[ti % 3], xt_b[ti % 3]
                xlni, xlni_b = xln[ti % 2], xln_b[ti % 2]
                xqTi, xqTi_hb = xqT[ti % 2], xqT_hb[ti % 2]
                hqi, hqi_b = hq[ti % 2], hq_b[ti % 2]
                hqTi, hqTi_b = hqT[ti % 2], hqT_b[ti % 2]
                if ti + 1 < NTT:
                    xside(ti + 1)
                for grp in range(2):
                    cis = [grp * 3, grp * 3 + 1, grp * 3 + 2]
                    for j in range(DC):
                        for k, ci in enumerate(cis):
                            i0, iw = ichunks[ci]
                            bld.t(lambda e, j=j, i0=i0, iw=iw, k=k, a=xqTi, w=wgqT: e.matmul(
                                    ps_g[k][:, 0:iw], a[:, j, :], w[:, j, i0:i0 + iw],
                                    start=(j == 0), stop=(j == DC - 1)),
                                  reads=[xqTi_hb[j // 4], wg_b], writes=[ps_g_b[k]], track=(j == DC - 1))
                            bld.t(lambda e, j=j, i0=i0, iw=iw, k=k, a=xqTi, w=wuqT: e.matmul(
                                    ps_u[k][:, 0:iw], a[:, j, :], w[:, j, i0:i0 + iw],
                                    start=(j == 0), stop=(j == DC - 1)),
                                  reads=[xqT_hb[ti % 2][j // 4], wu_b], writes=[ps_u_b[k]], track=(j == DC - 1))
                    for k, ci in enumerate(cis):
                        i0, iw = ichunks[ci]
                        hfc, hfc_b = hf[ci % 2], hf_b[ci % 2]
                        bld.s(lambda e, iw=iw, k=k, o=hfc: e.activation(o[:, 0:iw], ps_g[k][:, 0:iw], Act.Silu),
                              reads=[ps_g_b[k]], writes=[hfc_b])
                        bld.v(lambda e, iw=iw, k=k, o=hfc: e.tensor_tensor(o[:, 0:iw], o[:, 0:iw], ps_u[k][:, 0:iw], Alu.mult),
                              reads=[hfc_b, ps_u_b[k]], writes=[hfc_b])
                        emit_quantize(bld, nc, hfc[:, 0:iw], hfc_b, hqi[:, i0:i0 + iw], hqi_b, scr2[ci % 2], iw // 16)
                    if grp == 0 and pend:
                        emit_down(pend.pop(0))
                bld.dma("qA" if ti % 2 == 0 else "qS",
                        lambda e, a=hqi, b=hqTi: e.dma_start(
                            b[:], a[:].rearrange("p (j q) -> p j q", q=128), transpose=True),
                        reads=[hqi_b], writes=[hqTi_b])
                pend.append((ti, hqTi, hqTi_b, xti, xti_b, xlni, xlni_b))
                if len(pend) > 1:
                    emit_down(pend.pop(0))
            while pend:
                emit_down(pend.pop(0))
            bld.flush_phase(tag)

        with contextlib.ExitStack() as wstack:
            wgqT = sb("wgqT", [128, DC, DI], BF16, wstack, is_transpose_dest=True)
            wuqT = sb("wuqT", [128, DC, DI], BF16, wstack, is_transpose_dest=True)
            wdqT = sb("wdqT", [128, IC, D], BF16, wstack, is_transpose_dest=True)
            wgqT_b, wuqT_b, wdqT_b = Buf("wgqT"), Buf("wuqT"), Buf("wdqT")

            wsstack = wstack.enter_context(contextlib.ExitStack())
            wchunk = [sb("wch0", [128, DI], F32, wsstack), sb("wch1", [128, D], F32, wsstack)]
            wchunk_b = [Buf() for _ in range(2)]
            wqb = [sb("wqb0", [128, DI], BF16, wsstack), sb("wqb1", [128, D], BF16, wsstack)]
            wqb_b = [Buf() for _ in range(2)]
            wscr2 = []
            for si, w_ in ((0, DI), (1, D)):
                wscr2.append({
                    "bm": sb(f"wbm{si}", [128, w_ // 16], F32, wsstack), "bm_b": Buf(),
                    "rs": sb(f"wrs{si}", [128, w_ // 16], F32, wsstack), "rs_b": Buf(),
                    "xn": sb(f"wxn{si}", [128, w_], F32, wsstack), "xn_b": Buf(),
                    "axn": sb(f"waxn{si}", [128, w_], F32, wsstack), "axn_b": Buf(),
                    "nA": sb(f"wnA{si}", [128, w_], BF16, wsstack), "nA_b": Buf(),
                    "nB": sb(f"wnB{si}", [128, w_], BF16, wsstack), "nB_b": Buf(),
                })
            for wi, (wsrc, wdstT, wdstT_b) in enumerate(((wg_d, wgqT, wgqT_b), (wu_d, wuqT, wuqT_b))):
                for c in range(IC):
                    wc, wc_b = wchunk[c % 2], wchunk_b[c % 2]
                    wq_, wq_b_ = wqb[c % 2], wqb_b[c % 2]
                    bld.dma("qS", lambda e, c=c, wsrc=wsrc, wc=wc: e.dma_start(
                            wc[:, 0:D], wsrc.ap()[c * 128:(c + 1) * 128, :]), writes=[wc_b])
                    emit_quantize(bld, nc, wc[:, 0:D], wc_b, wq_[:, 0:D], wq_b_, wscr2[c % 2], D // 16, fin_v=True)
                    bld.dma("qS" if c % 2 == 0 else "qA",
                            lambda e, c=c, a=wq_, b=wdstT: e.dma_start(
                                b[:, :, c * 128:(c + 1) * 128], a[:, 0:D].rearrange("p (j q) -> p j q", q=128), transpose=True),
                            reads=[wq_b_], writes=[wdstT_b])
            for c in range(DC):
                wc, wc_b = wchunk[0], wchunk_b[0]
                wq_, wq_b_ = wqb[0], wqb_b[0]
                bld.dma("qS", lambda e, c=c, wc=wc: e.dma_start(
                        wc[:], wd_d.ap()[c * 128:(c + 1) * 128, :]), writes=[wc_b])
                emit_quantize(bld, nc, wc[:], wc_b, wq_[:], wq_b_, wscr2[0], DI // 16, fin_v=True)
                bld.dma("qS" if c % 2 == 0 else "qA",
                        lambda e, c=c, a=wq_: e.dma_start(
                            wdqT[:, :, c * 128:(c + 1) * 128], a[:].rearrange("p (j q) -> p j q", q=128), transpose=True),
                        reads=[wq_b_], writes=[wdqT_b])
            bld.dma("qS", lambda e: e.dma_start(wgc.ap().rearrange("(j p) i -> p j i", p=128), wgqT[:]),
                    reads=[wgqT_b])
            bld.dma("qS", lambda e: e.dma_start(wuc.ap().rearrange("(j p) i -> p j i", p=128), wuqT[:]),
                    reads=[wuqT_b])
            bld.dma("qS", lambda e: e.dma_start(wdc.ap().rearrange("(j p) i -> p j i", p=128), wdqT[:]),
                    reads=[wdqT_b])
            bld.flush_phase("wprep")
            wsstack.close()

            with contextlib.ExitStack() as fstack:
                ffn_phase("f1", x_in, xs1, wgqT, wuqT, wdqT, (wgqT_b, wuqT_b, wdqT_b), fstack)

        with contextlib.ExitStack() as astack:
            idf = sb("idf", [128, 128], F32, astack)
            idf_b = Buf()
            maskt = sb("maskt", [TT, TT], BF16, astack)
            maskt_b = Buf()
            bld.g(lambda e: e.memset(idf[:], 0.0), writes=[idf_b])
            bld.g(lambda e: e.affine_select(out=idf[:], in_=idf[:], compare_op=Alu.not_equal,
                                            fill=1.0, base=0, pattern=[[-1, 128]], channel_multiplier=1),
                  reads=[idf_b], writes=[idf_b])
            bld.g(lambda e: e.memset(maskt[:], 1.0), writes=[maskt_b])
            bld.g(lambda e: e.affine_select(out=maskt[:], in_=maskt[:], compare_op=Alu.is_ge,
                                            fill=0.0, base=0, pattern=[[1, TT]], channel_multiplier=-1),
                  reads=[maskt_b], writes=[maskt_b])

            QT = sb("QT", [128, 4, S], BF16, astack)
            QT_b = Buf("QT")
            KT = sb("KT", [128, 4, S], BF16, astack)
            KT_b = Buf("KT")
            VH = sb("VH", [TT, NTT, H, 132], BF16, astack)
            VH_b = Buf("VH")
            so = sb("so", [TT, NTT, D], BF16, astack)
            so_b = Buf("so")
            onw = sb("onw_sb", [128, D], F32, astack)
            onw_b = Buf()
            ecol_t = sb("ecol_t", [128, NTT, 128], F32, astack)
            ecol_t_b = Buf()
            erow_t = sb("erow_t", [128, 128], F32, astack)
            erow_t_b = Buf()
            emb_t = sb("emb_t", [128, 128], F32, astack)
            emb_t_b = Buf()
            bld.dma("qS", lambda e: e.dma_start(onw[:], onw_d.ap()), writes=[onw_b])
            bld.g(lambda e: e.memset(VH[:, :, :, 128:129], 1.0), writes=[VH_b])
            with contextlib.ExitStack() as a12stack:
                ifsb = sb("ifsb", [TT, NTT, 16], F32, a12stack)
                ifsb_b = Buf()
                ir_row = sb("ir_row", [8, S], F32, a12stack)
                ir_b = Buf()
                fr_row = sb("fr_row", [8, S], F32, a12stack)
                fr_b = Buf()
                bi_s = sb("bi_s", [8, 2], F32, a12stack)
                bi_b = Buf()
                bf_s = sb("bf_s", [8, 2], F32, a12stack)
                bf_b = Buf()
                bld.dma("qS", lambda e: e.dma_start(bi_s[:, 0:1], bif_d.ap()[0:8, :]), writes=[bi_b])
                bld.dma("qS", lambda e: e.dma_start(bf_s[:, 0:1], bif_d.ap()[8:16, :]), writes=[bf_b])
                bld.v(lambda e: e.tensor_single_scalar(bi_s[:, 1:2], bi_s[:, 0:1], 1.0 / CAP, Alu.mult),
                      reads=[bi_b], writes=[bi_b])
                bld.v(lambda e: e.tensor_single_scalar(bf_s[:, 1:2], bf_s[:, 0:1], 1.0 / CAP, Alu.mult),
                      reads=[bf_b], writes=[bf_b])

                with contextlib.ExitStack() as pstack:
                    wqs = sb("wqs", [128, DC, 512], BF16, pstack)
                    wqs_b = Buf()
                    wks = sb("wks", [128, DC, 512], BF16, pstack)
                    wks_b = Buf()
                    wvs = sb("wvs", [128, DC, D], BF16, pstack)
                    wvs_b = Buf()
                    wos = sb("wos", [128, DC, D], BF16, pstack)
                    wos_b = Buf()
                    wifs = sb("wifs", [128, DC, 16], BF16, pstack)
                    wifs_b = Buf()
                    xT2t = [sb(f"xT2t{i}", [128, DC, TT], BF16, pstack, is_transpose_dest=True) for i in range(2)]
                    xT2t_b = [Buf() for _ in range(2)]
                    xt = [sb(f"a1xt{i}", [TT, D], F32, pstack) for i in range(2)]
                    xt_b = [Buf() for _ in range(2)]
                    xln = [sb(f"a1xln{i}", [TT, D], BF16, pstack) for i in range(2)]
                    xln_b = [Buf() for _ in range(2)]
                    st_t = sb("a1st", [TT, 16], F32, pstack)
                    st_b = Buf()
                    ps_p = [psum(f"a1ps{i}", [128, 512], F32, pstack) for i in range(2)]
                    ps_p_b = [Buf() for _ in range(2)]
                    ps_v = [psum(f"a1psv{i}", [128, 512], F32, pstack) for i in range(2)]
                    ps_v_b = [Buf() for _ in range(2)]
                    ps_if = psum("a1psif", [128, 16], F32, pstack)
                    ps_if_b = Buf()
                    ps_t = psum("a1pst", [16, 128], F32, pstack)
                    ps_t_b = Buf()
                    bld.dma("qG", lambda e: e.dma_start(wqs[:], wq_d.ap().rearrange("(j p) q -> p j q", p=128)), writes=[wqs_b])
                    bld.dma("qG", lambda e: e.dma_start(wks[:], wk_d.ap().rearrange("(j p) q -> p j q", p=128)), writes=[wks_b])
                    bld.dma("qG", lambda e: e.dma_start(wvs[:], wv_d.ap().rearrange("(j p) q -> p j q", p=128)), writes=[wvs_b])
                    bld.dma("qG", lambda e: e.dma_start(wos[:], wo_d.ap().rearrange("(j p) q -> p j q", p=128)), writes=[wos_b])
                    bld.dma("qG", lambda e: e.dma_start(wifs[:], wif_d.ap().rearrange("(j p) q -> p j q", p=128)), writes=[wifs_b])
                    def a1_xside(ti):
                        xti, xti_b = xt[ti % 2], xt_b[ti % 2]
                        xlni, xlni_b = xln[ti % 2], xln_b[ti % 2]
                        xTt, xTt_b = xT2t[ti % 2], xT2t_b[ti % 2]
                        bld.dma("qG", lambda e, ti=ti, x=xti: e.dma_start(x[:], xs1.ap()[ti * TT:(ti + 1) * TT, :]),
                                writes=[xti_b])
                        bld.v(lambda e, x=xti, s=st_t: e.bn_stats(s[:, 0:6], x[:, 0:512]), reads=[xti_b], writes=[st_b])
                        bld.v(lambda e, x=xti, s=st_t: e.bn_stats(s[:, 6:12], x[:, 512:1024]), reads=[xti_b], writes=[st_b])
                        bld.v(lambda e, s=st_t: e.bn_aggr(s[:, 12:14], s[:, 0:12]), reads=[st_b], writes=[st_b])
                        bld.s(lambda e, s=st_t: e.activation(s[:, 14:15], s[:, 13:14], Act.Sqrt, bias=cst[:, 0:1]),
                              reads=[st_b, cst_b], writes=[st_b])
                        bld.v(lambda e, s=st_t: e.reciprocal(s[:, 15:16], s[:, 14:15]), reads=[st_b], writes=[st_b])
                        bld.v(lambda e, x=xti, o=xlni, s=st_t: e.tensor_scalar(o[:], x[:], s[:, 12:13], s[:, 15:16], Alu.subtract, Alu.mult),
                              reads=[xti_b, st_b], writes=[xlni_b])
                        bld.dma("qS" if ti % 2 == 0 else "qA",
                                lambda e, a=xlni, b=xTt: e.dma_start(
                                    b[:], a[:].rearrange("p (j q) -> p j q", q=128), transpose=True),
                                reads=[xlni_b], writes=[xTt_b])

                    a1_xside(0)
                    for ti in range(NTT):
                        xTt, xTt_b = xT2t[ti % 2], xT2t_b[ti % 2]
                        if ti + 1 < NTT:
                            a1_xside(ti + 1)
                        for gq in range(4):
                            pp, pp_b = ps_p[gq % 2], ps_p_b[gq % 2]
                            for j in range(DC):
                                bld.t(lambda e, j=j, gq=gq, pp=pp, a=xTt: e.matmul(
                                        pp[:, 0:TT], wqs[:, j, gq * 128:(gq + 1) * 128], a[:, j, :],
                                        start=(j == 0), stop=(j == DC - 1)),
                                      reads=[wqs_b, xTt_b], writes=[pp_b], track=(j == DC - 1))
                            bld.v(lambda e, gq=gq, ti=ti, pp=pp: e.tensor_copy(QT[:, gq, ti * TT:(ti + 1) * TT], pp[:, 0:TT]),
                                  reads=[pp_b], writes=[QT_b])
                            pp, pp_b = ps_p[(gq + 1) % 2], ps_p_b[(gq + 1) % 2]
                            for j in range(DC):
                                bld.t(lambda e, j=j, gq=gq, pp=pp, a=xTt: e.matmul(
                                        pp[:, 0:TT], wks[:, j, gq * 128:(gq + 1) * 128], a[:, j, :],
                                        start=(j == 0), stop=(j == DC - 1)),
                                      reads=[wks_b, xTt_b], writes=[pp_b], track=(j == DC - 1))
                            bld.v(lambda e, gq=gq, ti=ti, pp=pp: e.tensor_copy(KT[:, gq, ti * TT:(ti + 1) * TT], pp[:, 0:TT]),
                                  reads=[pp_b], writes=[KT_b])
                        for j in range(DC):
                            for half in range(2):
                                bld.t(lambda e, j=j, half=half, pp=ps_v[half], a=xTt: e.matmul(
                                        pp[:], a[:, j, :], wvs[:, j, half * 512:(half + 1) * 512],
                                        start=(j == 0), stop=(j == DC - 1)),
                                      reads=[wvs_b, xTt_b], writes=[ps_v_b[half]], track=(j == DC - 1))
                                bld.t(lambda e, j=j, half=half, pp=ps_p[half], a=xTt: e.matmul(
                                        pp[:], a[:, j, :], wos[:, j, half * 512:(half + 1) * 512],
                                        start=(j == 0), stop=(j == DC - 1)),
                                      reads=[wos_b, xTt_b], writes=[ps_p_b[half]], track=(j == DC - 1))
                            bld.t(lambda e, j=j, a=xTt: e.matmul(
                                    ps_if[:], a[:, j, :], wifs[:, j, :],
                                    start=(j == 0), stop=(j == DC - 1)),
                                  reads=[wifs_b, xTt_b], writes=[ps_if_b], track=(j == DC - 1))
                        for half in range(2):
                            bld.v(lambda e, ti=ti, half=half, pp=ps_v[half]: e.tensor_copy(
                                    VH[:, ti, half * 4:(half + 1) * 4, 0:128], pp[:].rearrange("p (h v) -> p h v", v=128)),
                                  reads=[ps_v_b[half]], writes=[VH_b])
                            bld.s(lambda e, ti=ti, half=half, pp=ps_p[half]: e.activation(
                                    so[:, ti, half * 512:(half + 1) * 512], pp[:], Act.Sigmoid),
                                  reads=[ps_p_b[half]], writes=[so_b])
                        bld.v(lambda e, ti=ti: e.tensor_copy(ifsb[:, ti, :], ps_if[:]),
                              reads=[ps_if_b], writes=[ifsb_b])
                        bld.t(lambda e, ti=ti: e.transpose(ps_t[0:8, :], ifsb[:, ti, 0:8], idf[:]),
                              reads=[ifsb_b, idf_b], writes=[ps_t_b])
                        bld.v(lambda e, ti=ti: e.tensor_copy(ir_row[:, ti * TT:(ti + 1) * TT], ps_t[0:8, :]),
                              reads=[ps_t_b], writes=[ir_b])
                        bld.t(lambda e, ti=ti: e.transpose(ps_t[0:8, :], ifsb[:, ti, 8:16], idf[:]),
                              reads=[ifsb_b, idf_b], writes=[ps_t_b])
                        bld.v(lambda e, ti=ti: e.tensor_copy(fr_row[:, ti * TT:(ti + 1) * TT], ps_t[0:8, :]),
                              reads=[ps_t_b], writes=[fr_b])
                    bld.flush_phase("a1")
                gr_row = sb("gr_row", [8, S], F32, a12stack)
                gr_b = Buf()
                tmp_row = sb("tmp_row", [8, S], F32, a12stack)
                tmp_b = Buf()
                lfcr = sb("lfcr", [8, S], F32, a12stack)
                lfcr_b = Buf()
                Mr = sb("Mr", [8, S], F32, a12stack)
                Mr_b = Buf()
                M_loc = sb("M_loc", [128, TT], F32, a12stack)
                M_loc_b = Buf()
                nm_loc = sb("nm_loc", [128, TT], F32, a12stack)
                nm_loc_b = Buf()
                g_rep = sb("g_rep", [128, S], F32, a12stack)
                g_rep_b = Buf()
                PL = sb("PL", [128, 2], F32, a12stack)
                PL_b = Buf()
                ps_tr = psum("a2pst", [128, 128], F32, a12stack)
                ps_tr_b = Buf()
                bld.s(lambda e: e.activation(ir_row[:], ir_row[:], Act.Tanh, bias=bi_s[:, 1:2], scale=1.0 / CAP),
                      reads=[ir_b, bi_b], writes=[ir_b])
                bld.v(lambda e: e.tensor_single_scalar(ir_row[:], ir_row[:], CAP, Alu.mult),
                      reads=[ir_b], writes=[ir_b])
                bld.s(lambda e: e.activation(fr_row[:], fr_row[:], Act.Tanh, bias=bf_s[:, 1:2], scale=1.0 / CAP),
                      reads=[fr_b, bf_b], writes=[fr_b])
                bld.v(lambda e: e.tensor_single_scalar(fr_row[:], fr_row[:], CAP, Alu.mult),
                      reads=[fr_b], writes=[fr_b])
                bld.s(lambda e: e.activation(tmp_row[:], fr_row[:], Act.Abs),
                      reads=[fr_b], writes=[tmp_b])
                bld.s(lambda e: e.activation(tmp_row[:], tmp_row[:], Act.Exp, scale=-1.0),
                      reads=[tmp_b], writes=[tmp_b])
                bld.s(lambda e: e.activation(tmp_row[:], tmp_row[:], Act.Ln, bias=1.0),
                      reads=[tmp_b], writes=[tmp_b])
                bld.v(lambda e: e.scalar_tensor_tensor(tmp_row[:], fr_row[:], 0.0, tmp_row[:], Alu.min, Alu.subtract),
                      reads=[fr_b, tmp_b], writes=[tmp_b])
                bld.v(lambda e: e.tensor_tensor_scan(lfcr[:], tmp_row[:], tmp_row[:], 0.0, Alu.add, Alu.bypass),
                      reads=[tmp_b], writes=[lfcr_b])
                bld.v(lambda e: e.tensor_sub(gr_row[:], ir_row[:], lfcr[:]),
                      reads=[ir_b, lfcr_b], writes=[gr_b])
                bld.v(lambda e: e.tensor_tensor_scan(Mr[:], gr_row[:], gr_row[:], -1e30, Alu.max, Alu.bypass),
                      reads=[gr_b], writes=[Mr_b])
                bld.v(lambda e: e.scalar_tensor_tensor(tmp_row[:], lfcr[:], -1.0, Mr[:], Alu.mult, Alu.subtract),
                      reads=[lfcr_b, Mr_b], writes=[tmp_b])
                for i in range(NTT):
                    bld.dma("qS", lambda e, i=i: e.dma_start(M_loc[i * 8:(i + 1) * 8, :], Mr[:, i * TT:(i + 1) * TT]),
                            reads=[Mr_b], writes=[M_loc_b])
                    bld.dma("qA", lambda e, i=i: e.dma_start(nm_loc[i * 8:(i + 1) * 8, :], tmp_row[:, i * TT:(i + 1) * TT]),
                            reads=[tmp_b], writes=[nm_loc_b])
                    bld.dma("qG", lambda e, i=i: e.dma_start(g_rep[i * 8:(i + 1) * 8, :], gr_row[:]),
                            reads=[gr_b], writes=[g_rep_b])
                bld.v(lambda e: e.tensor_copy(PL[:, 0:1], M_loc[:, TT - 1:TT]), reads=[M_loc_b], writes=[PL_b])
                bld.v(lambda e: e.tensor_single_scalar(PL[:, 1:2], PL[:, 0:1], LN8, Alu.add), reads=[PL_b], writes=[PL_b])
                bld.v(lambda e: e.tensor_scalar(g_rep[:], g_rep[:], PL[:, 1:2], 0.0, Alu.subtract, Alu.min),
                      reads=[g_rep_b, PL_b], writes=[g_rep_b])
                bld.s(lambda e: e.activation(g_rep[:], g_rep[:], Act.Exp), reads=[g_rep_b], writes=[g_rep_b])
                for tau in range(NTT):
                    bld.t(lambda e, tau=tau: e.transpose(ps_tr[:], g_rep[:, tau * TT:(tau + 1) * TT], idf[:]),
                          reads=[g_rep_b, idf_b], writes=[ps_tr_b])
                    bld.v(lambda e, tau=tau: e.tensor_copy(ecol_t[:, tau, :], ps_tr[:]),
                          reads=[ps_tr_b], writes=[ecol_t_b])
                bld.v(lambda e: e.tensor_scalar(M_loc[:], M_loc[:], PL[:, 0:1], -85.0, Alu.subtract, Alu.max),
                      reads=[M_loc_b, PL_b], writes=[M_loc_b])
                bld.s(lambda e: e.activation(M_loc[:], M_loc[:], Act.Exp, scale=-1.0), reads=[M_loc_b], writes=[M_loc_b])
                bld.t(lambda e: e.transpose(ps_tr[:], M_loc[:], idf[:]), reads=[M_loc_b, idf_b], writes=[ps_tr_b])
                bld.v(lambda e: e.tensor_copy(erow_t[:], ps_tr[:]), reads=[ps_tr_b], writes=[erow_t_b])
                bld.s(lambda e: e.activation(nm_loc[:], nm_loc[:], Act.Exp), reads=[nm_loc_b], writes=[nm_loc_b])
                bld.t(lambda e: e.transpose(ps_tr[:], nm_loc[:], idf[:]), reads=[nm_loc_b, idf_b], writes=[ps_tr_b])
                bld.v(lambda e: e.tensor_copy(emb_t[:], ps_tr[:]), reads=[ps_tr_b], writes=[emb_t_b])
                bld.flush_phase("a2")

            with contextlib.ExitStack() as astack3:
                wdns = sb("wdns", [128, DC, D], BF16, astack3)
                wdns_b = Buf()
                bld.dma("qG", lambda e: e.dma_start(wdns[:], wdn_d.ap().rearrange("(j p) q -> p j q", p=128)),
                        writes=[wdns_b])
                ct = [sb(f"ct{i}", [TT, 512], BF16, astack3) for i in range(2)]
                ct_b = [Buf() for _ in range(2)]
                acat = sb("acat", [TT, 4, H, 132], F32, astack3)
                acat_b = Buf()
                nsm = sb("nsm", [TT, 8 * 8], F32, astack3)
                nsm_b = Buf()
                sqt = sb("sqt", [TT, H, 128], F32, astack3)
                sqt_b = Buf()
                hub = sb("hub", [TT, H, 128], F32, astack3)
                hub_b = Buf()
                wsr = sb("wsr", [TT, D], F32, astack3)
                wsr_b = Buf()
                hg = [sb(f"hg{i}", [TT, D], BF16, astack3) for i in range(2)]
                hg_b = [Buf() for _ in range(2)]
                hgT = [sb(f"hgT{i}", [128, DC, TT], BF16, astack3, is_transpose_dest=True) for i in range(2)]
                hgT_b = [Buf() for _ in range(2)]
                x1t = [sb(f"x1t{i}", [TT, D], F32, astack3) for i in range(2)]
                x1t_b = [Buf() for _ in range(2)]
                x2t = [sb(f"x2t{i}", [TT, D], F32, astack3) for i in range(2)]
                x2t_b = [Buf() for _ in range(2)]
                ps_sc = [psum(f"pssc{i}", [TT, 512], F32, astack3) for i in range(2)]
                ps_sc_b = [Buf() for _ in range(2)]
                ps_a = [psum(f"psa{i}", [TT, 132], F32, astack3) for i in range(4)]
                ps_a_b = [Buf() for _ in range(4)]
                ps_o = [psum(f"pso{i}", [TT, 512], F32, astack3) for i in range(2)]
                ps_o_b = [Buf() for _ in range(2)]

                for i4 in range(4):
                    for h in range(H):
                        gq, sub = h // 2, (h % 2) * 64

                        def emit_scores(tau, i4=i4, gq=gq, sub=sub):
                            cb = tau % 2
                            psc, psc_b = ps_sc[cb], ps_sc_b[cb]
                            bld.t(lambda e, i4=i4, tau=tau, gq=gq, sub=sub, psc=psc: e.matmul(
                                    psc[:], KT[sub:sub + 64, gq, tau * TT:(tau + 1) * TT],
                                    QT[sub:sub + 64, gq, i4 * 512:(i4 + 1) * 512], start=True, stop=True),
                                  reads=[KT_b, QT_b], writes=[psc_b])

                        emit_scores(0)
                        for tau in range(4 * i4 + 4):
                            cb = tau % 2
                            psc, psc_b = ps_sc[cb], ps_sc_b[cb]
                            if tau + 1 < 4 * i4 + 4:
                                emit_scores(tau + 1)
                            for il in range(4):
                                i = 4 * i4 + il
                                if tau > i:
                                    continue
                                col = i * 8 + h
                                if tau == i:
                                    bld.v(lambda e, tau=tau, col=col, il=il, cbuf=ct[cb], psc=psc: e.scalar_tensor_tensor(
                                            cbuf[:, il * TT:(il + 1) * TT], psc[:, il * TT:(il + 1) * TT],
                                            ecol_t[:, tau, col:col + 1], maskt[:], Alu.mult, Alu.mult),
                                          reads=[psc_b, ecol_t_b, maskt_b], writes=[ct_b[cb]])
                                elif il % 2 == 0:
                                    bld.s(lambda e, tau=tau, col=col, il=il, cbuf=ct[cb], psc=psc: e.activation(
                                            cbuf[:, il * TT:(il + 1) * TT], psc[:, il * TT:(il + 1) * TT],
                                            Act.Copy, scale=ecol_t[:, tau, col:col + 1]),
                                          reads=[psc_b, ecol_t_b], writes=[ct_b[cb]])
                                else:
                                    bld.v(lambda e, tau=tau, col=col, il=il, cbuf=ct[cb], psc=psc: e.tensor_scalar(
                                            cbuf[:, il * TT:(il + 1) * TT], psc[:, il * TT:(il + 1) * TT],
                                            ecol_t[:, tau, col:col + 1], None, Alu.mult),
                                          reads=[psc_b, ecol_t_b], writes=[ct_b[cb]])
                                bld.t(lambda e, i=i, tau=tau, h=h, il=il, cbuf=ct[cb], pa=ps_a[il]: e.matmul(
                                        pa[:, 0:129], cbuf[:, il * TT:(il + 1) * TT], VH[:, tau, h, 0:129],
                                        start=(tau == 0), stop=(tau == i)),
                                      reads=[ct_b[cb], VH_b], writes=[ps_a_b[il]], track=(tau == i))
                        for il in range(4):
                            col = (4 * i4 + il) * 8 + h
                            bld.v(lambda e, h=h, il=il, col=col, pa=ps_a[il]: e.tensor_scalar(
                                    acat[:, il, h, 0:129], pa[:, 0:129], erow_t[:, col:col + 1], None, Alu.mult),
                                  reads=[ps_a_b[il], erow_t_b], writes=[acat_b])
                    for il in range(4):
                        i = 4 * i4 + il
                        bld.dma("qG", lambda e, i=i, x=x1t[i % 2]: e.dma_start(x[:], xs1.ap()[i * TT:(i + 1) * TT, :]),
                                writes=[x1t_b[i % 2]])
                        c0 = i * 8
                        rsv = acat[:, il, :, 128:129]
                        bld.v(lambda e, rsv=rsv: e.tensor_scalar(nsm[:, 0:8], rsv, -1.0, None, Alu.mult),
                              reads=[acat_b], writes=[nsm_b])
                        bld.v(lambda e, rsv=rsv: e.tensor_tensor(nsm[:, 0:8].rearrange("p (h o) -> p h o", o=1), rsv,
                                                        nsm[:, 0:8].rearrange("p (h o) -> p h o", o=1), Alu.max),
                              reads=[acat_b, nsm_b], writes=[nsm_b])
                        bld.v(lambda e, c0=c0: e.tensor_tensor(nsm[:, 0:8], nsm[:, 0:8], emb_t[:, c0:c0 + 8], Alu.max),
                              reads=[nsm_b, emb_t_b], writes=[nsm_b])
                        bld.v(lambda e: e.tensor_single_scalar(nsm[:, 0:8], nsm[:, 0:8], ATT_EPS, Alu.add),
                              reads=[nsm_b], writes=[nsm_b])
                        bld.v(lambda e: e.reciprocal(nsm[:, 8:16], nsm[:, 0:8]), reads=[nsm_b], writes=[nsm_b])
                        av = acat[:, il, :, 0:128]
                        bld.v(lambda e, av=av: e.tensor_reduce(nsm[:, 16:24].rearrange("p (h o) -> p h o", o=1), av, Ax.X, Alu.add),
                              reads=[acat_b], writes=[nsm_b])
                        bld.s(lambda e, av=av: e.activation(sqt[:], av, Act.Square), reads=[acat_b], writes=[sqt_b])
                        bld.v(lambda e: e.tensor_reduce(nsm[:, 24:32].rearrange("p (h o) -> p h o", o=1), sqt[:], Ax.X, Alu.add),
                              reads=[sqt_b], writes=[nsm_b])
                        bld.v(lambda e: e.tensor_single_scalar(nsm[:, 16:24], nsm[:, 16:24], 1.0 / 128.0, Alu.mult),
                              reads=[nsm_b], writes=[nsm_b])
                        bld.v(lambda e: e.tensor_single_scalar(nsm[:, 24:32], nsm[:, 24:32], 1.0 / 128.0, Alu.mult),
                              reads=[nsm_b], writes=[nsm_b])
                        bld.v(lambda e: e.tensor_tensor(nsm[:, 32:40], nsm[:, 16:24], nsm[:, 16:24], Alu.mult),
                              reads=[nsm_b], writes=[nsm_b])
                        bld.v(lambda e: e.tensor_tensor(nsm[:, 24:32], nsm[:, 24:32], nsm[:, 32:40], Alu.subtract),
                              reads=[nsm_b], writes=[nsm_b])
                        bld.v(lambda e: e.tensor_tensor(nsm[:, 32:40], nsm[:, 8:16], nsm[:, 8:16], Alu.mult),
                              reads=[nsm_b], writes=[nsm_b])
                        bld.v(lambda e: e.tensor_tensor(nsm[:, 24:32], nsm[:, 24:32], nsm[:, 32:40], Alu.mult),
                              reads=[nsm_b], writes=[nsm_b])
                        bld.s(lambda e: e.activation(nsm[:, 32:40], nsm[:, 24:32], Act.Sqrt, bias=cst[:, 1:2]),
                              reads=[nsm_b, cst_b], writes=[nsm_b])
                        bld.v(lambda e: e.reciprocal(nsm[:, 40:48], nsm[:, 32:40]), reads=[nsm_b], writes=[nsm_b])
                        bld.v(lambda e: e.tensor_tensor(nsm[:, 40:48], nsm[:, 40:48], nsm[:, 8:16], Alu.mult),
                              reads=[nsm_b], writes=[nsm_b])
                        mu3 = nsm[:, 16:24].rearrange("p (h o) -> p h o", o=1)
                        bld.v(lambda e, av=av: e.tensor_tensor(hub[:], av, bcast(av, mu3), Alu.subtract),
                              reads=[acat_b, nsm_b], writes=[hub_b])
                        w3 = wsr[:].rearrange("p (h v) -> p h v", v=128)
                        rr3 = nsm[:, 40:48].rearrange("p (h o) -> p h o", o=1)
                        bld.g(lambda e, i=i: e.tensor_tensor(wsr[:], onw[:], so[:, i, :], Alu.mult),
                              reads=[onw_b, so_b], writes=[wsr_b])
                        bld.g(lambda e: e.tensor_tensor(w3, w3, bcast(w3, rr3), Alu.mult),
                              reads=[wsr_b, nsm_b], writes=[wsr_b])
                        hgi, hgi_b = hg[i % 2], hg_b[i % 2]
                        bld.v(lambda e, o=hgi: e.tensor_tensor(o[:].rearrange("p (h v) -> p h v", v=128), hub[:], w3, Alu.mult),
                              reads=[hub_b, wsr_b], writes=[hgi_b])
                        hgTi, hgTi_b = hgT[i % 2], hgT_b[i % 2]
                        bld.dma("qS" if i % 2 == 0 else "qA",
                                lambda e, a=hgi, b=hgTi: e.dma_start(
                                    b[:], a[:].rearrange("p (j q) -> p j q", q=128), transpose=True),
                                reads=[hgi_b], writes=[hgTi_b])
                        for j in range(DC):
                            for half in range(2):
                                po, po_b = ps_o[half], ps_o_b[half]
                                bld.t(lambda e, j=j, half=half, a=hgTi, po=po: e.matmul(
                                        po[:], a[:, j, :], wdns[:, j, half * 512:(half + 1) * 512],
                                        start=(j == 0), stop=(j == DC - 1)),
                                      reads=[hgTi_b, wdns_b], writes=[po_b], track=(j == DC - 1))
                        x2i, x2i_b = x2t[i % 2], x2t_b[i % 2]
                        for half in range(2):
                            po, po_b = ps_o[half], ps_o_b[half]
                            bld.v(lambda e, half=half, x=x1t[i % 2], o=x2i, po=po: e.tensor_tensor(
                                    o[:, half * 512:(half + 1) * 512], po[:], x[:, half * 512:(half + 1) * 512], Alu.add),
                                  reads=[po_b, x1t_b[i % 2]], writes=[x2i_b])
                        bld.dma("qS", lambda e, i=i, o=x2i: e.dma_start(xs2.ap()[i * TT:(i + 1) * TT, :], o[:]),
                                reads=[x2i_b])
                bld.flush_phase("a3")

        with contextlib.ExitStack() as w2stack:
            wgqT2 = sb("wgqT2", [128, DC, DI], BF16, w2stack)
            wuqT2 = sb("wuqT2", [128, DC, DI], BF16, w2stack)
            wdqT2 = sb("wdqT2", [128, IC, D], BF16, w2stack)
            wgqT2_b, wuqT2_b, wdqT2_b = Buf(), Buf(), Buf()
            bld.dma("qS", lambda e: e.dma_start(wgqT2[:], wgc.ap().rearrange("(j p) i -> p j i", p=128)), writes=[wgqT2_b])
            bld.dma("qA", lambda e: e.dma_start(wuqT2[:], wuc.ap().rearrange("(j p) i -> p j i", p=128)), writes=[wuqT2_b])
            bld.dma("qS", lambda e: e.dma_start(wdqT2[:], wdc.ap().rearrange("(j p) i -> p j i", p=128)), writes=[wdqT2_b])
            with contextlib.ExitStack() as fstack2:
                ffn_phase("f2", xs2, out_d, wgqT2, wuqT2, wdqT2, (wgqT2_b, wuqT2_b, wdqT2_b), fstack2)

    return nc


def kernel(L0_attn_x, L0_attn_Wq, L0_attn_Wk, L0_attn_Wv, L0_attn_Wo, L0_attn_Wi,
           L0_attn_bi, L0_attn_Wf, L0_attn_bf, L0_attn_outnorm_w, L0_attn_Wdown,
           L0_ffn_w_gate, L0_ffn_w_up, L0_ffn_w_down):
    f32 = lambda a: np.ascontiguousarray(np.asarray(a), dtype=np.float32)
    x = f32(L0_attn_x)
    wif = np.ascontiguousarray(np.concatenate([f32(L0_attn_Wi), f32(L0_attn_Wf)], axis=1))
    bif = np.ascontiguousarray(np.concatenate([f32(L0_attn_bi), f32(L0_attn_bf)]).reshape(16, 1))
    onw = np.ascontiguousarray(np.broadcast_to(f32(L0_attn_outnorm_w).reshape(1, D), (128, D)))
    base = {
        "wq": f32(L0_attn_Wq), "wk": f32(L0_attn_Wk), "wv": f32(L0_attn_Wv),
        "wo": f32(L0_attn_Wo), "wif": wif, "bif": bif, "onw": onw,
        "wdn": f32(L0_attn_Wdown), "wg": f32(L0_ffn_w_gate), "wu": f32(L0_ffn_w_up),
        "wd": f32(L0_ffn_w_down),
    }
    in_maps = [dict(base, x=np.ascontiguousarray(x[c])) for c in range(8)]
    nc = build_core_program()
    res = run_bass_kernel_spmd(nc, in_maps, core_ids=list(range(8)))
    return np.stack([np.asarray(r["out"], dtype=np.float32) for r in res.results], axis=0)


# version 17
# speedup vs baseline: 1.1362x; 1.0147x over previous
import contextlib
import numpy as np

import concourse.bass as bass
import concourse.mybir as mybir
from concourse.bass_utils import run_bass_kernel_spmd

F32 = mybir.dt.float32
BF16 = mybir.dt.bfloat16
FP16 = mybir.dt.float16
Alu = mybir.AluOpType
Act = mybir.ActivationFunctionType
Ax = mybir.AxisListType

ENGINES = ("tensor", "vector", "scalar", "gpsimd", "sync")
QUEUES = {"qS": "sync", "qA": "scalar", "qG": "gpsimd"}
DMA_POOL = 10

S, D, DI, H = 2048, 1024, 2816, 8
TT = 128
NTT = S // TT
DC = D // 128
IC = DI // 128
DHQK, DHV = 64, 128
LN_EPS = 1e-5
ATT_EPS = 1e-6
CAP = 15.0
LN8 = float(np.log(8.0))


class Buf:
    __slots__ = ("w", "rs", "name")

    def __init__(self, name=""):
        self.w = None
        self.rs = []
        self.name = name


class B:

    def __init__(self, nc, stack):
        self.nc = nc
        self.stack = stack
        self.sems = {}
        self.counts = {}
        self.dma_sems = {}
        self.dma_rr = {}
        for name in ("T", "V", "Sx", "G"):
            self._mk(name)
        for q in QUEUES:
            self.dma_sems[q] = []
            self.dma_rr[q] = 0
            for k in range(DMA_POOL):
                n = f"{q}_{k}"
                self._mk(n)
                self.dma_sems[q].append(n)
        self.engine_sem = {"tensor": "T", "vector": "V", "scalar": "Sx", "gpsimd": "G"}
        self.streams = {e: [] for e in ENGINES}
        self.is_dma_sem = {s for q in QUEUES for s in self.dma_sems[q]}

    def _mk(self, name):
        self.sems[name] = self.stack.enter_context(self.nc.semaphore(name))
        self.counts[name] = 0

    def _fix(self, tok):
        s, v = tok
        if s in self.is_dma_sem:
            return (s, self.counts[s])
        return tok

    def op(self, eng, fn, reads=(), writes=(), extra=(), sem=None, amt=1, track=True):
        waits = []
        for b in reads:
            if b.w is not None:
                waits.append(self._fix(b.w))
        for b in writes:
            for t_ in b.rs:
                waits.append(self._fix(t_))
            if b.w is not None:
                waits.append(self._fix(b.w))
        waits.extend(self._fix(t_) for t_ in extra)
        tok = None
        if track:
            s = sem if sem is not None else self.engine_sem[eng]
            self.counts[s] += amt
            tok = (s, self.counts[s])
        self.streams[eng].append((fn, tuple(waits), (tok[0], amt) if tok else None))
        if tok:
            for b in reads:
                b.rs.append(tok)
            for b in writes:
                b.w = tok
                b.rs = []
        return tok

    def t(self, fn, reads=(), writes=(), extra=(), track=True):
        return self.op("tensor", fn, reads, writes, extra, track=track)

    def v(self, fn, reads=(), writes=(), extra=()):
        return self.op("vector", fn, reads, writes, extra)

    def s(self, fn, reads=(), writes=(), extra=()):
        return self.op("scalar", fn, reads, writes, extra)

    def g(self, fn, reads=(), writes=(), extra=()):
        return self.op("gpsimd", fn, reads, writes, extra)

    def vg(self, k, fn, reads=(), writes=(), extra=()):
        return self.v(fn, reads, writes, extra)

    def dma(self, queue, fn, reads=(), writes=(), extra=()):
        eng = QUEUES[queue]
        idx = self.dma_rr[queue]
        self.dma_rr[queue] = (idx + 1) % DMA_POOL
        sem = self.dma_sems[queue][idx]
        return self.op(eng, fn, reads, writes, extra, sem=sem, amt=16)

    def flush_phase(self, name=None):
        for q, eng in QUEUES.items():
            toks = [(s, self.counts[s]) for s in self.dma_sems[q] if self.counts[s] > 0]
            if toks:
                self.streams[eng].append((None, tuple(toks), None))
        streams = {e: list(s) for e, s in self.streams.items()}
        self.streams = {e: [] for e in ENGINES}
        nonempty = [e for e in ENGINES if streams[e]]
        if not nonempty:
            return
        with self.nc.Block(name) as block:
            for eng in nonempty:
                def emitter(e, _eng=eng):
                    lw = {}
                    for fn, waits, inc in streams[_eng]:
                        for (sname, v_) in waits:
                            if lw.get(sname, 0) < v_:
                                e.wait_ge(self.sems[sname], v_)
                                lw[sname] = v_
                        if fn is None:
                            continue
                        inst = fn(e)
                        if inc is not None:
                            inst.then_inc(self.sems[inc[0]], inc[1])
                getattr(block, eng)(emitter)


def bcast(full_view, small_view):
    _, b2 = bass.broadcast_tensor_aps(full_view, small_view)
    return b2


def emit_quantize(bld, nc, x_ap, x_buf, out_ap, out_buf, scr, nblk, fin_v=False):
    bm, bm_b = scr["bm"], scr["bm_b"]
    rs, rs_b = scr["rs"], scr["rs_b"]
    xn, xn_b = scr["xn"], scr["xn_b"]
    axn, axn_b = scr["axn"], scr["axn_b"]
    nA, nA_b = scr["nA"], scr["nA_b"]
    nB, nB_b = scr["nB"], scr["nB_b"]
    n = nblk * 16
    xreads = x_buf if isinstance(x_buf, list) else [x_buf]
    x3 = x_ap.rearrange("p (b s) -> p b s", s=16)
    bm3 = bm[:, 0:nblk].rearrange("p (b o) -> p b o", o=1)
    bld.v(lambda e: e.tensor_reduce(bm3, x3, Ax.X, Alu.max, apply_absolute_value=True),
          reads=xreads, writes=[bm_b])
    bld.v(lambda e: e.tensor_scalar(bm[:, 0:nblk], bm[:, 0:nblk], 1.0 / 6.0, 1e-12, Alu.mult, Alu.max),
          reads=[bm_b], writes=[bm_b])
    bld.v(lambda e: e.reciprocal(rs[:, 0:nblk], bm[:, 0:nblk]), reads=[bm_b], writes=[rs_b])
    xn3 = xn[:, 0:n].rearrange("p (b s) -> p b s", s=16)
    rs3 = rs[:, 0:nblk].rearrange("p (b o) -> p b o", o=1)
    bld.v(lambda e: e.tensor_tensor(xn3, x3, bcast(x3, rs3), Alu.mult),
          reads=xreads + [rs_b], writes=[xn_b])
    bld.s(lambda e: e.activation(axn[:, 0:n], xn[:, 0:n], Act.Square),
          reads=[xn_b], writes=[axn_b])
    bld.s(lambda e: e.sign(xn[:, 0:n], xn[:, 0:n]), reads=[xn_b], writes=[xn_b])
    bld.vg(0, lambda e: e.tensor_single_scalar(nA[:, 0:n], axn[:, 0:n], 0.0625, Alu.is_gt),
           reads=[axn_b], writes=[nA_b])
    for k, c in enumerate((0.5625, 1.5625, 3.0625)):
        bld.vg(k + 1, lambda e, c=c: e.scalar_tensor_tensor(nA[:, 0:n], axn[:, 0:n], c, nA[:, 0:n], Alu.is_gt, Alu.add),
               reads=[axn_b, nA_b], writes=[nA_b])
    bld.vg(0, lambda e: e.tensor_single_scalar(nB[:, 0:n], axn[:, 0:n], 6.25, Alu.is_gt),
           reads=[axn_b], writes=[nB_b])
    for k, c in enumerate((12.25, 25.0, 25.0)):
        bld.vg(k + 1, lambda e, c=c: e.scalar_tensor_tensor(nB[:, 0:n], axn[:, 0:n], c, nB[:, 0:n], Alu.is_gt, Alu.add),
               reads=[axn_b, nB_b], writes=[nB_b])
    bld.vg(0, lambda e: e.scalar_tensor_tensor(nA[:, 0:n], nA[:, 0:n], 0.5, nB[:, 0:n], Alu.mult, Alu.add),
           reads=[nA_b, nB_b], writes=[nA_b])
    fin = bld.v if fin_v else bld.g
    fin(lambda e: e.tensor_tensor(nA[:, 0:n], nA[:, 0:n], xn[:, 0:n], Alu.mult),
        reads=[nA_b, xn_b], writes=[nA_b])
    o3 = out_ap.rearrange("p (b s) -> p b s", s=16)
    nA3 = nA[:, 0:n].rearrange("p (b s) -> p b s", s=16)
    fin(lambda e: e.tensor_tensor(o3, nA3, bcast(nA3, bm3), Alu.mult),
        reads=[nA_b, bm_b], writes=[out_buf])


def build_core_program():
    nc = bass.Bass("TRN2", target_bir_lowering=False)
    x_in = nc.dram_tensor("x", [S, D], F32, kind="ExternalInput")
    wq_d = nc.dram_tensor("wq", [D, 512], F32, kind="ExternalInput")
    wk_d = nc.dram_tensor("wk", [D, 512], F32, kind="ExternalInput")
    wv_d = nc.dram_tensor("wv", [D, D], F32, kind="ExternalInput")
    wo_d = nc.dram_tensor("wo", [D, D], F32, kind="ExternalInput")
    wif_d = nc.dram_tensor("wif", [D, 16], F32, kind="ExternalInput")
    bif_d = nc.dram_tensor("bif", [16, 1], F32, kind="ExternalInput")
    onw_d = nc.dram_tensor("onw", [128, D], F32, kind="ExternalInput")
    wdn_d = nc.dram_tensor("wdn", [D, D], F32, kind="ExternalInput")
    wg_d = nc.dram_tensor("wg", [DI, D], F32, kind="ExternalInput")
    wu_d = nc.dram_tensor("wu", [DI, D], F32, kind="ExternalInput")
    wd_d = nc.dram_tensor("wd", [D, DI], F32, kind="ExternalInput")
    out_d = nc.dram_tensor("out", [S, D], F32, kind="ExternalOutput")
    xs1 = nc.dram_tensor("xs1", [S, D], F32)
    xs2 = nc.dram_tensor("xs2", [S, D], F32)
    wgc = nc.dram_tensor("wgc", [D, DI], BF16)
    wuc = nc.dram_tensor("wuc", [D, DI], BF16)
    wdc = nc.dram_tensor("wdc", [DI, D], BF16)

    with contextlib.ExitStack() as gstack:
        bld = B(nc, gstack)
        sb = lambda name, shape, dt, st=gstack, **kw: st.enter_context(nc.sbuf_tensor(name, shape, dt, **kw))
        psum = lambda name, shape, dt, st=gstack: st.enter_context(nc.psum_tensor(name, shape, dt))

        cst = sb("cst", [128, 2], F32)
        cst_b = Buf("cst")
        bld.g(lambda e: e.memset(cst[:, 0:1], LN_EPS), writes=[cst_b])
        bld.g(lambda e: e.memset(cst[:, 1:2], ATT_EPS), writes=[cst_b])

        def ffn_phase(tag, x_src, x_dst, wgqT, wuqT, wdqT, w_bufs, stack):
            wg_b, wu_b, wd_b = w_bufs
            xt = [sb(f"{tag}xt{i}", [TT, D], F32, stack) for i in range(3)]
            xt_b = [Buf() for _ in range(3)]
            xln = [sb(f"{tag}xln{i}", [TT, D], F32, stack) for i in range(2)]
            xln_b = [Buf() for _ in range(2)]
            xq = [sb(f"{tag}xq{i}", [TT, D], BF16, stack) for i in range(2)]
            xq_b = [Buf() for _ in range(2)]
            xqT = [sb(f"{tag}xqT{i}", [128, DC, TT], BF16, stack, is_transpose_dest=True) for i in range(2)]
            xqT_hb = [[Buf(), Buf()] for _ in range(2)]
            hq = [sb(f"{tag}hq{i}", [TT, DI], BF16, stack) for i in range(2)]
            hq_b = [Buf() for _ in range(2)]
            hqT = [sb(f"{tag}hqT{i}", [128, IC, TT], BF16, stack, is_transpose_dest=True) for i in range(2)]
            hqT_b = [Buf() for _ in range(2)]
            hft = sb(f"{tag}hf", [TT, 1024], F32, stack)
            hf_b = [Buf() for _ in range(2)]
            st_t = sb(f"{tag}st", [TT, 16], F32, stack)
            st_b = Buf()
            scr2 = []
            for si, wd_ in ((0, 1024), (1, 512)):
                scr2.append({
                    "bm": sb(f"{tag}bm{si}", [TT, wd_ // 16], F32, stack), "bm_b": Buf(),
                    "rs": sb(f"{tag}rs{si}", [TT, wd_ // 16], F32, stack), "rs_b": Buf(),
                    "xn": sb(f"{tag}xn{si}", [TT, wd_], F32, stack), "xn_b": Buf(),
                    "axn": sb(f"{tag}axn{si}", [TT, wd_], F32, stack), "axn_b": Buf(),
                    "nA": sb(f"{tag}nA{si}", [TT, wd_], BF16, stack), "nA_b": Buf(),
                    "nB": sb(f"{tag}nB{si}", [TT, wd_], BF16, stack), "nB_b": Buf(),
                })
            ps_g = [psum(f"{tag}psg{k}", [TT, 512], F32, stack) for k in range(3)]
            ps_g_b = [Buf() for _ in range(3)]
            ps_u = [psum(f"{tag}psu{k}", [TT, 512], F32, stack) for k in range(3)]
            ps_u_b = [Buf() for _ in range(3)]
            ps_d = [psum(f"{tag}psd{i}", [TT, 512], F32, stack) for i in range(2)]
            ps_d_b = [Buf() for _ in range(2)]

            ichunks = [(0, 512), (512, 512), (1024, 512), (1536, 512), (2048, 512), (2560, 256)]
            pend = []

            def emit_down(item):
                dti, dhqT, dhqT_b, dxt, dxt_b, dxn, dxn_b = item
                for j in range(IC):
                    for half in range(2):
                        pd, pd_b = ps_d[half], ps_d_b[half]
                        bld.t(lambda e, j=j, half=half, a=dhqT, w=wdqT, pd=pd: e.matmul(
                                pd[:], a[:, j, :], w[:, j, half * 512:(half + 1) * 512],
                                start=(j == 0), stop=(j == IC - 1)),
                              reads=[dhqT_b, wd_b], writes=[pd_b], track=(j == IC - 1))
                for half in range(2):
                    pd, pd_b = ps_d[half], ps_d_b[half]
                    bld.v(lambda e, half=half, x=dxt, o=dxn, pd=pd: e.scalar_tensor_tensor(
                            o[:, half * 512:(half + 1) * 512], pd[:], 0.5, x[:, half * 512:(half + 1) * 512],
                            Alu.mult, Alu.add),
                          reads=[pd_b, dxt_b], writes=[dxn_b])
                bld.dma("qS", lambda e, dti=dti, o=dxn: e.dma_start(x_dst.ap()[dti * TT:(dti + 1) * TT, :], o[:]),
                        reads=[dxn_b])

            def xload(ti):
                xti, xti_b = xt[ti % 3], xt_b[ti % 3]
                bld.dma("qG", lambda e, ti=ti, x=xti: e.dma_start(x[:], x_src.ap()[ti * TT:(ti + 1) * TT, :]),
                        writes=[xti_b])

            def xside(ti):
                xti, xti_b = xt[ti % 3], xt_b[ti % 3]
                xlni, xlni_b = xln[ti % 2], xln_b[ti % 2]
                xqi, xqi_b = xq[ti % 2], xq_b[ti % 2]
                xqTi, xqTi_hb = xqT[ti % 2], xqT_hb[ti % 2]
                bld.v(lambda e, x=xti, s=st_t: e.bn_stats(s[:, 0:6], x[:, 0:512]),
                      reads=[xti_b], writes=[st_b])
                bld.v(lambda e, x=xti, s=st_t: e.bn_stats(s[:, 6:12], x[:, 512:1024]),
                      reads=[xti_b], writes=[st_b])
                bld.v(lambda e, s=st_t: e.bn_aggr(s[:, 12:14], s[:, 0:12]),
                      reads=[st_b], writes=[st_b])
                bld.s(lambda e, s=st_t: e.activation(s[:, 14:15], s[:, 13:14], Act.Sqrt, bias=cst[:, 0:1]),
                      reads=[st_b, cst_b], writes=[st_b])
                bld.v(lambda e, s=st_t: e.reciprocal(s[:, 15:16], s[:, 14:15]),
                      reads=[st_b], writes=[st_b])
                bld.v(lambda e, x=xti, o=xlni, s=st_t: e.tensor_scalar(o[:], x[:], s[:, 12:13], s[:, 15:16], Alu.subtract, Alu.mult),
                      reads=[xti_b, st_b], writes=[xlni_b])
                for hh in range(2):
                    emit_quantize(bld, nc, xlni[:, hh * 512:(hh + 1) * 512], xlni_b,
                                  xqi[:, hh * 512:(hh + 1) * 512], xqi_b, scr2[1], 32)
                    bld.dma("qS" if (ti + hh) % 2 == 0 else "qA",
                            lambda e, hh=hh, a=xqi, b=xqTi: e.dma_start(
                                b[:, hh * 4:(hh + 1) * 4, :],
                                a[:, hh * 512:(hh + 1) * 512].rearrange("p (j q) -> p j q", q=128), transpose=True),
                            reads=[xqi_b], writes=[xqT_hb[ti % 2][hh]])

            xload(0)
            xload(1)
            xside(0)
            for ti in range(NTT):
                xti, xti_b = xt[ti % 3], xt_b[ti % 3]
                xlni, xlni_b = xln[ti % 2], xln_b[ti % 2]
                xqTi, xqTi_hb = xqT[ti % 2], xqT_hb[ti % 2]
                hqi, hqi_b = hq[ti % 2], hq_b[ti % 2]
                hqTi, hqTi_b = hqT[ti % 2], hqT_b[ti % 2]
                if ti + 1 < NTT:
                    xside(ti + 1)
                for grp in range(2):
                    cis = [grp * 3, grp * 3 + 1, grp * 3 + 2]
                    for j in range(DC):
                        for k, ci in enumerate(cis):
                            i0, iw = ichunks[ci]
                            bld.t(lambda e, j=j, i0=i0, iw=iw, k=k, a=xqTi, w=wgqT: e.matmul(
                                    ps_g[k][:, 0:iw], a[:, j, :], w[:, j, i0:i0 + iw],
                                    start=(j == 0), stop=(j == DC - 1)),
                                  reads=[xqTi_hb[j // 4], wg_b], writes=[ps_g_b[k]], track=(j == DC - 1))
                            bld.t(lambda e, j=j, i0=i0, iw=iw, k=k, a=xqTi, w=wuqT: e.matmul(
                                    ps_u[k][:, 0:iw], a[:, j, :], w[:, j, i0:i0 + iw],
                                    start=(j == 0), stop=(j == DC - 1)),
                                  reads=[xqT_hb[ti % 2][j // 4], wu_b], writes=[ps_u_b[k]], track=(j == DC - 1))
                    pieces = []
                    for k, ci in enumerate(cis):
                        i0, iw = ichunks[ci]
                        off = (ci % 2) * 512
                        hfc_b = hf_b[ci % 2]
                        bld.s(lambda e, iw=iw, k=k, off=off: e.activation(hft[:, off:off + iw], ps_g[k][:, 0:iw], Act.Silu),
                              reads=[ps_g_b[k]], writes=[hfc_b])
                        bld.v(lambda e, iw=iw, k=k, off=off: e.tensor_tensor(hft[:, off:off + iw], hft[:, off:off + iw], ps_u[k][:, 0:iw], Alu.mult),
                              reads=[hfc_b, ps_u_b[k]], writes=[hfc_b])
                        pieces.append((ci, i0, iw, off))
                        if len(pieces) == 2 or ci == cis[-1]:
                            if len(pieces) == 2 and pieces[0][3] == 0:
                                (ciA, i0A, iwA, _), (_, _, iwB, _) = pieces
                                emit_quantize(bld, nc, hft[:, 0:iwA + iwB], [hf_b[0], hf_b[1]],
                                              hqi[:, i0A:i0A + iwA + iwB], hqi_b, scr2[0], (iwA + iwB) // 16)
                                pieces = []
                            elif len(pieces) == 2:
                                (ciA, i0A, iwA, offA) = pieces.pop(0)
                                emit_quantize(bld, nc, hft[:, offA:offA + iwA], hf_b[ciA % 2],
                                              hqi[:, i0A:i0A + iwA], hqi_b, scr2[0], iwA // 16)
                            if pieces and ci == cis[-1]:
                                (ciA, i0A, iwA, offA) = pieces.pop(0)
                                emit_quantize(bld, nc, hft[:, offA:offA + iwA], hf_b[ciA % 2],
                                              hqi[:, i0A:i0A + iwA], hqi_b, scr2[0], iwA // 16)
                    if grp == 0 and pend:
                        emit_down(pend.pop(0))
                bld.dma("qA" if ti % 2 == 0 else "qS",
                        lambda e, a=hqi, b=hqTi: e.dma_start(
                            b[:], a[:].rearrange("p (j q) -> p j q", q=128), transpose=True),
                        reads=[hqi_b], writes=[hqTi_b])
                pend.append((ti, hqTi, hqTi_b, xti, xti_b, xlni, xlni_b))
                if len(pend) > 1:
                    emit_down(pend.pop(0))
                if ti + 2 < NTT:
                    xload(ti + 2)
            while pend:
                emit_down(pend.pop(0))
            bld.flush_phase(tag)

        with contextlib.ExitStack() as wstack:
            wgqT = sb("wgqT", [128, DC, DI], BF16, wstack, is_transpose_dest=True)
            wuqT = sb("wuqT", [128, DC, DI], BF16, wstack, is_transpose_dest=True)
            wdqT = sb("wdqT", [128, IC, D], BF16, wstack, is_transpose_dest=True)
            wgqT_b, wuqT_b, wdqT_b = Buf("wgqT"), Buf("wuqT"), Buf("wdqT")

            wsstack = wstack.enter_context(contextlib.ExitStack())
            wchunk = [sb("wch0", [128, DI], F32, wsstack), sb("wch1", [128, D], F32, wsstack)]
            wchunk_b = [Buf() for _ in range(2)]
            wqb = [sb("wqb0", [128, DI], BF16, wsstack), sb("wqb1", [128, D], BF16, wsstack)]
            wqb_b = [Buf() for _ in range(2)]
            wscr2 = []
            for si, w_ in ((0, DI), (1, D)):
                wscr2.append({
                    "bm": sb(f"wbm{si}", [128, w_ // 16], F32, wsstack), "bm_b": Buf(),
                    "rs": sb(f"wrs{si}", [128, w_ // 16], F32, wsstack), "rs_b": Buf(),
                    "xn": sb(f"wxn{si}", [128, w_], F32, wsstack), "xn_b": Buf(),
                    "axn": sb(f"waxn{si}", [128, w_], F32, wsstack), "axn_b": Buf(),
                    "nA": sb(f"wnA{si}", [128, w_], BF16, wsstack), "nA_b": Buf(),
                    "nB": sb(f"wnB{si}", [128, w_], BF16, wsstack), "nB_b": Buf(),
                })
            for wi, (wsrc, wdstT, wdstT_b) in enumerate(((wg_d, wgqT, wgqT_b), (wu_d, wuqT, wuqT_b))):
                for c in range(IC):
                    wc, wc_b = wchunk[c % 2], wchunk_b[c % 2]
                    wq_, wq_b_ = wqb[c % 2], wqb_b[c % 2]
                    bld.dma("qS", lambda e, c=c, wsrc=wsrc, wc=wc: e.dma_start(
                            wc[:, 0:D], wsrc.ap()[c * 128:(c + 1) * 128, :]), writes=[wc_b])
                    emit_quantize(bld, nc, wc[:, 0:D], wc_b, wq_[:, 0:D], wq_b_, wscr2[c % 2], D // 16, fin_v=True)
                    bld.dma("qS" if c % 2 == 0 else "qA",
                            lambda e, c=c, a=wq_, b=wdstT: e.dma_start(
                                b[:, :, c * 128:(c + 1) * 128], a[:, 0:D].rearrange("p (j q) -> p j q", q=128), transpose=True),
                            reads=[wq_b_], writes=[wdstT_b])
            for c in range(DC):
                wc, wc_b = wchunk[0], wchunk_b[0]
                wq_, wq_b_ = wqb[0], wqb_b[0]
                bld.dma("qS", lambda e, c=c, wc=wc: e.dma_start(
                        wc[:], wd_d.ap()[c * 128:(c + 1) * 128, :]), writes=[wc_b])
                emit_quantize(bld, nc, wc[:], wc_b, wq_[:], wq_b_, wscr2[0], DI // 16, fin_v=True)
                bld.dma("qS" if c % 2 == 0 else "qA",
                        lambda e, c=c, a=wq_: e.dma_start(
                            wdqT[:, :, c * 128:(c + 1) * 128], a[:].rearrange("p (j q) -> p j q", q=128), transpose=True),
                        reads=[wq_b_], writes=[wdqT_b])
            bld.dma("qS", lambda e: e.dma_start(wgc.ap().rearrange("(j p) i -> p j i", p=128), wgqT[:]),
                    reads=[wgqT_b])
            bld.dma("qS", lambda e: e.dma_start(wuc.ap().rearrange("(j p) i -> p j i", p=128), wuqT[:]),
                    reads=[wuqT_b])
            bld.dma("qS", lambda e: e.dma_start(wdc.ap().rearrange("(j p) i -> p j i", p=128), wdqT[:]),
                    reads=[wdqT_b])
            bld.flush_phase("wprep")
            wsstack.close()

            with contextlib.ExitStack() as fstack:
                ffn_phase("f1", x_in, xs1, wgqT, wuqT, wdqT, (wgqT_b, wuqT_b, wdqT_b), fstack)

        with contextlib.ExitStack() as astack:
            idf = sb("idf", [128, 128], F32, astack)
            idf_b = Buf()
            maskt = sb("maskt", [TT, TT], BF16, astack)
            maskt_b = Buf()
            bld.g(lambda e: e.memset(idf[:], 0.0), writes=[idf_b])
            bld.g(lambda e: e.affine_select(out=idf[:], in_=idf[:], compare_op=Alu.not_equal,
                                            fill=1.0, base=0, pattern=[[-1, 128]], channel_multiplier=1),
                  reads=[idf_b], writes=[idf_b])
            bld.g(lambda e: e.memset(maskt[:], 1.0), writes=[maskt_b])
            bld.g(lambda e: e.affine_select(out=maskt[:], in_=maskt[:], compare_op=Alu.is_ge,
                                            fill=0.0, base=0, pattern=[[1, TT]], channel_multiplier=-1),
                  reads=[maskt_b], writes=[maskt_b])

            QT = sb("QT", [128, 4, S], BF16, astack)
            QT_b = Buf("QT")
            KT = sb("KT", [128, 4, S], BF16, astack)
            KT_b = Buf("KT")
            VH = sb("VH", [TT, NTT, H, 132], BF16, astack)
            VH_b = Buf("VH")
            so = sb("so", [TT, NTT, D], BF16, astack)
            so_b = Buf("so")
            onw = sb("onw_sb", [128, D], F32, astack)
            onw_b = Buf()
            ecol_t = sb("ecol_t", [128, NTT, 128], F32, astack)
            ecol_t_b = Buf()
            erow_t = sb("erow_t", [128, 128], F32, astack)
            erow_t_b = Buf()
            emb_t = sb("emb_t", [128, 128], F32, astack)
            emb_t_b = Buf()
            bld.dma("qS", lambda e: e.dma_start(onw[:], onw_d.ap()), writes=[onw_b])
            bld.g(lambda e: e.memset(VH[:, :, :, 128:129], 1.0), writes=[VH_b])
            with contextlib.ExitStack() as a12stack:
                ifsb = sb("ifsb", [TT, NTT, 16], F32, a12stack)
                ifsb_b = Buf()
                ir_row = sb("ir_row", [8, S], F32, a12stack)
                ir_b = Buf()
                fr_row = sb("fr_row", [8, S], F32, a12stack)
                fr_b = Buf()
                bi_s = sb("bi_s", [8, 2], F32, a12stack)
                bi_b = Buf()
                bf_s = sb("bf_s", [8, 2], F32, a12stack)
                bf_b = Buf()
                bld.dma("qS", lambda e: e.dma_start(bi_s[:, 0:1], bif_d.ap()[0:8, :]), writes=[bi_b])
                bld.dma("qS", lambda e: e.dma_start(bf_s[:, 0:1], bif_d.ap()[8:16, :]), writes=[bf_b])
                bld.v(lambda e: e.tensor_single_scalar(bi_s[:, 1:2], bi_s[:, 0:1], 1.0 / CAP, Alu.mult),
                      reads=[bi_b], writes=[bi_b])
                bld.v(lambda e: e.tensor_single_scalar(bf_s[:, 1:2], bf_s[:, 0:1], 1.0 / CAP, Alu.mult),
                      reads=[bf_b], writes=[bf_b])

                with contextlib.ExitStack() as pstack:
                    wqs = sb("wqs", [128, DC, 512], BF16, pstack)
                    wqs_b = Buf()
                    wks = sb("wks", [128, DC, 512], BF16, pstack)
                    wks_b = Buf()
                    wvs = sb("wvs", [128, DC, D], BF16, pstack)
                    wvs_b = Buf()
                    wos = sb("wos", [128, DC, D], BF16, pstack)
                    wos_b = Buf()
                    wifs = sb("wifs", [128, DC, 16], BF16, pstack)
                    wifs_b = Buf()
                    xT2t = [sb(f"xT2t{i}", [128, DC, TT], BF16, pstack, is_transpose_dest=True) for i in range(2)]
                    xT2t_b = [Buf() for _ in range(2)]
                    xt = [sb(f"a1xt{i}", [TT, D], F32, pstack) for i in range(2)]
                    xt_b = [Buf() for _ in range(2)]
                    xln = [sb(f"a1xln{i}", [TT, D], BF16, pstack) for i in range(2)]
                    xln_b = [Buf() for _ in range(2)]
                    st_t = sb("a1st", [TT, 16], F32, pstack)
                    st_b = Buf()
                    ps_p = [psum(f"a1ps{i}", [128, 512], F32, pstack) for i in range(2)]
                    ps_p_b = [Buf() for _ in range(2)]
                    ps_v = [psum(f"a1psv{i}", [128, 512], F32, pstack) for i in range(2)]
                    ps_v_b = [Buf() for _ in range(2)]
                    ps_if = psum("a1psif", [128, 16], F32, pstack)
                    ps_if_b = Buf()
                    ps_t = psum("a1pst", [16, 128], F32, pstack)
                    ps_t_b = Buf()
                    bld.dma("qG", lambda e: e.dma_start(wqs[:], wq_d.ap().rearrange("(j p) q -> p j q", p=128)), writes=[wqs_b])
                    bld.dma("qG", lambda e: e.dma_start(wks[:], wk_d.ap().rearrange("(j p) q -> p j q", p=128)), writes=[wks_b])
                    bld.dma("qG", lambda e: e.dma_start(wvs[:], wv_d.ap().rearrange("(j p) q -> p j q", p=128)), writes=[wvs_b])
                    bld.dma("qG", lambda e: e.dma_start(wos[:], wo_d.ap().rearrange("(j p) q -> p j q", p=128)), writes=[wos_b])
                    bld.dma("qG", lambda e: e.dma_start(wifs[:], wif_d.ap().rearrange("(j p) q -> p j q", p=128)), writes=[wifs_b])
                    def a1_xside(ti):
                        xti, xti_b = xt[ti % 2], xt_b[ti % 2]
                        xlni, xlni_b = xln[ti % 2], xln_b[ti % 2]
                        xTt, xTt_b = xT2t[ti % 2], xT2t_b[ti % 2]
                        bld.dma("qG", lambda e, ti=ti, x=xti: e.dma_start(x[:], xs1.ap()[ti * TT:(ti + 1) * TT, :]),
                                writes=[xti_b])
                        bld.v(lambda e, x=xti, s=st_t: e.bn_stats(s[:, 0:6], x[:, 0:512]), reads=[xti_b], writes=[st_b])
                        bld.v(lambda e, x=xti, s=st_t: e.bn_stats(s[:, 6:12], x[:, 512:1024]), reads=[xti_b], writes=[st_b])
                        bld.v(lambda e, s=st_t: e.bn_aggr(s[:, 12:14], s[:, 0:12]), reads=[st_b], writes=[st_b])
                        bld.s(lambda e, s=st_t: e.activation(s[:, 14:15], s[:, 13:14], Act.Sqrt, bias=cst[:, 0:1]),
                              reads=[st_b, cst_b], writes=[st_b])
                        bld.v(lambda e, s=st_t: e.reciprocal(s[:, 15:16], s[:, 14:15]), reads=[st_b], writes=[st_b])
                        bld.v(lambda e, x=xti, o=xlni, s=st_t: e.tensor_scalar(o[:], x[:], s[:, 12:13], s[:, 15:16], Alu.subtract, Alu.mult),
                              reads=[xti_b, st_b], writes=[xlni_b])
                        bld.dma("qS" if ti % 2 == 0 else "qA",
                                lambda e, a=xlni, b=xTt: e.dma_start(
                                    b[:], a[:].rearrange("p (j q) -> p j q", q=128), transpose=True),
                                reads=[xlni_b], writes=[xTt_b])

                    a1_xside(0)
                    for ti in range(NTT):
                        xTt, xTt_b = xT2t[ti % 2], xT2t_b[ti % 2]
                        if ti + 1 < NTT:
                            a1_xside(ti + 1)
                        for gq in range(4):
                            pp, pp_b = ps_p[gq % 2], ps_p_b[gq % 2]
                            for j in range(DC):
                                bld.t(lambda e, j=j, gq=gq, pp=pp, a=xTt: e.matmul(
                                        pp[:, 0:TT], wqs[:, j, gq * 128:(gq + 1) * 128], a[:, j, :],
                                        start=(j == 0), stop=(j == DC - 1)),
                                      reads=[wqs_b, xTt_b], writes=[pp_b], track=(j == DC - 1))
                            bld.v(lambda e, gq=gq, ti=ti, pp=pp: e.tensor_copy(QT[:, gq, ti * TT:(ti + 1) * TT], pp[:, 0:TT]),
                                  reads=[pp_b], writes=[QT_b])
                            pp, pp_b = ps_p[(gq + 1) % 2], ps_p_b[(gq + 1) % 2]
                            for j in range(DC):
                                bld.t(lambda e, j=j, gq=gq, pp=pp, a=xTt: e.matmul(
                                        pp[:, 0:TT], wks[:, j, gq * 128:(gq + 1) * 128], a[:, j, :],
                                        start=(j == 0), stop=(j == DC - 1)),
                                      reads=[wks_b, xTt_b], writes=[pp_b], track=(j == DC - 1))
                            bld.v(lambda e, gq=gq, ti=ti, pp=pp: e.tensor_copy(KT[:, gq, ti * TT:(ti + 1) * TT], pp[:, 0:TT]),
                                  reads=[pp_b], writes=[KT_b])
                        for j in range(DC):
                            for half in range(2):
                                bld.t(lambda e, j=j, half=half, pp=ps_v[half], a=xTt: e.matmul(
                                        pp[:], a[:, j, :], wvs[:, j, half * 512:(half + 1) * 512],
                                        start=(j == 0), stop=(j == DC - 1)),
                                      reads=[wvs_b, xTt_b], writes=[ps_v_b[half]], track=(j == DC - 1))
                                bld.t(lambda e, j=j, half=half, pp=ps_p[half], a=xTt: e.matmul(
                                        pp[:], a[:, j, :], wos[:, j, half * 512:(half + 1) * 512],
                                        start=(j == 0), stop=(j == DC - 1)),
                                      reads=[wos_b, xTt_b], writes=[ps_p_b[half]], track=(j == DC - 1))
                            bld.t(lambda e, j=j, a=xTt: e.matmul(
                                    ps_if[:], a[:, j, :], wifs[:, j, :],
                                    start=(j == 0), stop=(j == DC - 1)),
                                  reads=[wifs_b, xTt_b], writes=[ps_if_b], track=(j == DC - 1))
                        for half in range(2):
                            bld.v(lambda e, ti=ti, half=half, pp=ps_v[half]: e.tensor_copy(
                                    VH[:, ti, half * 4:(half + 1) * 4, 0:128], pp[:].rearrange("p (h v) -> p h v", v=128)),
                                  reads=[ps_v_b[half]], writes=[VH_b])
                            bld.s(lambda e, ti=ti, half=half, pp=ps_p[half]: e.activation(
                                    so[:, ti, half * 512:(half + 1) * 512], pp[:], Act.Sigmoid),
                                  reads=[ps_p_b[half]], writes=[so_b])
                        bld.v(lambda e, ti=ti: e.tensor_copy(ifsb[:, ti, :], ps_if[:]),
                              reads=[ps_if_b], writes=[ifsb_b])
                        bld.t(lambda e, ti=ti: e.transpose(ps_t[0:8, :], ifsb[:, ti, 0:8], idf[:]),
                              reads=[ifsb_b, idf_b], writes=[ps_t_b])
                        bld.v(lambda e, ti=ti: e.tensor_copy(ir_row[:, ti * TT:(ti + 1) * TT], ps_t[0:8, :]),
                              reads=[ps_t_b], writes=[ir_b])
                        bld.t(lambda e, ti=ti: e.transpose(ps_t[0:8, :], ifsb[:, ti, 8:16], idf[:]),
                              reads=[ifsb_b, idf_b], writes=[ps_t_b])
                        bld.v(lambda e, ti=ti: e.tensor_copy(fr_row[:, ti * TT:(ti + 1) * TT], ps_t[0:8, :]),
                              reads=[ps_t_b], writes=[fr_b])
                    bld.flush_phase("a1")
                gr_row = sb("gr_row", [8, S], F32, a12stack)
                gr_b = Buf()
                tmp_row = sb("tmp_row", [8, S], F32, a12stack)
                tmp_b = Buf()
                lfcr = sb("lfcr", [8, S], F32, a12stack)
                lfcr_b = Buf()
                Mr = sb("Mr", [8, S], F32, a12stack)
                Mr_b = Buf()
                M_loc = sb("M_loc", [128, TT], F32, a12stack)
                M_loc_b = Buf()
                nm_loc = sb("nm_loc", [128, TT], F32, a12stack)
                nm_loc_b = Buf()
                g_rep = sb("g_rep", [128, S], F32, a12stack)
                g_rep_b = Buf()
                PL = sb("PL", [128, 2], F32, a12stack)
                PL_b = Buf()
                ps_tr = psum("a2pst", [128, 128], F32, a12stack)
                ps_tr_b = Buf()
                bld.s(lambda e: e.activation(ir_row[:], ir_row[:], Act.Tanh, bias=bi_s[:, 1:2], scale=1.0 / CAP),
                      reads=[ir_b, bi_b], writes=[ir_b])
                bld.v(lambda e: e.tensor_single_scalar(ir_row[:], ir_row[:], CAP, Alu.mult),
                      reads=[ir_b], writes=[ir_b])
                bld.s(lambda e: e.activation(fr_row[:], fr_row[:], Act.Tanh, bias=bf_s[:, 1:2], scale=1.0 / CAP),
                      reads=[fr_b, bf_b], writes=[fr_b])
                bld.v(lambda e: e.tensor_single_scalar(fr_row[:], fr_row[:], CAP, Alu.mult),
                      reads=[fr_b], writes=[fr_b])
                bld.s(lambda e: e.activation(tmp_row[:], fr_row[:], Act.Abs),
                      reads=[fr_b], writes=[tmp_b])
                bld.s(lambda e: e.activation(tmp_row[:], tmp_row[:], Act.Exp, scale=-1.0),
                      reads=[tmp_b], writes=[tmp_b])
                bld.s(lambda e: e.activation(tmp_row[:], tmp_row[:], Act.Ln, bias=1.0),
                      reads=[tmp_b], writes=[tmp_b])
                bld.v(lambda e: e.scalar_tensor_tensor(tmp_row[:], fr_row[:], 0.0, tmp_row[:], Alu.min, Alu.subtract),
                      reads=[fr_b, tmp_b], writes=[tmp_b])
                bld.v(lambda e: e.tensor_tensor_scan(lfcr[:], tmp_row[:], tmp_row[:], 0.0, Alu.add, Alu.bypass),
                      reads=[tmp_b], writes=[lfcr_b])
                bld.v(lambda e: e.tensor_sub(gr_row[:], ir_row[:], lfcr[:]),
                      reads=[ir_b, lfcr_b], writes=[gr_b])
                bld.v(lambda e: e.tensor_tensor_scan(Mr[:], gr_row[:], gr_row[:], -1e30, Alu.max, Alu.bypass),
                      reads=[gr_b], writes=[Mr_b])
                bld.v(lambda e: e.scalar_tensor_tensor(tmp_row[:], lfcr[:], -1.0, Mr[:], Alu.mult, Alu.subtract),
                      reads=[lfcr_b, Mr_b], writes=[tmp_b])
                for i in range(NTT):
                    bld.dma("qS", lambda e, i=i: e.dma_start(M_loc[i * 8:(i + 1) * 8, :], Mr[:, i * TT:(i + 1) * TT]),
                            reads=[Mr_b], writes=[M_loc_b])
                    bld.dma("qA", lambda e, i=i: e.dma_start(nm_loc[i * 8:(i + 1) * 8, :], tmp_row[:, i * TT:(i + 1) * TT]),
                            reads=[tmp_b], writes=[nm_loc_b])
                    bld.dma("qG", lambda e, i=i: e.dma_start(g_rep[i * 8:(i + 1) * 8, :], gr_row[:]),
                            reads=[gr_b], writes=[g_rep_b])
                bld.v(lambda e: e.tensor_copy(PL[:, 0:1], M_loc[:, TT - 1:TT]), reads=[M_loc_b], writes=[PL_b])
                bld.v(lambda e: e.tensor_single_scalar(PL[:, 1:2], PL[:, 0:1], LN8, Alu.add), reads=[PL_b], writes=[PL_b])
                bld.v(lambda e: e.tensor_scalar(g_rep[:], g_rep[:], PL[:, 1:2], 0.0, Alu.subtract, Alu.min),
                      reads=[g_rep_b, PL_b], writes=[g_rep_b])
                bld.s(lambda e: e.activation(g_rep[:], g_rep[:], Act.Exp), reads=[g_rep_b], writes=[g_rep_b])
                for tau in range(NTT):
                    bld.t(lambda e, tau=tau: e.transpose(ps_tr[:], g_rep[:, tau * TT:(tau + 1) * TT], idf[:]),
                          reads=[g_rep_b, idf_b], writes=[ps_tr_b])
                    bld.v(lambda e, tau=tau: e.tensor_copy(ecol_t[:, tau, :], ps_tr[:]),
                          reads=[ps_tr_b], writes=[ecol_t_b])
                bld.v(lambda e: e.tensor_scalar(M_loc[:], M_loc[:], PL[:, 0:1], -85.0, Alu.subtract, Alu.max),
                      reads=[M_loc_b, PL_b], writes=[M_loc_b])
                bld.s(lambda e: e.activation(M_loc[:], M_loc[:], Act.Exp, scale=-1.0), reads=[M_loc_b], writes=[M_loc_b])
                bld.t(lambda e: e.transpose(ps_tr[:], M_loc[:], idf[:]), reads=[M_loc_b, idf_b], writes=[ps_tr_b])
                bld.v(lambda e: e.tensor_copy(erow_t[:], ps_tr[:]), reads=[ps_tr_b], writes=[erow_t_b])
                bld.s(lambda e: e.activation(nm_loc[:], nm_loc[:], Act.Exp), reads=[nm_loc_b], writes=[nm_loc_b])
                bld.t(lambda e: e.transpose(ps_tr[:], nm_loc[:], idf[:]), reads=[nm_loc_b, idf_b], writes=[ps_tr_b])
                bld.v(lambda e: e.tensor_copy(emb_t[:], ps_tr[:]), reads=[ps_tr_b], writes=[emb_t_b])
                bld.flush_phase("a2")

            with contextlib.ExitStack() as astack3:
                wdns = sb("wdns", [128, DC, D], BF16, astack3)
                wdns_b = Buf()
                bld.dma("qG", lambda e: e.dma_start(wdns[:], wdn_d.ap().rearrange("(j p) q -> p j q", p=128)),
                        writes=[wdns_b])
                ct = [sb(f"ct{i}", [TT, 512], BF16, astack3) for i in range(2)]
                ct_b = [Buf() for _ in range(2)]
                acat = sb("acat", [TT, 4, H, 132], F32, astack3)
                acat_b = Buf()
                nsm = sb("nsm", [TT, 8 * 8], F32, astack3)
                nsm_b = Buf()
                sqt = sb("sqt", [TT, H, 128], F32, astack3)
                sqt_b = Buf()
                hub = sb("hub", [TT, H, 128], F32, astack3)
                hub_b = Buf()
                wsr = sb("wsr", [TT, D], F32, astack3)
                wsr_b = Buf()
                hg = [sb(f"hg{i}", [TT, D], BF16, astack3) for i in range(2)]
                hg_b = [Buf() for _ in range(2)]
                hgT = [sb(f"hgT{i}", [128, DC, TT], BF16, astack3, is_transpose_dest=True) for i in range(2)]
                hgT_b = [Buf() for _ in range(2)]
                x1t = [sb(f"x1t{i}", [TT, D], F32, astack3) for i in range(2)]
                x1t_b = [Buf() for _ in range(2)]
                x2t = [sb(f"x2t{i}", [TT, D], F32, astack3) for i in range(2)]
                x2t_b = [Buf() for _ in range(2)]
                ps_sc = [psum(f"pssc{i}", [TT, 512], F32, astack3) for i in range(2)]
                ps_sc_b = [Buf() for _ in range(2)]
                ps_a = [psum(f"psa{i}", [TT, 132], F32, astack3) for i in range(4)]
                ps_a_b = [Buf() for _ in range(4)]
                ps_o = [psum(f"pso{i}", [TT, 512], F32, astack3) for i in range(2)]
                ps_o_b = [Buf() for _ in range(2)]

                for i4 in range(4):
                    for h in range(H):
                        gq, sub = h // 2, (h % 2) * 64

                        def emit_scores(tau, i4=i4, gq=gq, sub=sub):
                            cb = tau % 2
                            psc, psc_b = ps_sc[cb], ps_sc_b[cb]
                            bld.t(lambda e, i4=i4, tau=tau, gq=gq, sub=sub, psc=psc: e.matmul(
                                    psc[:], KT[sub:sub + 64, gq, tau * TT:(tau + 1) * TT],
                                    QT[sub:sub + 64, gq, i4 * 512:(i4 + 1) * 512], start=True, stop=True),
                                  reads=[KT_b, QT_b], writes=[psc_b])

                        emit_scores(0)
                        for tau in range(4 * i4 + 4):
                            cb = tau % 2
                            psc, psc_b = ps_sc[cb], ps_sc_b[cb]
                            if tau + 1 < 4 * i4 + 4:
                                emit_scores(tau + 1)
                            for il in range(4):
                                i = 4 * i4 + il
                                if tau > i:
                                    continue
                                col = i * 8 + h
                                if tau == i:
                                    bld.v(lambda e, tau=tau, col=col, il=il, cbuf=ct[cb], psc=psc: e.scalar_tensor_tensor(
                                            cbuf[:, il * TT:(il + 1) * TT], psc[:, il * TT:(il + 1) * TT],
                                            ecol_t[:, tau, col:col + 1], maskt[:], Alu.mult, Alu.mult),
                                          reads=[psc_b, ecol_t_b, maskt_b], writes=[ct_b[cb]])
                                elif il % 2 == 0:
                                    bld.s(lambda e, tau=tau, col=col, il=il, cbuf=ct[cb], psc=psc: e.activation(
                                            cbuf[:, il * TT:(il + 1) * TT], psc[:, il * TT:(il + 1) * TT],
                                            Act.Copy, scale=ecol_t[:, tau, col:col + 1]),
                                          reads=[psc_b, ecol_t_b], writes=[ct_b[cb]])
                                else:
                                    bld.v(lambda e, tau=tau, col=col, il=il, cbuf=ct[cb], psc=psc: e.tensor_scalar(
                                            cbuf[:, il * TT:(il + 1) * TT], psc[:, il * TT:(il + 1) * TT],
                                            ecol_t[:, tau, col:col + 1], None, Alu.mult),
                                          reads=[psc_b, ecol_t_b], writes=[ct_b[cb]])
                                bld.t(lambda e, i=i, tau=tau, h=h, il=il, cbuf=ct[cb], pa=ps_a[il]: e.matmul(
                                        pa[:, 0:129], cbuf[:, il * TT:(il + 1) * TT], VH[:, tau, h, 0:129],
                                        start=(tau == 0), stop=(tau == i)),
                                      reads=[ct_b[cb], VH_b], writes=[ps_a_b[il]], track=(tau == i))
                        for il in range(4):
                            col = (4 * i4 + il) * 8 + h
                            bld.v(lambda e, h=h, il=il, col=col, pa=ps_a[il]: e.tensor_scalar(
                                    acat[:, il, h, 0:129], pa[:, 0:129], erow_t[:, col:col + 1], None, Alu.mult),
                                  reads=[ps_a_b[il], erow_t_b], writes=[acat_b])
                    for il in range(4):
                        i = 4 * i4 + il
                        bld.dma("qG", lambda e, i=i, x=x1t[i % 2]: e.dma_start(x[:], xs1.ap()[i * TT:(i + 1) * TT, :]),
                                writes=[x1t_b[i % 2]])
                        c0 = i * 8
                        rsv = acat[:, il, :, 128:129]
                        bld.v(lambda e, rsv=rsv: e.tensor_scalar(nsm[:, 0:8], rsv, -1.0, None, Alu.mult),
                              reads=[acat_b], writes=[nsm_b])
                        bld.v(lambda e, rsv=rsv: e.tensor_tensor(nsm[:, 0:8].rearrange("p (h o) -> p h o", o=1), rsv,
                                                        nsm[:, 0:8].rearrange("p (h o) -> p h o", o=1), Alu.max),
                              reads=[acat_b, nsm_b], writes=[nsm_b])
                        bld.v(lambda e, c0=c0: e.tensor_tensor(nsm[:, 0:8], nsm[:, 0:8], emb_t[:, c0:c0 + 8], Alu.max),
                              reads=[nsm_b, emb_t_b], writes=[nsm_b])
                        bld.v(lambda e: e.tensor_single_scalar(nsm[:, 0:8], nsm[:, 0:8], ATT_EPS, Alu.add),
                              reads=[nsm_b], writes=[nsm_b])
                        bld.v(lambda e: e.reciprocal(nsm[:, 8:16], nsm[:, 0:8]), reads=[nsm_b], writes=[nsm_b])
                        av = acat[:, il, :, 0:128]
                        bld.v(lambda e, av=av: e.tensor_reduce(nsm[:, 16:24].rearrange("p (h o) -> p h o", o=1), av, Ax.X, Alu.add),
                              reads=[acat_b], writes=[nsm_b])
                        bld.s(lambda e, av=av: e.activation(sqt[:], av, Act.Square), reads=[acat_b], writes=[sqt_b])
                        bld.v(lambda e: e.tensor_reduce(nsm[:, 24:32].rearrange("p (h o) -> p h o", o=1), sqt[:], Ax.X, Alu.add),
                              reads=[sqt_b], writes=[nsm_b])
                        bld.v(lambda e: e.tensor_single_scalar(nsm[:, 16:24], nsm[:, 16:24], 1.0 / 128.0, Alu.mult),
                              reads=[nsm_b], writes=[nsm_b])
                        bld.v(lambda e: e.tensor_single_scalar(nsm[:, 24:32], nsm[:, 24:32], 1.0 / 128.0, Alu.mult),
                              reads=[nsm_b], writes=[nsm_b])
                        bld.v(lambda e: e.tensor_tensor(nsm[:, 32:40], nsm[:, 16:24], nsm[:, 16:24], Alu.mult),
                              reads=[nsm_b], writes=[nsm_b])
                        bld.v(lambda e: e.tensor_tensor(nsm[:, 24:32], nsm[:, 24:32], nsm[:, 32:40], Alu.subtract),
                              reads=[nsm_b], writes=[nsm_b])
                        bld.v(lambda e: e.tensor_tensor(nsm[:, 32:40], nsm[:, 8:16], nsm[:, 8:16], Alu.mult),
                              reads=[nsm_b], writes=[nsm_b])
                        bld.v(lambda e: e.tensor_tensor(nsm[:, 24:32], nsm[:, 24:32], nsm[:, 32:40], Alu.mult),
                              reads=[nsm_b], writes=[nsm_b])
                        bld.s(lambda e: e.activation(nsm[:, 32:40], nsm[:, 24:32], Act.Sqrt, bias=cst[:, 1:2]),
                              reads=[nsm_b, cst_b], writes=[nsm_b])
                        bld.v(lambda e: e.reciprocal(nsm[:, 40:48], nsm[:, 32:40]), reads=[nsm_b], writes=[nsm_b])
                        bld.v(lambda e: e.tensor_tensor(nsm[:, 40:48], nsm[:, 40:48], nsm[:, 8:16], Alu.mult),
                              reads=[nsm_b], writes=[nsm_b])
                        mu3 = nsm[:, 16:24].rearrange("p (h o) -> p h o", o=1)
                        bld.v(lambda e, av=av: e.tensor_tensor(hub[:], av, bcast(av, mu3), Alu.subtract),
                              reads=[acat_b, nsm_b], writes=[hub_b])
                        w3 = wsr[:].rearrange("p (h v) -> p h v", v=128)
                        rr3 = nsm[:, 40:48].rearrange("p (h o) -> p h o", o=1)
                        bld.g(lambda e, i=i: e.tensor_tensor(wsr[:], onw[:], so[:, i, :], Alu.mult),
                              reads=[onw_b, so_b], writes=[wsr_b])
                        bld.g(lambda e: e.tensor_tensor(w3, w3, bcast(w3, rr3), Alu.mult),
                              reads=[wsr_b, nsm_b], writes=[wsr_b])
                        hgi, hgi_b = hg[i % 2], hg_b[i % 2]
                        bld.v(lambda e, o=hgi: e.tensor_tensor(o[:].rearrange("p (h v) -> p h v", v=128), hub[:], w3, Alu.mult),
                              reads=[hub_b, wsr_b], writes=[hgi_b])
                        hgTi, hgTi_b = hgT[i % 2], hgT_b[i % 2]
                        bld.dma("qS" if i % 2 == 0 else "qA",
                                lambda e, a=hgi, b=hgTi: e.dma_start(
                                    b[:], a[:].rearrange("p (j q) -> p j q", q=128), transpose=True),
                                reads=[hgi_b], writes=[hgTi_b])
                        for j in range(DC):
                            for half in range(2):
                                po, po_b = ps_o[half], ps_o_b[half]
                                bld.t(lambda e, j=j, half=half, a=hgTi, po=po: e.matmul(
                                        po[:], a[:, j, :], wdns[:, j, half * 512:(half + 1) * 512],
                                        start=(j == 0), stop=(j == DC - 1)),
                                      reads=[hgTi_b, wdns_b], writes=[po_b], track=(j == DC - 1))
                        x2i, x2i_b = x2t[i % 2], x2t_b[i % 2]
                        for half in range(2):
                            po, po_b = ps_o[half], ps_o_b[half]
                            bld.v(lambda e, half=half, x=x1t[i % 2], o=x2i, po=po: e.tensor_tensor(
                                    o[:, half * 512:(half + 1) * 512], po[:], x[:, half * 512:(half + 1) * 512], Alu.add),
                                  reads=[po_b, x1t_b[i % 2]], writes=[x2i_b])
                        bld.dma("qS", lambda e, i=i, o=x2i: e.dma_start(xs2.ap()[i * TT:(i + 1) * TT, :], o[:]),
                                reads=[x2i_b])
                bld.flush_phase("a3")

        with contextlib.ExitStack() as w2stack:
            wgqT2 = sb("wgqT2", [128, DC, DI], BF16, w2stack)
            wuqT2 = sb("wuqT2", [128, DC, DI], BF16, w2stack)
            wdqT2 = sb("wdqT2", [128, IC, D], BF16, w2stack)
            wgqT2_b, wuqT2_b, wdqT2_b = Buf(), Buf(), Buf()
            bld.dma("qS", lambda e: e.dma_start(wgqT2[:], wgc.ap().rearrange("(j p) i -> p j i", p=128)), writes=[wgqT2_b])
            bld.dma("qA", lambda e: e.dma_start(wuqT2[:], wuc.ap().rearrange("(j p) i -> p j i", p=128)), writes=[wuqT2_b])
            bld.dma("qS", lambda e: e.dma_start(wdqT2[:], wdc.ap().rearrange("(j p) i -> p j i", p=128)), writes=[wdqT2_b])
            with contextlib.ExitStack() as fstack2:
                ffn_phase("f2", xs2, out_d, wgqT2, wuqT2, wdqT2, (wgqT2_b, wuqT2_b, wdqT2_b), fstack2)

    return nc


def kernel(L0_attn_x, L0_attn_Wq, L0_attn_Wk, L0_attn_Wv, L0_attn_Wo, L0_attn_Wi,
           L0_attn_bi, L0_attn_Wf, L0_attn_bf, L0_attn_outnorm_w, L0_attn_Wdown,
           L0_ffn_w_gate, L0_ffn_w_up, L0_ffn_w_down):
    f32 = lambda a: np.ascontiguousarray(np.asarray(a), dtype=np.float32)
    x = f32(L0_attn_x)
    wif = np.ascontiguousarray(np.concatenate([f32(L0_attn_Wi), f32(L0_attn_Wf)], axis=1))
    bif = np.ascontiguousarray(np.concatenate([f32(L0_attn_bi), f32(L0_attn_bf)]).reshape(16, 1))
    onw = np.ascontiguousarray(np.broadcast_to(f32(L0_attn_outnorm_w).reshape(1, D), (128, D)))
    base = {
        "wq": f32(L0_attn_Wq), "wk": f32(L0_attn_Wk), "wv": f32(L0_attn_Wv),
        "wo": f32(L0_attn_Wo), "wif": wif, "bif": bif, "onw": onw,
        "wdn": f32(L0_attn_Wdown), "wg": f32(L0_ffn_w_gate), "wu": f32(L0_ffn_w_up),
        "wd": f32(L0_ffn_w_down),
    }
    in_maps = [dict(base, x=np.ascontiguousarray(x[c])) for c in range(8)]
    nc = build_core_program()
    res = run_bass_kernel_spmd(nc, in_maps, core_ids=list(range(8)))
    return np.stack([np.asarray(r["out"], dtype=np.float32) for r in res.results], axis=0)


# version 18
# speedup vs baseline: 1.1587x; 1.0198x over previous
import contextlib
import numpy as np

import concourse.bass as bass
import concourse.mybir as mybir
from concourse.bass_utils import run_bass_kernel_spmd

F32 = mybir.dt.float32
BF16 = mybir.dt.bfloat16
FP16 = mybir.dt.float16
Alu = mybir.AluOpType
Act = mybir.ActivationFunctionType
Ax = mybir.AxisListType

ENGINES = ("tensor", "vector", "scalar", "gpsimd", "sync")
QUEUES = {"qS": "sync", "qA": "scalar", "qG": "gpsimd"}
DMA_POOL = 10

S, D, DI, H = 2048, 1024, 2816, 8
TT = 128
NTT = S // TT
DC = D // 128
IC = DI // 128
DHQK, DHV = 64, 128
LN_EPS = 1e-5
ATT_EPS = 1e-6
CAP = 15.0
LN8 = float(np.log(8.0))


class Buf:
    __slots__ = ("w", "rs", "name")

    def __init__(self, name=""):
        self.w = None
        self.rs = []
        self.name = name


class B:

    def __init__(self, nc, stack):
        self.nc = nc
        self.stack = stack
        self.sems = {}
        self.counts = {}
        self.dma_sems = {}
        self.dma_rr = {}
        for name in ("T", "V", "Sx", "G"):
            self._mk(name)
        for q in QUEUES:
            self.dma_sems[q] = []
            self.dma_rr[q] = 0
            for k in range(DMA_POOL):
                n = f"{q}_{k}"
                self._mk(n)
                self.dma_sems[q].append(n)
        self.engine_sem = {"tensor": "T", "vector": "V", "scalar": "Sx", "gpsimd": "G"}
        self.streams = {e: [] for e in ENGINES}
        self.is_dma_sem = {s for q in QUEUES for s in self.dma_sems[q]}

    def _mk(self, name):
        self.sems[name] = self.stack.enter_context(self.nc.semaphore(name))
        self.counts[name] = 0

    def _fix(self, tok):
        s, v = tok
        if s in self.is_dma_sem:
            return (s, self.counts[s])
        return tok

    def op(self, eng, fn, reads=(), writes=(), extra=(), sem=None, amt=1, track=True):
        waits = []
        for b in reads:
            if b.w is not None:
                waits.append(self._fix(b.w))
        for b in writes:
            for t_ in b.rs:
                waits.append(self._fix(t_))
            if b.w is not None:
                waits.append(self._fix(b.w))
        waits.extend(self._fix(t_) for t_ in extra)
        tok = None
        if track:
            s = sem if sem is not None else self.engine_sem[eng]
            self.counts[s] += amt
            tok = (s, self.counts[s])
        self.streams[eng].append((fn, tuple(waits), (tok[0], amt) if tok else None))
        if tok:
            for b in reads:
                b.rs.append(tok)
            for b in writes:
                b.w = tok
                b.rs = []
        return tok

    def t(self, fn, reads=(), writes=(), extra=(), track=True):
        return self.op("tensor", fn, reads, writes, extra, track=track)

    def v(self, fn, reads=(), writes=(), extra=()):
        return self.op("vector", fn, reads, writes, extra)

    def s(self, fn, reads=(), writes=(), extra=()):
        return self.op("scalar", fn, reads, writes, extra)

    def g(self, fn, reads=(), writes=(), extra=()):
        return self.op("gpsimd", fn, reads, writes, extra)

    def vg(self, k, fn, reads=(), writes=(), extra=()):
        return self.v(fn, reads, writes, extra)

    def dma(self, queue, fn, reads=(), writes=(), extra=()):
        eng = QUEUES[queue]
        idx = self.dma_rr[queue]
        self.dma_rr[queue] = (idx + 1) % DMA_POOL
        sem = self.dma_sems[queue][idx]
        return self.op(eng, fn, reads, writes, extra, sem=sem, amt=16)

    def flush_phase(self, name=None):
        for q, eng in QUEUES.items():
            toks = [(s, self.counts[s]) for s in self.dma_sems[q] if self.counts[s] > 0]
            if toks:
                self.streams[eng].append((None, tuple(toks), None))
        streams = {e: list(s) for e, s in self.streams.items()}
        self.streams = {e: [] for e in ENGINES}
        nonempty = [e for e in ENGINES if streams[e]]
        if not nonempty:
            return
        with self.nc.Block(name) as block:
            for eng in nonempty:
                def emitter(e, _eng=eng):
                    lw = {}
                    for fn, waits, inc in streams[_eng]:
                        for (sname, v_) in waits:
                            if lw.get(sname, 0) < v_:
                                e.wait_ge(self.sems[sname], v_)
                                lw[sname] = v_
                        if fn is None:
                            continue
                        inst = fn(e)
                        if inc is not None:
                            inst.then_inc(self.sems[inc[0]], inc[1])
                getattr(block, eng)(emitter)


def bcast(full_view, small_view):
    _, b2 = bass.broadcast_tensor_aps(full_view, small_view)
    return b2


def emit_quantize(bld, nc, x_ap, x_buf, out_ap, out_buf, scr, nblk, fin_v=False):
    bm, bm_b = scr["bm"], scr["bm_b"]
    rs, rs_b = scr["rs"], scr["rs_b"]
    xn, xn_b = scr["xn"], scr["xn_b"]
    axn, axn_b = scr["axn"], scr["axn_b"]
    nA, nA_b = scr["nA"], scr["nA_b"]
    nB, nB_b = scr["nB"], scr["nB_b"]
    n = nblk * 16
    xreads = x_buf if isinstance(x_buf, list) else [x_buf]
    x3 = x_ap.rearrange("p (b s) -> p b s", s=16)
    bm3 = bm[:, 0:nblk].rearrange("p (b o) -> p b o", o=1)
    bld.v(lambda e: e.tensor_reduce(bm3, x3, Ax.X, Alu.max, apply_absolute_value=True),
          reads=xreads, writes=[bm_b])
    bld.v(lambda e: e.tensor_scalar(bm[:, 0:nblk], bm[:, 0:nblk], 1.0 / 6.0, 1e-12, Alu.mult, Alu.max),
          reads=[bm_b], writes=[bm_b])
    bld.v(lambda e: e.reciprocal(rs[:, 0:nblk], bm[:, 0:nblk]), reads=[bm_b], writes=[rs_b])
    xn3 = xn[:, 0:n].rearrange("p (b s) -> p b s", s=16)
    rs3 = rs[:, 0:nblk].rearrange("p (b o) -> p b o", o=1)
    bld.v(lambda e: e.tensor_tensor(xn3, x3, bcast(x3, rs3), Alu.mult),
          reads=xreads + [rs_b], writes=[xn_b])
    bld.s(lambda e: e.activation(axn[:, 0:n], xn[:, 0:n], Act.Square),
          reads=[xn_b], writes=[axn_b])
    bld.s(lambda e: e.sign(xn[:, 0:n], xn[:, 0:n]), reads=[xn_b], writes=[xn_b])
    bld.vg(0, lambda e: e.tensor_single_scalar(nA[:, 0:n], axn[:, 0:n], 0.0625, Alu.is_gt),
           reads=[axn_b], writes=[nA_b])
    for k, c in enumerate((0.5625, 1.5625, 3.0625)):
        bld.vg(k + 1, lambda e, c=c: e.scalar_tensor_tensor(nA[:, 0:n], axn[:, 0:n], c, nA[:, 0:n], Alu.is_gt, Alu.add),
               reads=[axn_b, nA_b], writes=[nA_b])
    bld.vg(0, lambda e: e.tensor_single_scalar(nB[:, 0:n], axn[:, 0:n], 6.25, Alu.is_gt),
           reads=[axn_b], writes=[nB_b])
    for k, c in enumerate((12.25, 25.0, 25.0)):
        bld.vg(k + 1, lambda e, c=c: e.scalar_tensor_tensor(nB[:, 0:n], axn[:, 0:n], c, nB[:, 0:n], Alu.is_gt, Alu.add),
               reads=[axn_b, nB_b], writes=[nB_b])
    bld.vg(0, lambda e: e.scalar_tensor_tensor(nA[:, 0:n], nA[:, 0:n], 0.5, nB[:, 0:n], Alu.mult, Alu.add),
           reads=[nA_b, nB_b], writes=[nA_b])
    fin = bld.v if fin_v else bld.g
    fin(lambda e: e.tensor_tensor(nA[:, 0:n], nA[:, 0:n], xn[:, 0:n], Alu.mult),
        reads=[nA_b, xn_b], writes=[nA_b])
    o3 = out_ap.rearrange("p (b s) -> p b s", s=16)
    nA3 = nA[:, 0:n].rearrange("p (b s) -> p b s", s=16)
    fin(lambda e: e.tensor_tensor(o3, nA3, bcast(nA3, bm3), Alu.mult),
        reads=[nA_b, bm_b], writes=[out_buf])


def build_core_program():
    nc = bass.Bass("TRN2", target_bir_lowering=False)
    x_in = nc.dram_tensor("x", [S, D], F32, kind="ExternalInput")
    wq_d = nc.dram_tensor("wq", [D, 512], F32, kind="ExternalInput")
    wk_d = nc.dram_tensor("wk", [D, 512], F32, kind="ExternalInput")
    wv_d = nc.dram_tensor("wv", [D, D], F32, kind="ExternalInput")
    wo_d = nc.dram_tensor("wo", [D, D], F32, kind="ExternalInput")
    wif_d = nc.dram_tensor("wif", [D, 16], F32, kind="ExternalInput")
    bif_d = nc.dram_tensor("bif", [16, 1], F32, kind="ExternalInput")
    onw_d = nc.dram_tensor("onw", [128, D], F32, kind="ExternalInput")
    wdn_d = nc.dram_tensor("wdn", [D, D], F32, kind="ExternalInput")
    wg_d = nc.dram_tensor("wg", [DI, D], F32, kind="ExternalInput")
    wu_d = nc.dram_tensor("wu", [DI, D], F32, kind="ExternalInput")
    wd_d = nc.dram_tensor("wd", [D, DI], F32, kind="ExternalInput")
    out_d = nc.dram_tensor("out", [S, D], F32, kind="ExternalOutput")
    xs1 = nc.dram_tensor("xs1", [S, D], F32)
    xs2 = nc.dram_tensor("xs2", [S, D], F32)
    wgc = nc.dram_tensor("wgc", [D, DI], BF16)
    wuc = nc.dram_tensor("wuc", [D, DI], BF16)
    wdc = nc.dram_tensor("wdc", [DI, D], BF16)

    with contextlib.ExitStack() as gstack:
        bld = B(nc, gstack)
        sb = lambda name, shape, dt, st=gstack, **kw: st.enter_context(nc.sbuf_tensor(name, shape, dt, **kw))
        psum = lambda name, shape, dt, st=gstack: st.enter_context(nc.psum_tensor(name, shape, dt))

        cst = sb("cst", [128, 2], F32)
        cst_b = Buf("cst")
        bld.g(lambda e: e.memset(cst[:, 0:1], LN_EPS), writes=[cst_b])
        bld.g(lambda e: e.memset(cst[:, 1:2], ATT_EPS), writes=[cst_b])

        def ffn_phase(tag, x_src, x_dst, wgqT, wuqT, wdqT, w_bufs, stack, wprep=None):
            wg_hb, wu_hb, wd_b = w_bufs
            xt = [sb(f"{tag}xt{i}", [TT, D], F32, stack) for i in range(3)]
            xt_b = [Buf() for _ in range(3)]
            xln = [sb(f"{tag}xln{i}", [TT, D], F32, stack) for i in range(2)]
            xln_b = [Buf() for _ in range(2)]
            xq = [sb(f"{tag}xq{i}", [TT, D], BF16, stack) for i in range(2)]
            xq_b = [Buf() for _ in range(2)]
            xqT = [sb(f"{tag}xqT{i}", [128, DC, TT], BF16, stack, is_transpose_dest=True) for i in range(2)]
            xqT_hb = [[Buf(), Buf()] for _ in range(2)]
            hq = [sb(f"{tag}hq{i}", [TT, DI], BF16, stack) for i in range(2)]
            hq_b = [Buf() for _ in range(2)]
            hqT = [sb(f"{tag}hqT{i}", [128, IC, TT], BF16, stack, is_transpose_dest=True) for i in range(2)]
            hqT_b = [Buf() for _ in range(2)]
            hft = sb(f"{tag}hf", [TT, 1024], F32, stack)
            hf_b = [Buf() for _ in range(2)]
            st_t = sb(f"{tag}st", [TT, 16], F32, stack)
            st_b = Buf()
            scr2 = []
            for si, wd_ in ((0, 1024), (1, 512)):
                scr2.append({
                    "bm": sb(f"{tag}bm{si}", [TT, wd_ // 16], F32, stack), "bm_b": Buf(),
                    "rs": sb(f"{tag}rs{si}", [TT, wd_ // 16], F32, stack), "rs_b": Buf(),
                    "xn": sb(f"{tag}xn{si}", [TT, wd_], F32, stack), "xn_b": Buf(),
                    "axn": sb(f"{tag}axn{si}", [TT, wd_], F32, stack), "axn_b": Buf(),
                    "nA": sb(f"{tag}nA{si}", [TT, wd_], BF16, stack), "nA_b": Buf(),
                    "nB": sb(f"{tag}nB{si}", [TT, wd_], BF16, stack), "nB_b": Buf(),
                })
            ps_g = [psum(f"{tag}psg{k}", [TT, 512], F32, stack) for k in range(3)]
            ps_g_b = [Buf() for _ in range(3)]
            ps_u = [psum(f"{tag}psu{k}", [TT, 512], F32, stack) for k in range(3)]
            ps_u_b = [Buf() for _ in range(3)]
            ps_d = [psum(f"{tag}psd{i}", [TT, 512], F32, stack) for i in range(2)]
            ps_d_b = [Buf() for _ in range(2)]

            ichunks = [(0, 512), (512, 512), (1024, 512), (1536, 512), (2048, 512), (2560, 256)]
            pend = []

            def emit_down(item):
                dti, dhqT, dhqT_b, dxt, dxt_b, dxn, dxn_b = item
                for j in range(IC):
                    for half in range(2):
                        pd, pd_b = ps_d[half], ps_d_b[half]
                        bld.t(lambda e, j=j, half=half, a=dhqT, w=wdqT, pd=pd: e.matmul(
                                pd[:], a[:, j, :], w[:, j, half * 512:(half + 1) * 512],
                                start=(j == 0), stop=(j == IC - 1)),
                              reads=[dhqT_b, wd_b], writes=[pd_b], track=(j == IC - 1))
                for half in range(2):
                    pd, pd_b = ps_d[half], ps_d_b[half]
                    bld.v(lambda e, half=half, x=dxt, o=dxn, pd=pd: e.scalar_tensor_tensor(
                            o[:, half * 512:(half + 1) * 512], pd[:], 0.5, x[:, half * 512:(half + 1) * 512],
                            Alu.mult, Alu.add),
                          reads=[pd_b, dxt_b], writes=[dxn_b])
                bld.dma("qS", lambda e, dti=dti, o=dxn: e.dma_start(x_dst.ap()[dti * TT:(dti + 1) * TT, :], o[:]),
                        reads=[dxn_b])

            def xload(ti):
                xti, xti_b = xt[ti % 3], xt_b[ti % 3]
                bld.dma("qG", lambda e, ti=ti, x=xti: e.dma_start(x[:], x_src.ap()[ti * TT:(ti + 1) * TT, :]),
                        writes=[xti_b])

            def xside(ti):
                xti, xti_b = xt[ti % 3], xt_b[ti % 3]
                xlni, xlni_b = xln[ti % 2], xln_b[ti % 2]
                xqi, xqi_b = xq[ti % 2], xq_b[ti % 2]
                xqTi, xqTi_hb = xqT[ti % 2], xqT_hb[ti % 2]
                bld.v(lambda e, x=xti, s=st_t: e.bn_stats(s[:, 0:6], x[:, 0:512]),
                      reads=[xti_b], writes=[st_b])
                bld.v(lambda e, x=xti, s=st_t: e.bn_stats(s[:, 6:12], x[:, 512:1024]),
                      reads=[xti_b], writes=[st_b])
                bld.v(lambda e, s=st_t: e.bn_aggr(s[:, 12:14], s[:, 0:12]),
                      reads=[st_b], writes=[st_b])
                bld.s(lambda e, s=st_t: e.activation(s[:, 14:15], s[:, 13:14], Act.Sqrt, bias=cst[:, 0:1]),
                      reads=[st_b, cst_b], writes=[st_b])
                bld.v(lambda e, s=st_t: e.reciprocal(s[:, 15:16], s[:, 14:15]),
                      reads=[st_b], writes=[st_b])
                bld.v(lambda e, x=xti, o=xlni, s=st_t: e.tensor_scalar(o[:], x[:], s[:, 12:13], s[:, 15:16], Alu.subtract, Alu.mult),
                      reads=[xti_b, st_b], writes=[xlni_b])
                for hh in range(2):
                    emit_quantize(bld, nc, xlni[:, hh * 512:(hh + 1) * 512], xlni_b,
                                  xqi[:, hh * 512:(hh + 1) * 512], xqi_b, scr2[1], 32)
                    bld.dma("qS" if (ti + hh) % 2 == 0 else "qA",
                            lambda e, hh=hh, a=xqi, b=xqTi: e.dma_start(
                                b[:, hh * 4:(hh + 1) * 4, :],
                                a[:, hh * 512:(hh + 1) * 512].rearrange("p (j q) -> p j q", q=128), transpose=True),
                            reads=[xqi_b], writes=[xqT_hb[ti % 2][hh]])

            if wprep is not None:
                wg_d_, wu_d_, wd_d_, wgc_, wuc_, wdc_ = wprep
                wload, wload_b = [xln[0], xln[1]], [xln_b[0], xln_b[1]]
                wqp, wqp_b = [xq[0], xq[1]], [xq_b[0], xq_b[1]]
                for half, (c0, c1) in enumerate(((0, 12), (12, IC))):
                    for wsrc, wdstT, whb in ((wg_d_, wgqT, wg_hb), (wu_d_, wuqT, wu_hb)):
                        for c in range(c0, c1):
                            wc, wc_b = wload[c % 2], wload_b[c % 2]
                            wq_, wq_b_ = wqp[c % 2], wqp_b[c % 2]
                            bld.dma("qG", lambda e, c=c, wsrc=wsrc, wc=wc: e.dma_start(
                                    wc[:], wsrc.ap()[c * 128:(c + 1) * 128, :]), writes=[wc_b])
                            emit_quantize(bld, nc, wc[:], wc_b, wq_[:], wq_b_, scr2[0], 64, fin_v=True)
                            bld.dma("qS" if c % 2 == 0 else "qA",
                                    lambda e, c=c, a=wq_, b=wdstT: e.dma_start(
                                        b[:, :, c * 128:(c + 1) * 128],
                                        a[:].rearrange("p (j q) -> p j q", q=128), transpose=True),
                                    reads=[wq_b_], writes=[whb[half]])
                bld.dma("qS", lambda e: e.dma_start(wgc_.ap().rearrange("(j p) i -> p j i", p=128), wgqT[:]),
                        reads=[wg_hb[0], wg_hb[1]])
                bld.dma("qA", lambda e: e.dma_start(wuc_.ap().rearrange("(j p) i -> p j i", p=128), wuqT[:]),
                        reads=[wu_hb[0], wu_hb[1]])
                for c in range(DC):
                    for sl, (wi0, wiw) in enumerate(((0, 1024), (1024, 1024), (2048, 768))):
                        wc, wc_b = wload[sl % 2], wload_b[sl % 2]
                        wq_, wq_b_ = wqp[sl % 2], wqp_b[sl % 2]
                        bld.dma("qG", lambda e, c=c, wi0=wi0, wiw=wiw, wc=wc: e.dma_start(
                                wc[:, 0:wiw], wd_d_.ap()[c * 128:(c + 1) * 128, wi0:wi0 + wiw]), writes=[wc_b])
                        emit_quantize(bld, nc, wc[:, 0:wiw], wc_b, wq_[:, 0:wiw], wq_b_, scr2[0], wiw // 16, fin_v=True)
                        bld.dma("qS" if sl % 2 == 0 else "qA",
                                lambda e, c=c, wi0=wi0, wiw=wiw, a=wq_: e.dma_start(
                                    wdqT[:, wi0 // 128:(wi0 + wiw) // 128, c * 128:(c + 1) * 128],
                                    a[:, 0:wiw].rearrange("p (j q) -> p j q", q=128), transpose=True),
                                reads=[wq_b_], writes=[wd_b])
                bld.dma("qS", lambda e: e.dma_start(wdc_.ap().rearrange("(j p) i -> p j i", p=128), wdqT[:]),
                        reads=[wd_b])

            xload(0)
            xload(1)
            xside(0)
            for ti in range(NTT):
                xti, xti_b = xt[ti % 3], xt_b[ti % 3]
                xlni, xlni_b = xln[ti % 2], xln_b[ti % 2]
                xqTi, xqTi_hb = xqT[ti % 2], xqT_hb[ti % 2]
                hqi, hqi_b = hq[ti % 2], hq_b[ti % 2]
                hqTi, hqTi_b = hqT[ti % 2], hqT_b[ti % 2]
                if ti + 1 < NTT:
                    xside(ti + 1)
                for grp in range(2):
                    cis = [grp * 3, grp * 3 + 1, grp * 3 + 2]
                    for j in range(DC):
                        for k, ci in enumerate(cis):
                            i0, iw = ichunks[ci]
                            bld.t(lambda e, j=j, i0=i0, iw=iw, k=k, a=xqTi, w=wgqT: e.matmul(
                                    ps_g[k][:, 0:iw], a[:, j, :], w[:, j, i0:i0 + iw],
                                    start=(j == 0), stop=(j == DC - 1)),
                                  reads=[xqTi_hb[j // 4], wg_hb[grp]], writes=[ps_g_b[k]], track=(j == DC - 1))
                            bld.t(lambda e, j=j, i0=i0, iw=iw, k=k, a=xqTi, w=wuqT: e.matmul(
                                    ps_u[k][:, 0:iw], a[:, j, :], w[:, j, i0:i0 + iw],
                                    start=(j == 0), stop=(j == DC - 1)),
                                  reads=[xqT_hb[ti % 2][j // 4], wu_hb[grp]], writes=[ps_u_b[k]], track=(j == DC - 1))
                    pieces = []
                    for k, ci in enumerate(cis):
                        i0, iw = ichunks[ci]
                        off = (ci % 2) * 512
                        hfc_b = hf_b[ci % 2]
                        bld.s(lambda e, iw=iw, k=k, off=off: e.activation(hft[:, off:off + iw], ps_g[k][:, 0:iw], Act.Silu),
                              reads=[ps_g_b[k]], writes=[hfc_b])
                        bld.v(lambda e, iw=iw, k=k, off=off: e.tensor_tensor(hft[:, off:off + iw], hft[:, off:off + iw], ps_u[k][:, 0:iw], Alu.mult),
                              reads=[hfc_b, ps_u_b[k]], writes=[hfc_b])
                        pieces.append((ci, i0, iw, off))
                        if len(pieces) == 2 or ci == cis[-1]:
                            if len(pieces) == 2 and pieces[0][3] == 0:
                                (ciA, i0A, iwA, _), (_, _, iwB, _) = pieces
                                emit_quantize(bld, nc, hft[:, 0:iwA + iwB], [hf_b[0], hf_b[1]],
                                              hqi[:, i0A:i0A + iwA + iwB], hqi_b, scr2[0], (iwA + iwB) // 16)
                                pieces = []
                            elif len(pieces) == 2:
                                (ciA, i0A, iwA, offA) = pieces.pop(0)
                                emit_quantize(bld, nc, hft[:, offA:offA + iwA], hf_b[ciA % 2],
                                              hqi[:, i0A:i0A + iwA], hqi_b, scr2[0], iwA // 16)
                            if pieces and ci == cis[-1]:
                                (ciA, i0A, iwA, offA) = pieces.pop(0)
                                emit_quantize(bld, nc, hft[:, offA:offA + iwA], hf_b[ciA % 2],
                                              hqi[:, i0A:i0A + iwA], hqi_b, scr2[0], iwA // 16)
                    if grp == 0 and pend:
                        emit_down(pend.pop(0))
                bld.dma("qA" if ti % 2 == 0 else "qS",
                        lambda e, a=hqi, b=hqTi: e.dma_start(
                            b[:], a[:].rearrange("p (j q) -> p j q", q=128), transpose=True),
                        reads=[hqi_b], writes=[hqTi_b])
                pend.append((ti, hqTi, hqTi_b, xti, xti_b, xlni, xlni_b))
                if len(pend) > 1:
                    emit_down(pend.pop(0))
                if ti + 2 < NTT:
                    xload(ti + 2)
            while pend:
                emit_down(pend.pop(0))
            bld.flush_phase(tag)

        with contextlib.ExitStack() as wstack:
            wgqT = sb("wgqT", [128, DC, DI], BF16, wstack, is_transpose_dest=True)
            wuqT = sb("wuqT", [128, DC, DI], BF16, wstack, is_transpose_dest=True)
            wdqT = sb("wdqT", [128, IC, D], BF16, wstack, is_transpose_dest=True)
            wgqT_hb = [Buf("wg0"), Buf("wg1")]
            wuqT_hb = [Buf("wu0"), Buf("wu1")]
            wdqT_b = Buf("wdqT")
            with contextlib.ExitStack() as fstack:
                ffn_phase("f1", x_in, xs1, wgqT, wuqT, wdqT, (wgqT_hb, wuqT_hb, wdqT_b), fstack,
                          wprep=(wg_d, wu_d, wd_d, wgc, wuc, wdc))

        with contextlib.ExitStack() as astack:
            idf = sb("idf", [128, 128], F32, astack)
            idf_b = Buf()
            maskt = sb("maskt", [TT, TT], BF16, astack)
            maskt_b = Buf()
            bld.g(lambda e: e.memset(idf[:], 0.0), writes=[idf_b])
            bld.g(lambda e: e.affine_select(out=idf[:], in_=idf[:], compare_op=Alu.not_equal,
                                            fill=1.0, base=0, pattern=[[-1, 128]], channel_multiplier=1),
                  reads=[idf_b], writes=[idf_b])
            bld.g(lambda e: e.memset(maskt[:], 1.0), writes=[maskt_b])
            bld.g(lambda e: e.affine_select(out=maskt[:], in_=maskt[:], compare_op=Alu.is_ge,
                                            fill=0.0, base=0, pattern=[[1, TT]], channel_multiplier=-1),
                  reads=[maskt_b], writes=[maskt_b])

            QT = sb("QT", [128, 4, S], BF16, astack)
            QT_b = Buf("QT")
            KT = sb("KT", [128, 4, S], BF16, astack)
            KT_b = Buf("KT")
            VH = sb("VH", [TT, NTT, H, 132], BF16, astack)
            VH_b = Buf("VH")
            so = sb("so", [TT, NTT, D], BF16, astack)
            so_b = Buf("so")
            onw = sb("onw_sb", [128, D], F32, astack)
            onw_b = Buf()
            ecol_t = sb("ecol_t", [128, NTT, 128], F32, astack)
            ecol_t_b = Buf()
            erow_t = sb("erow_t", [128, 128], F32, astack)
            erow_t_b = Buf()
            emb_t = sb("emb_t", [128, 128], F32, astack)
            emb_t_b = Buf()
            bld.dma("qS", lambda e: e.dma_start(onw[:], onw_d.ap()), writes=[onw_b])
            bld.g(lambda e: e.memset(VH[:, :, :, 128:129], 1.0), writes=[VH_b])
            with contextlib.ExitStack() as a12stack:
                ifsb = sb("ifsb", [TT, NTT, 16], F32, a12stack)
                ifsb_b = Buf()
                ir_row = sb("ir_row", [8, S], F32, a12stack)
                ir_b = Buf()
                fr_row = sb("fr_row", [8, S], F32, a12stack)
                fr_b = Buf()
                bi_s = sb("bi_s", [8, 2], F32, a12stack)
                bi_b = Buf()
                bf_s = sb("bf_s", [8, 2], F32, a12stack)
                bf_b = Buf()
                bld.dma("qS", lambda e: e.dma_start(bi_s[:, 0:1], bif_d.ap()[0:8, :]), writes=[bi_b])
                bld.dma("qS", lambda e: e.dma_start(bf_s[:, 0:1], bif_d.ap()[8:16, :]), writes=[bf_b])
                bld.v(lambda e: e.tensor_single_scalar(bi_s[:, 1:2], bi_s[:, 0:1], 1.0 / CAP, Alu.mult),
                      reads=[bi_b], writes=[bi_b])
                bld.v(lambda e: e.tensor_single_scalar(bf_s[:, 1:2], bf_s[:, 0:1], 1.0 / CAP, Alu.mult),
                      reads=[bf_b], writes=[bf_b])

                with contextlib.ExitStack() as pstack:
                    wqs = sb("wqs", [128, DC, 512], BF16, pstack)
                    wqs_b = Buf()
                    wks = sb("wks", [128, DC, 512], BF16, pstack)
                    wks_b = Buf()
                    wvs = sb("wvs", [128, DC, D], BF16, pstack)
                    wvs_b = Buf()
                    wos = sb("wos", [128, DC, D], BF16, pstack)
                    wos_b = Buf()
                    wifs = sb("wifs", [128, DC, 16], BF16, pstack)
                    wifs_b = Buf()
                    xT2t = [sb(f"xT2t{i}", [128, DC, TT], BF16, pstack, is_transpose_dest=True) for i in range(2)]
                    xT2t_b = [Buf() for _ in range(2)]
                    xt = [sb(f"a1xt{i}", [TT, D], F32, pstack) for i in range(2)]
                    xt_b = [Buf() for _ in range(2)]
                    xln = [sb(f"a1xln{i}", [TT, D], BF16, pstack) for i in range(2)]
                    xln_b = [Buf() for _ in range(2)]
                    st_t = sb("a1st", [TT, 16], F32, pstack)
                    st_b = Buf()
                    ps_p = [psum(f"a1ps{i}", [128, 512], F32, pstack) for i in range(2)]
                    ps_p_b = [Buf() for _ in range(2)]
                    ps_v = [psum(f"a1psv{i}", [128, 512], F32, pstack) for i in range(2)]
                    ps_v_b = [Buf() for _ in range(2)]
                    ps_if = psum("a1psif", [128, 16], F32, pstack)
                    ps_if_b = Buf()
                    ps_t = psum("a1pst", [16, 128], F32, pstack)
                    ps_t_b = Buf()
                    bld.dma("qG", lambda e: e.dma_start(wqs[:], wq_d.ap().rearrange("(j p) q -> p j q", p=128)), writes=[wqs_b])
                    bld.dma("qG", lambda e: e.dma_start(wks[:], wk_d.ap().rearrange("(j p) q -> p j q", p=128)), writes=[wks_b])
                    bld.dma("qG", lambda e: e.dma_start(wvs[:], wv_d.ap().rearrange("(j p) q -> p j q", p=128)), writes=[wvs_b])
                    bld.dma("qG", lambda e: e.dma_start(wos[:], wo_d.ap().rearrange("(j p) q -> p j q", p=128)), writes=[wos_b])
                    bld.dma("qG", lambda e: e.dma_start(wifs[:], wif_d.ap().rearrange("(j p) q -> p j q", p=128)), writes=[wifs_b])
                    def a1_xside(ti):
                        xti, xti_b = xt[ti % 2], xt_b[ti % 2]
                        xlni, xlni_b = xln[ti % 2], xln_b[ti % 2]
                        xTt, xTt_b = xT2t[ti % 2], xT2t_b[ti % 2]
                        bld.dma("qG", lambda e, ti=ti, x=xti: e.dma_start(x[:], xs1.ap()[ti * TT:(ti + 1) * TT, :]),
                                writes=[xti_b])
                        bld.v(lambda e, x=xti, s=st_t: e.bn_stats(s[:, 0:6], x[:, 0:512]), reads=[xti_b], writes=[st_b])
                        bld.v(lambda e, x=xti, s=st_t: e.bn_stats(s[:, 6:12], x[:, 512:1024]), reads=[xti_b], writes=[st_b])
                        bld.v(lambda e, s=st_t: e.bn_aggr(s[:, 12:14], s[:, 0:12]), reads=[st_b], writes=[st_b])
                        bld.s(lambda e, s=st_t: e.activation(s[:, 14:15], s[:, 13:14], Act.Sqrt, bias=cst[:, 0:1]),
                              reads=[st_b, cst_b], writes=[st_b])
                        bld.v(lambda e, s=st_t: e.reciprocal(s[:, 15:16], s[:, 14:15]), reads=[st_b], writes=[st_b])
                        bld.v(lambda e, x=xti, o=xlni, s=st_t: e.tensor_scalar(o[:], x[:], s[:, 12:13], s[:, 15:16], Alu.subtract, Alu.mult),
                              reads=[xti_b, st_b], writes=[xlni_b])
                        bld.dma("qS" if ti % 2 == 0 else "qA",
                                lambda e, a=xlni, b=xTt: e.dma_start(
                                    b[:], a[:].rearrange("p (j q) -> p j q", q=128), transpose=True),
                                reads=[xlni_b], writes=[xTt_b])

                    a1_xside(0)
                    for ti in range(NTT):
                        xTt, xTt_b = xT2t[ti % 2], xT2t_b[ti % 2]
                        if ti + 1 < NTT:
                            a1_xside(ti + 1)
                        for gq in range(4):
                            pp, pp_b = ps_p[gq % 2], ps_p_b[gq % 2]
                            for j in range(DC):
                                bld.t(lambda e, j=j, gq=gq, pp=pp, a=xTt: e.matmul(
                                        pp[:, 0:TT], wqs[:, j, gq * 128:(gq + 1) * 128], a[:, j, :],
                                        start=(j == 0), stop=(j == DC - 1)),
                                      reads=[wqs_b, xTt_b], writes=[pp_b], track=(j == DC - 1))
                            bld.v(lambda e, gq=gq, ti=ti, pp=pp: e.tensor_copy(QT[:, gq, ti * TT:(ti + 1) * TT], pp[:, 0:TT]),
                                  reads=[pp_b], writes=[QT_b])
                            pp, pp_b = ps_p[(gq + 1) % 2], ps_p_b[(gq + 1) % 2]
                            for j in range(DC):
                                bld.t(lambda e, j=j, gq=gq, pp=pp, a=xTt: e.matmul(
                                        pp[:, 0:TT], wks[:, j, gq * 128:(gq + 1) * 128], a[:, j, :],
                                        start=(j == 0), stop=(j == DC - 1)),
                                      reads=[wks_b, xTt_b], writes=[pp_b], track=(j == DC - 1))
                            bld.v(lambda e, gq=gq, ti=ti, pp=pp: e.tensor_copy(KT[:, gq, ti * TT:(ti + 1) * TT], pp[:, 0:TT]),
                                  reads=[pp_b], writes=[KT_b])
                        for j in range(DC):
                            for half in range(2):
                                bld.t(lambda e, j=j, half=half, pp=ps_v[half], a=xTt: e.matmul(
                                        pp[:], a[:, j, :], wvs[:, j, half * 512:(half + 1) * 512],
                                        start=(j == 0), stop=(j == DC - 1)),
                                      reads=[wvs_b, xTt_b], writes=[ps_v_b[half]], track=(j == DC - 1))
                                bld.t(lambda e, j=j, half=half, pp=ps_p[half], a=xTt: e.matmul(
                                        pp[:], a[:, j, :], wos[:, j, half * 512:(half + 1) * 512],
                                        start=(j == 0), stop=(j == DC - 1)),
                                      reads=[wos_b, xTt_b], writes=[ps_p_b[half]], track=(j == DC - 1))
                            bld.t(lambda e, j=j, a=xTt: e.matmul(
                                    ps_if[:], a[:, j, :], wifs[:, j, :],
                                    start=(j == 0), stop=(j == DC - 1)),
                                  reads=[wifs_b, xTt_b], writes=[ps_if_b], track=(j == DC - 1))
                        for half in range(2):
                            bld.v(lambda e, ti=ti, half=half, pp=ps_v[half]: e.tensor_copy(
                                    VH[:, ti, half * 4:(half + 1) * 4, 0:128], pp[:].rearrange("p (h v) -> p h v", v=128)),
                                  reads=[ps_v_b[half]], writes=[VH_b])
                            bld.s(lambda e, ti=ti, half=half, pp=ps_p[half]: e.activation(
                                    so[:, ti, half * 512:(half + 1) * 512], pp[:], Act.Sigmoid),
                                  reads=[ps_p_b[half]], writes=[so_b])
                        bld.v(lambda e, ti=ti: e.tensor_copy(ifsb[:, ti, :], ps_if[:]),
                              reads=[ps_if_b], writes=[ifsb_b])
                        bld.t(lambda e, ti=ti: e.transpose(ps_t[0:8, :], ifsb[:, ti, 0:8], idf[:]),
                              reads=[ifsb_b, idf_b], writes=[ps_t_b])
                        bld.v(lambda e, ti=ti: e.tensor_copy(ir_row[:, ti * TT:(ti + 1) * TT], ps_t[0:8, :]),
                              reads=[ps_t_b], writes=[ir_b])
                        bld.t(lambda e, ti=ti: e.transpose(ps_t[0:8, :], ifsb[:, ti, 8:16], idf[:]),
                              reads=[ifsb_b, idf_b], writes=[ps_t_b])
                        bld.v(lambda e, ti=ti: e.tensor_copy(fr_row[:, ti * TT:(ti + 1) * TT], ps_t[0:8, :]),
                              reads=[ps_t_b], writes=[fr_b])
                    bld.flush_phase("a1")
                gr_row = sb("gr_row", [8, S], F32, a12stack)
                gr_b = Buf()
                tmp_row = sb("tmp_row", [8, S], F32, a12stack)
                tmp_b = Buf()
                lfcr = sb("lfcr", [8, S], F32, a12stack)
                lfcr_b = Buf()
                Mr = sb("Mr", [8, S], F32, a12stack)
                Mr_b = Buf()
                M_loc = sb("M_loc", [128, TT], F32, a12stack)
                M_loc_b = Buf()
                nm_loc = sb("nm_loc", [128, TT], F32, a12stack)
                nm_loc_b = Buf()
                g_rep = sb("g_rep", [128, S], F32, a12stack)
                g_rep_b = Buf()
                PL = sb("PL", [128, 2], F32, a12stack)
                PL_b = Buf()
                ps_tr = psum("a2pst", [128, 128], F32, a12stack)
                ps_tr_b = Buf()
                bld.s(lambda e: e.activation(ir_row[:], ir_row[:], Act.Tanh, bias=bi_s[:, 1:2], scale=1.0 / CAP),
                      reads=[ir_b, bi_b], writes=[ir_b])
                bld.v(lambda e: e.tensor_single_scalar(ir_row[:], ir_row[:], CAP, Alu.mult),
                      reads=[ir_b], writes=[ir_b])
                bld.s(lambda e: e.activation(fr_row[:], fr_row[:], Act.Tanh, bias=bf_s[:, 1:2], scale=1.0 / CAP),
                      reads=[fr_b, bf_b], writes=[fr_b])
                bld.v(lambda e: e.tensor_single_scalar(fr_row[:], fr_row[:], CAP, Alu.mult),
                      reads=[fr_b], writes=[fr_b])
                bld.s(lambda e: e.activation(tmp_row[:], fr_row[:], Act.Abs),
                      reads=[fr_b], writes=[tmp_b])
                bld.s(lambda e: e.activation(tmp_row[:], tmp_row[:], Act.Exp, scale=-1.0),
                      reads=[tmp_b], writes=[tmp_b])
                bld.s(lambda e: e.activation(tmp_row[:], tmp_row[:], Act.Ln, bias=1.0),
                      reads=[tmp_b], writes=[tmp_b])
                bld.v(lambda e: e.scalar_tensor_tensor(tmp_row[:], fr_row[:], 0.0, tmp_row[:], Alu.min, Alu.subtract),
                      reads=[fr_b, tmp_b], writes=[tmp_b])
                bld.v(lambda e: e.tensor_tensor_scan(lfcr[:], tmp_row[:], tmp_row[:], 0.0, Alu.add, Alu.bypass),
                      reads=[tmp_b], writes=[lfcr_b])
                bld.v(lambda e: e.tensor_sub(gr_row[:], ir_row[:], lfcr[:]),
                      reads=[ir_b, lfcr_b], writes=[gr_b])
                bld.v(lambda e: e.tensor_tensor_scan(Mr[:], gr_row[:], gr_row[:], -1e30, Alu.max, Alu.bypass),
                      reads=[gr_b], writes=[Mr_b])
                bld.v(lambda e: e.scalar_tensor_tensor(tmp_row[:], lfcr[:], -1.0, Mr[:], Alu.mult, Alu.subtract),
                      reads=[lfcr_b, Mr_b], writes=[tmp_b])
                for i in range(NTT):
                    bld.dma("qS", lambda e, i=i: e.dma_start(M_loc[i * 8:(i + 1) * 8, :], Mr[:, i * TT:(i + 1) * TT]),
                            reads=[Mr_b], writes=[M_loc_b])
                    bld.dma("qA", lambda e, i=i: e.dma_start(nm_loc[i * 8:(i + 1) * 8, :], tmp_row[:, i * TT:(i + 1) * TT]),
                            reads=[tmp_b], writes=[nm_loc_b])
                    bld.dma("qG", lambda e, i=i: e.dma_start(g_rep[i * 8:(i + 1) * 8, :], gr_row[:]),
                            reads=[gr_b], writes=[g_rep_b])
                bld.v(lambda e: e.tensor_copy(PL[:, 0:1], M_loc[:, TT - 1:TT]), reads=[M_loc_b], writes=[PL_b])
                bld.v(lambda e: e.tensor_single_scalar(PL[:, 1:2], PL[:, 0:1], LN8, Alu.add), reads=[PL_b], writes=[PL_b])
                bld.v(lambda e: e.tensor_scalar(g_rep[:], g_rep[:], PL[:, 1:2], 0.0, Alu.subtract, Alu.min),
                      reads=[g_rep_b, PL_b], writes=[g_rep_b])
                bld.s(lambda e: e.activation(g_rep[:], g_rep[:], Act.Exp), reads=[g_rep_b], writes=[g_rep_b])
                for tau in range(NTT):
                    bld.t(lambda e, tau=tau: e.transpose(ps_tr[:], g_rep[:, tau * TT:(tau + 1) * TT], idf[:]),
                          reads=[g_rep_b, idf_b], writes=[ps_tr_b])
                    bld.v(lambda e, tau=tau: e.tensor_copy(ecol_t[:, tau, :], ps_tr[:]),
                          reads=[ps_tr_b], writes=[ecol_t_b])
                bld.v(lambda e: e.tensor_scalar(M_loc[:], M_loc[:], PL[:, 0:1], -85.0, Alu.subtract, Alu.max),
                      reads=[M_loc_b, PL_b], writes=[M_loc_b])
                bld.s(lambda e: e.activation(M_loc[:], M_loc[:], Act.Exp, scale=-1.0), reads=[M_loc_b], writes=[M_loc_b])
                bld.t(lambda e: e.transpose(ps_tr[:], M_loc[:], idf[:]), reads=[M_loc_b, idf_b], writes=[ps_tr_b])
                bld.v(lambda e: e.tensor_copy(erow_t[:], ps_tr[:]), reads=[ps_tr_b], writes=[erow_t_b])
                bld.s(lambda e: e.activation(nm_loc[:], nm_loc[:], Act.Exp), reads=[nm_loc_b], writes=[nm_loc_b])
                bld.t(lambda e: e.transpose(ps_tr[:], nm_loc[:], idf[:]), reads=[nm_loc_b, idf_b], writes=[ps_tr_b])
                bld.v(lambda e: e.tensor_copy(emb_t[:], ps_tr[:]), reads=[ps_tr_b], writes=[emb_t_b])
                bld.flush_phase("a2")

            with contextlib.ExitStack() as astack3:
                wdns = sb("wdns", [128, DC, D], BF16, astack3)
                wdns_b = Buf()
                bld.dma("qG", lambda e: e.dma_start(wdns[:], wdn_d.ap().rearrange("(j p) q -> p j q", p=128)),
                        writes=[wdns_b])
                ct = [sb(f"ct{i}", [TT, 512], BF16, astack3) for i in range(2)]
                ct_b = [Buf() for _ in range(2)]
                acat = sb("acat", [TT, 4, H, 132], F32, astack3)
                acat_b = Buf()
                nsm = sb("nsm", [TT, 8 * 8], F32, astack3)
                nsm_b = Buf()
                sqt = sb("sqt", [TT, H, 128], F32, astack3)
                sqt_b = Buf()
                hub = sb("hub", [TT, H, 128], F32, astack3)
                hub_b = Buf()
                wsr = sb("wsr", [TT, D], F32, astack3)
                wsr_b = Buf()
                hg = [sb(f"hg{i}", [TT, D], BF16, astack3) for i in range(2)]
                hg_b = [Buf() for _ in range(2)]
                hgT = [sb(f"hgT{i}", [128, DC, TT], BF16, astack3, is_transpose_dest=True) for i in range(2)]
                hgT_b = [Buf() for _ in range(2)]
                x1t = [sb(f"x1t{i}", [TT, D], F32, astack3) for i in range(2)]
                x1t_b = [Buf() for _ in range(2)]
                x2t = [sb(f"x2t{i}", [TT, D], F32, astack3) for i in range(2)]
                x2t_b = [Buf() for _ in range(2)]
                ps_sc = [psum(f"pssc{i}", [TT, 512], F32, astack3) for i in range(2)]
                ps_sc_b = [Buf() for _ in range(2)]
                ps_a = [psum(f"psa{i}", [TT, 132], F32, astack3) for i in range(4)]
                ps_a_b = [Buf() for _ in range(4)]
                ps_o = [psum(f"pso{i}", [TT, 512], F32, astack3) for i in range(2)]
                ps_o_b = [Buf() for _ in range(2)]

                for i4 in range(4):
                    for h in range(H):
                        gq, sub = h // 2, (h % 2) * 64

                        def emit_scores(tau, i4=i4, gq=gq, sub=sub):
                            cb = tau % 2
                            psc, psc_b = ps_sc[cb], ps_sc_b[cb]
                            bld.t(lambda e, i4=i4, tau=tau, gq=gq, sub=sub, psc=psc: e.matmul(
                                    psc[:], KT[sub:sub + 64, gq, tau * TT:(tau + 1) * TT],
                                    QT[sub:sub + 64, gq, i4 * 512:(i4 + 1) * 512], start=True, stop=True),
                                  reads=[KT_b, QT_b], writes=[psc_b])

                        emit_scores(0)
                        for tau in range(4 * i4 + 4):
                            cb = tau % 2
                            psc, psc_b = ps_sc[cb], ps_sc_b[cb]
                            if tau + 1 < 4 * i4 + 4:
                                emit_scores(tau + 1)
                            for il in range(4):
                                i = 4 * i4 + il
                                if tau > i:
                                    continue
                                col = i * 8 + h
                                if tau == i:
                                    bld.v(lambda e, tau=tau, col=col, il=il, cbuf=ct[cb], psc=psc: e.scalar_tensor_tensor(
                                            cbuf[:, il * TT:(il + 1) * TT], psc[:, il * TT:(il + 1) * TT],
                                            ecol_t[:, tau, col:col + 1], maskt[:], Alu.mult, Alu.mult),
                                          reads=[psc_b, ecol_t_b, maskt_b], writes=[ct_b[cb]])
                                elif il % 2 == 0:
                                    bld.s(lambda e, tau=tau, col=col, il=il, cbuf=ct[cb], psc=psc: e.activation(
                                            cbuf[:, il * TT:(il + 1) * TT], psc[:, il * TT:(il + 1) * TT],
                                            Act.Copy, scale=ecol_t[:, tau, col:col + 1]),
                                          reads=[psc_b, ecol_t_b], writes=[ct_b[cb]])
                                else:
                                    bld.v(lambda e, tau=tau, col=col, il=il, cbuf=ct[cb], psc=psc: e.tensor_scalar(
                                            cbuf[:, il * TT:(il + 1) * TT], psc[:, il * TT:(il + 1) * TT],
                                            ecol_t[:, tau, col:col + 1], None, Alu.mult),
                                          reads=[psc_b, ecol_t_b], writes=[ct_b[cb]])
                                bld.t(lambda e, i=i, tau=tau, h=h, il=il, cbuf=ct[cb], pa=ps_a[il]: e.matmul(
                                        pa[:, 0:129], cbuf[:, il * TT:(il + 1) * TT], VH[:, tau, h, 0:129],
                                        start=(tau == 0), stop=(tau == i)),
                                      reads=[ct_b[cb], VH_b], writes=[ps_a_b[il]], track=(tau == i))
                        for il in range(4):
                            col = (4 * i4 + il) * 8 + h
                            bld.v(lambda e, h=h, il=il, col=col, pa=ps_a[il]: e.tensor_scalar(
                                    acat[:, il, h, 0:129], pa[:, 0:129], erow_t[:, col:col + 1], None, Alu.mult),
                                  reads=[ps_a_b[il], erow_t_b], writes=[acat_b])
                    for il in range(4):
                        i = 4 * i4 + il
                        bld.dma("qG", lambda e, i=i, x=x1t[i % 2]: e.dma_start(x[:], xs1.ap()[i * TT:(i + 1) * TT, :]),
                                writes=[x1t_b[i % 2]])
                        c0 = i * 8
                        rsv = acat[:, il, :, 128:129]
                        bld.v(lambda e, rsv=rsv: e.tensor_scalar(nsm[:, 0:8], rsv, -1.0, None, Alu.mult),
                              reads=[acat_b], writes=[nsm_b])
                        bld.v(lambda e, rsv=rsv: e.tensor_tensor(nsm[:, 0:8].rearrange("p (h o) -> p h o", o=1), rsv,
                                                        nsm[:, 0:8].rearrange("p (h o) -> p h o", o=1), Alu.max),
                              reads=[acat_b, nsm_b], writes=[nsm_b])
                        bld.v(lambda e, c0=c0: e.tensor_tensor(nsm[:, 0:8], nsm[:, 0:8], emb_t[:, c0:c0 + 8], Alu.max),
                              reads=[nsm_b, emb_t_b], writes=[nsm_b])
                        bld.v(lambda e: e.tensor_single_scalar(nsm[:, 0:8], nsm[:, 0:8], ATT_EPS, Alu.add),
                              reads=[nsm_b], writes=[nsm_b])
                        bld.v(lambda e: e.reciprocal(nsm[:, 8:16], nsm[:, 0:8]), reads=[nsm_b], writes=[nsm_b])
                        av = acat[:, il, :, 0:128]
                        bld.v(lambda e, av=av: e.tensor_reduce(nsm[:, 16:24].rearrange("p (h o) -> p h o", o=1), av, Ax.X, Alu.add),
                              reads=[acat_b], writes=[nsm_b])
                        bld.s(lambda e, av=av: e.activation(sqt[:], av, Act.Square), reads=[acat_b], writes=[sqt_b])
                        bld.v(lambda e: e.tensor_reduce(nsm[:, 24:32].rearrange("p (h o) -> p h o", o=1), sqt[:], Ax.X, Alu.add),
                              reads=[sqt_b], writes=[nsm_b])
                        bld.v(lambda e: e.tensor_single_scalar(nsm[:, 16:24], nsm[:, 16:24], 1.0 / 128.0, Alu.mult),
                              reads=[nsm_b], writes=[nsm_b])
                        bld.v(lambda e: e.tensor_single_scalar(nsm[:, 24:32], nsm[:, 24:32], 1.0 / 128.0, Alu.mult),
                              reads=[nsm_b], writes=[nsm_b])
                        bld.v(lambda e: e.tensor_tensor(nsm[:, 32:40], nsm[:, 16:24], nsm[:, 16:24], Alu.mult),
                              reads=[nsm_b], writes=[nsm_b])
                        bld.v(lambda e: e.tensor_tensor(nsm[:, 24:32], nsm[:, 24:32], nsm[:, 32:40], Alu.subtract),
                              reads=[nsm_b], writes=[nsm_b])
                        bld.v(lambda e: e.tensor_tensor(nsm[:, 32:40], nsm[:, 8:16], nsm[:, 8:16], Alu.mult),
                              reads=[nsm_b], writes=[nsm_b])
                        bld.v(lambda e: e.tensor_tensor(nsm[:, 24:32], nsm[:, 24:32], nsm[:, 32:40], Alu.mult),
                              reads=[nsm_b], writes=[nsm_b])
                        bld.s(lambda e: e.activation(nsm[:, 32:40], nsm[:, 24:32], Act.Sqrt, bias=cst[:, 1:2]),
                              reads=[nsm_b, cst_b], writes=[nsm_b])
                        bld.v(lambda e: e.reciprocal(nsm[:, 40:48], nsm[:, 32:40]), reads=[nsm_b], writes=[nsm_b])
                        bld.v(lambda e: e.tensor_tensor(nsm[:, 40:48], nsm[:, 40:48], nsm[:, 8:16], Alu.mult),
                              reads=[nsm_b], writes=[nsm_b])
                        mu3 = nsm[:, 16:24].rearrange("p (h o) -> p h o", o=1)
                        bld.v(lambda e, av=av: e.tensor_tensor(hub[:], av, bcast(av, mu3), Alu.subtract),
                              reads=[acat_b, nsm_b], writes=[hub_b])
                        w3 = wsr[:].rearrange("p (h v) -> p h v", v=128)
                        rr3 = nsm[:, 40:48].rearrange("p (h o) -> p h o", o=1)
                        bld.g(lambda e, i=i: e.tensor_tensor(wsr[:], onw[:], so[:, i, :], Alu.mult),
                              reads=[onw_b, so_b], writes=[wsr_b])
                        bld.g(lambda e: e.tensor_tensor(w3, w3, bcast(w3, rr3), Alu.mult),
                              reads=[wsr_b, nsm_b], writes=[wsr_b])
                        hgi, hgi_b = hg[i % 2], hg_b[i % 2]
                        bld.v(lambda e, o=hgi: e.tensor_tensor(o[:].rearrange("p (h v) -> p h v", v=128), hub[:], w3, Alu.mult),
                              reads=[hub_b, wsr_b], writes=[hgi_b])
                        hgTi, hgTi_b = hgT[i % 2], hgT_b[i % 2]
                        bld.dma("qS" if i % 2 == 0 else "qA",
                                lambda e, a=hgi, b=hgTi: e.dma_start(
                                    b[:], a[:].rearrange("p (j q) -> p j q", q=128), transpose=True),
                                reads=[hgi_b], writes=[hgTi_b])
                        for j in range(DC):
                            for half in range(2):
                                po, po_b = ps_o[half], ps_o_b[half]
                                bld.t(lambda e, j=j, half=half, a=hgTi, po=po: e.matmul(
                                        po[:], a[:, j, :], wdns[:, j, half * 512:(half + 1) * 512],
                                        start=(j == 0), stop=(j == DC - 1)),
                                      reads=[hgTi_b, wdns_b], writes=[po_b], track=(j == DC - 1))
                        x2i, x2i_b = x2t[i % 2], x2t_b[i % 2]
                        for half in range(2):
                            po, po_b = ps_o[half], ps_o_b[half]
                            bld.v(lambda e, half=half, x=x1t[i % 2], o=x2i, po=po: e.tensor_tensor(
                                    o[:, half * 512:(half + 1) * 512], po[:], x[:, half * 512:(half + 1) * 512], Alu.add),
                                  reads=[po_b, x1t_b[i % 2]], writes=[x2i_b])
                        bld.dma("qS", lambda e, i=i, o=x2i: e.dma_start(xs2.ap()[i * TT:(i + 1) * TT, :], o[:]),
                                reads=[x2i_b])
                bld.flush_phase("a3")

        with contextlib.ExitStack() as w2stack:
            wgqT2 = sb("wgqT2", [128, DC, DI], BF16, w2stack)
            wuqT2 = sb("wuqT2", [128, DC, DI], BF16, w2stack)
            wdqT2 = sb("wdqT2", [128, IC, D], BF16, w2stack)
            wgqT2_hb = [Buf(), Buf()]
            wuqT2_hb = [Buf(), Buf()]
            wdqT2_b = Buf()
            bld.dma("qS", lambda e: e.dma_start(wgqT2[:], wgc.ap().rearrange("(j p) i -> p j i", p=128)),
                    writes=[wgqT2_hb[0], wgqT2_hb[1]])
            bld.dma("qA", lambda e: e.dma_start(wuqT2[:], wuc.ap().rearrange("(j p) i -> p j i", p=128)),
                    writes=[wuqT2_hb[0], wuqT2_hb[1]])
            bld.dma("qS", lambda e: e.dma_start(wdqT2[:], wdc.ap().rearrange("(j p) i -> p j i", p=128)), writes=[wdqT2_b])
            with contextlib.ExitStack() as fstack2:
                ffn_phase("f2", xs2, out_d, wgqT2, wuqT2, wdqT2, (wgqT2_hb, wuqT2_hb, wdqT2_b), fstack2)

    return nc


def kernel(L0_attn_x, L0_attn_Wq, L0_attn_Wk, L0_attn_Wv, L0_attn_Wo, L0_attn_Wi,
           L0_attn_bi, L0_attn_Wf, L0_attn_bf, L0_attn_outnorm_w, L0_attn_Wdown,
           L0_ffn_w_gate, L0_ffn_w_up, L0_ffn_w_down):
    f32 = lambda a: np.ascontiguousarray(np.asarray(a), dtype=np.float32)
    x = f32(L0_attn_x)
    wif = np.ascontiguousarray(np.concatenate([f32(L0_attn_Wi), f32(L0_attn_Wf)], axis=1))
    bif = np.ascontiguousarray(np.concatenate([f32(L0_attn_bi), f32(L0_attn_bf)]).reshape(16, 1))
    onw = np.ascontiguousarray(np.broadcast_to(f32(L0_attn_outnorm_w).reshape(1, D), (128, D)))
    base = {
        "wq": f32(L0_attn_Wq), "wk": f32(L0_attn_Wk), "wv": f32(L0_attn_Wv),
        "wo": f32(L0_attn_Wo), "wif": wif, "bif": bif, "onw": onw,
        "wdn": f32(L0_attn_Wdown), "wg": f32(L0_ffn_w_gate), "wu": f32(L0_ffn_w_up),
        "wd": f32(L0_ffn_w_down),
    }
    in_maps = [dict(base, x=np.ascontiguousarray(x[c])) for c in range(8)]
    nc = build_core_program()
    res = run_bass_kernel_spmd(nc, in_maps, core_ids=list(range(8)))
    return np.stack([np.asarray(r["out"], dtype=np.float32) for r in res.results], axis=0)


# version 19 (final)
# speedup vs baseline: 1.1919x; 1.0287x over previous
import contextlib
import numpy as np

import concourse.bass as bass
import concourse.mybir as mybir
from concourse.bass_utils import run_bass_kernel_spmd

F32 = mybir.dt.float32
BF16 = mybir.dt.bfloat16
FP16 = mybir.dt.float16
Alu = mybir.AluOpType
Act = mybir.ActivationFunctionType
Ax = mybir.AxisListType

ENGINES = ("tensor", "vector", "scalar", "gpsimd", "sync")
QUEUES = {"qS": "sync", "qA": "scalar", "qG": "gpsimd"}
DMA_POOL = 10

S, D, DI, H = 2048, 1024, 2816, 8
TT = 128
NTT = S // TT
DC = D // 128
IC = DI // 128
DHQK, DHV = 64, 128
LN_EPS = 1e-5
ATT_EPS = 1e-6
CAP = 15.0
LN8 = float(np.log(8.0))


class Buf:
    __slots__ = ("w", "rs", "name")

    def __init__(self, name=""):
        self.w = None
        self.rs = []
        self.name = name


class B:

    def __init__(self, nc, stack):
        self.nc = nc
        self.stack = stack
        self.sems = {}
        self.counts = {}
        self.dma_sems = {}
        self.dma_rr = {}
        for name in ("T", "V", "Sx", "G"):
            self._mk(name)
        for q in QUEUES:
            self.dma_sems[q] = []
            self.dma_rr[q] = 0
            for k in range(DMA_POOL):
                n = f"{q}_{k}"
                self._mk(n)
                self.dma_sems[q].append(n)
        self.engine_sem = {"tensor": "T", "vector": "V", "scalar": "Sx", "gpsimd": "G"}
        self.streams = {e: [] for e in ENGINES}
        self.is_dma_sem = {s for q in QUEUES for s in self.dma_sems[q]}

    def _mk(self, name):
        self.sems[name] = self.stack.enter_context(self.nc.semaphore(name))
        self.counts[name] = 0

    def _fix(self, tok):
        s, v = tok
        if s in self.is_dma_sem:
            return (s, self.counts[s])
        return tok

    def op(self, eng, fn, reads=(), writes=(), extra=(), sem=None, amt=1, track=True):
        waits = []
        for b in reads:
            if b.w is not None:
                waits.append(self._fix(b.w))
        for b in writes:
            for t_ in b.rs:
                waits.append(self._fix(t_))
            if b.w is not None:
                waits.append(self._fix(b.w))
        waits.extend(self._fix(t_) for t_ in extra)
        tok = None
        if track:
            s = sem if sem is not None else self.engine_sem[eng]
            self.counts[s] += amt
            tok = (s, self.counts[s])
        self.streams[eng].append((fn, tuple(waits), (tok[0], amt) if tok else None))
        if tok:
            for b in reads:
                b.rs.append(tok)
            for b in writes:
                b.w = tok
                b.rs = []
        return tok

    def t(self, fn, reads=(), writes=(), extra=(), track=True):
        return self.op("tensor", fn, reads, writes, extra, track=track)

    def v(self, fn, reads=(), writes=(), extra=()):
        return self.op("vector", fn, reads, writes, extra)

    def s(self, fn, reads=(), writes=(), extra=()):
        return self.op("scalar", fn, reads, writes, extra)

    def g(self, fn, reads=(), writes=(), extra=()):
        return self.op("gpsimd", fn, reads, writes, extra)

    def vg(self, k, fn, reads=(), writes=(), extra=()):
        return self.v(fn, reads, writes, extra)

    def dma(self, queue, fn, reads=(), writes=(), extra=()):
        eng = QUEUES[queue]
        idx = self.dma_rr[queue]
        self.dma_rr[queue] = (idx + 1) % DMA_POOL
        sem = self.dma_sems[queue][idx]
        return self.op(eng, fn, reads, writes, extra, sem=sem, amt=16)

    def flush_phase(self, name=None):
        for q, eng in QUEUES.items():
            toks = [(s, self.counts[s]) for s in self.dma_sems[q] if self.counts[s] > 0]
            if toks:
                self.streams[eng].append((None, tuple(toks), None))
        streams = {e: list(s) for e, s in self.streams.items()}
        self.streams = {e: [] for e in ENGINES}
        nonempty = [e for e in ENGINES if streams[e]]
        if not nonempty:
            return
        with self.nc.Block(name) as block:
            for eng in nonempty:
                def emitter(e, _eng=eng):
                    lw = {}
                    for fn, waits, inc in streams[_eng]:
                        for (sname, v_) in waits:
                            if lw.get(sname, 0) < v_:
                                e.wait_ge(self.sems[sname], v_)
                                lw[sname] = v_
                        if fn is None:
                            continue
                        inst = fn(e)
                        if inc is not None:
                            inst.then_inc(self.sems[inc[0]], inc[1])
                getattr(block, eng)(emitter)


def bcast(full_view, small_view):
    _, b2 = bass.broadcast_tensor_aps(full_view, small_view)
    return b2


def emit_quantize(bld, nc, x_ap, x_buf, out_ap, out_buf, scr, nblk, fin_v=False):
    bm, bm_b = scr["bm"], scr["bm_b"]
    rs, rs_b = scr["rs"], scr["rs_b"]
    xn, xn_b = scr["xn"], scr["xn_b"]
    axn, axn_b = scr["axn"], scr["axn_b"]
    nA, nA_b = scr["nA"], scr["nA_b"]
    nB, nB_b = scr["nB"], scr["nB_b"]
    n = nblk * 16
    xreads = x_buf if isinstance(x_buf, list) else [x_buf]
    x3 = x_ap.rearrange("p (b s) -> p b s", s=16)
    bm3 = bm[:, 0:nblk].rearrange("p (b o) -> p b o", o=1)
    bld.v(lambda e: e.tensor_reduce(bm3, x3, Ax.X, Alu.max, apply_absolute_value=True),
          reads=xreads, writes=[bm_b])
    bld.v(lambda e: e.tensor_scalar(bm[:, 0:nblk], bm[:, 0:nblk], 1.0 / 6.0, 1e-12, Alu.mult, Alu.max),
          reads=[bm_b], writes=[bm_b])
    bld.v(lambda e: e.reciprocal(rs[:, 0:nblk], bm[:, 0:nblk]), reads=[bm_b], writes=[rs_b])
    xn3 = xn[:, 0:n].rearrange("p (b s) -> p b s", s=16)
    rs3 = rs[:, 0:nblk].rearrange("p (b o) -> p b o", o=1)
    bld.v(lambda e: e.tensor_tensor(xn3, x3, bcast(x3, rs3), Alu.mult),
          reads=xreads + [rs_b], writes=[xn_b])
    bld.s(lambda e: e.activation(axn[:, 0:n], xn[:, 0:n], Act.Square),
          reads=[xn_b], writes=[axn_b])
    bld.s(lambda e: e.sign(xn[:, 0:n], xn[:, 0:n]), reads=[xn_b], writes=[xn_b])
    bld.vg(0, lambda e: e.tensor_single_scalar(nA[:, 0:n], axn[:, 0:n], 0.0625, Alu.is_gt),
           reads=[axn_b], writes=[nA_b])
    for k, c in enumerate((0.5625, 1.5625, 3.0625)):
        bld.vg(k + 1, lambda e, c=c: e.scalar_tensor_tensor(nA[:, 0:n], axn[:, 0:n], c, nA[:, 0:n], Alu.is_gt, Alu.add),
               reads=[axn_b, nA_b], writes=[nA_b])
    bld.vg(0, lambda e: e.tensor_single_scalar(nB[:, 0:n], axn[:, 0:n], 6.25, Alu.is_gt),
           reads=[axn_b], writes=[nB_b])
    for k, c in enumerate((12.25, 25.0, 25.0)):
        bld.vg(k + 1, lambda e, c=c: e.scalar_tensor_tensor(nB[:, 0:n], axn[:, 0:n], c, nB[:, 0:n], Alu.is_gt, Alu.add),
               reads=[axn_b, nB_b], writes=[nB_b])
    bld.vg(0, lambda e: e.scalar_tensor_tensor(nA[:, 0:n], nA[:, 0:n], 0.5, nB[:, 0:n], Alu.mult, Alu.add),
           reads=[nA_b, nB_b], writes=[nA_b])
    fin = bld.v if fin_v else bld.g
    fin(lambda e: e.tensor_tensor(nA[:, 0:n], nA[:, 0:n], xn[:, 0:n], Alu.mult),
        reads=[nA_b, xn_b], writes=[nA_b])
    o3 = out_ap.rearrange("p (b s) -> p b s", s=16)
    nA3 = nA[:, 0:n].rearrange("p (b s) -> p b s", s=16)
    fin(lambda e: e.tensor_tensor(o3, nA3, bcast(nA3, bm3), Alu.mult),
        reads=[nA_b, bm_b], writes=[out_buf])


def build_core_program():
    nc = bass.Bass("TRN2", target_bir_lowering=False)
    x_in = nc.dram_tensor("x", [S, D], F32, kind="ExternalInput")
    wq_d = nc.dram_tensor("wq", [D, 512], F32, kind="ExternalInput")
    wk_d = nc.dram_tensor("wk", [D, 512], F32, kind="ExternalInput")
    wv_d = nc.dram_tensor("wv", [D, D], F32, kind="ExternalInput")
    wo_d = nc.dram_tensor("wo", [D, D], F32, kind="ExternalInput")
    wif_d = nc.dram_tensor("wif", [D, 16], F32, kind="ExternalInput")
    bif_d = nc.dram_tensor("bif", [16, 1], F32, kind="ExternalInput")
    onw_d = nc.dram_tensor("onw", [128, D], F32, kind="ExternalInput")
    wdn_d = nc.dram_tensor("wdn", [D, D], F32, kind="ExternalInput")
    wg_d = nc.dram_tensor("wg", [DI, D], F32, kind="ExternalInput")
    wu_d = nc.dram_tensor("wu", [DI, D], F32, kind="ExternalInput")
    wd_d = nc.dram_tensor("wd", [D, DI], F32, kind="ExternalInput")
    out_d = nc.dram_tensor("out", [S, D], F32, kind="ExternalOutput")
    xs1 = nc.dram_tensor("xs1", [S, D], F32)
    xs2 = nc.dram_tensor("xs2", [S, D], F32)
    wgc = nc.dram_tensor("wgc", [D, DI], BF16)
    wuc = nc.dram_tensor("wuc", [D, DI], BF16)
    wdc = nc.dram_tensor("wdc", [DI, D], BF16)

    with contextlib.ExitStack() as gstack:
        bld = B(nc, gstack)
        sb = lambda name, shape, dt, st=gstack, **kw: st.enter_context(nc.sbuf_tensor(name, shape, dt, **kw))
        psum = lambda name, shape, dt, st=gstack: st.enter_context(nc.psum_tensor(name, shape, dt))

        cst = sb("cst", [128, 2], F32)
        cst_b = Buf("cst")
        bld.g(lambda e: e.memset(cst[:, 0:1], LN_EPS), writes=[cst_b])
        bld.g(lambda e: e.memset(cst[:, 1:2], ATT_EPS), writes=[cst_b])

        def ffn_phase(tag, x_src, x_dst, wgqT, wuqT, wdqT, w_bufs, stack, wprep=None):
            wg_hb, wu_hb, wd_b = w_bufs
            xt = [sb(f"{tag}xt{i}", [TT, D], F32, stack) for i in range(3)]
            xt_b = [Buf() for _ in range(3)]
            xln = [sb(f"{tag}xln{i}", [TT, D], F32, stack) for i in range(1)]
            xln_b = [Buf() for _ in range(1)]
            xq = [sb(f"{tag}xq{i}", [TT, D], BF16, stack) for i in range(2)]
            xq_b = [Buf() for _ in range(2)]
            xqT = [sb(f"{tag}xqT{i}", [128, DC, TT], BF16, stack, is_transpose_dest=True) for i in range(2)]
            xqT_hb = [[Buf(), Buf()] for _ in range(2)]
            hq = [sb(f"{tag}hq{i}", [TT, DI], BF16, stack) for i in range(2)]
            hq_b = [Buf() for _ in range(2)]
            hqT = [sb(f"{tag}hqT{i}", [128, IC, TT], BF16, stack, is_transpose_dest=True) for i in range(2)]
            hqT_b = [Buf() for _ in range(2)]
            hft = sb(f"{tag}hf", [TT, 1024], F32, stack)
            hf_b = [Buf() for _ in range(2)]
            st_t = sb(f"{tag}st", [TT, 16], F32, stack)
            st_b = Buf()
            scr2 = []
            for si, wd_ in ((0, 1024), (1, 1024)):
                scr2.append({
                    "bm": sb(f"{tag}bm{si}", [TT, wd_ // 16], F32, stack), "bm_b": Buf(),
                    "rs": sb(f"{tag}rs{si}", [TT, wd_ // 16], F32, stack), "rs_b": Buf(),
                    "xn": sb(f"{tag}xn{si}", [TT, wd_], F32, stack), "xn_b": Buf(),
                    "axn": sb(f"{tag}axn{si}", [TT, wd_], F32, stack), "axn_b": Buf(),
                    "nA": sb(f"{tag}nA{si}", [TT, wd_], BF16, stack), "nA_b": Buf(),
                    "nB": sb(f"{tag}nB{si}", [TT, wd_], BF16, stack), "nB_b": Buf(),
                })
            ps_g = [psum(f"{tag}psg{k}", [TT, 512], F32, stack) for k in range(3)]
            ps_g_b = [Buf() for _ in range(3)]
            ps_u = [psum(f"{tag}psu{k}", [TT, 512], F32, stack) for k in range(3)]
            ps_u_b = [Buf() for _ in range(3)]
            ps_d = [psum(f"{tag}psd{i}", [TT, 512], F32, stack) for i in range(2)]
            ps_d_b = [Buf() for _ in range(2)]

            ichunks = [(0, 512), (512, 512), (1024, 512), (1536, 512), (2048, 512), (2560, 256)]
            pend = []

            def emit_down(item):
                dti, dhqT, dhqT_b, dxt, dxt_b, dxn, dxn_b = item
                for j in range(IC):
                    for half in range(2):
                        pd, pd_b = ps_d[half], ps_d_b[half]
                        bld.t(lambda e, j=j, half=half, a=dhqT, w=wdqT, pd=pd: e.matmul(
                                pd[:], a[:, j, :], w[:, j, half * 512:(half + 1) * 512],
                                start=(j == 0), stop=(j == IC - 1)),
                              reads=[dhqT_b, wd_b], writes=[pd_b], track=(j == IC - 1))
                for half in range(2):
                    pd, pd_b = ps_d[half], ps_d_b[half]
                    bld.v(lambda e, half=half, x=dxt, pd=pd: e.scalar_tensor_tensor(
                            x[:, half * 512:(half + 1) * 512], pd[:], 0.5, x[:, half * 512:(half + 1) * 512],
                            Alu.mult, Alu.add),
                          reads=[pd_b], writes=[dxt_b])
                bld.dma("qS", lambda e, dti=dti, o=dxt: e.dma_start(x_dst.ap()[dti * TT:(dti + 1) * TT, :], o[:]),
                        reads=[dxt_b])

            def xload(ti):
                xti, xti_b = xt[ti % 3], xt_b[ti % 3]
                bld.dma("qG", lambda e, ti=ti, x=xti: e.dma_start(x[:], x_src.ap()[ti * TT:(ti + 1) * TT, :]),
                        writes=[xti_b])

            def xside(ti):
                xti, xti_b = xt[ti % 3], xt_b[ti % 3]
                xlni, xlni_b = xln[0], xln_b[0]
                xqi, xqi_b = xq[ti % 2], xq_b[ti % 2]
                xqTi, xqTi_hb = xqT[ti % 2], xqT_hb[ti % 2]
                bld.v(lambda e, x=xti, s=st_t: e.bn_stats(s[:, 0:6], x[:, 0:512]),
                      reads=[xti_b], writes=[st_b])
                bld.v(lambda e, x=xti, s=st_t: e.bn_stats(s[:, 6:12], x[:, 512:1024]),
                      reads=[xti_b], writes=[st_b])
                bld.v(lambda e, s=st_t: e.bn_aggr(s[:, 12:14], s[:, 0:12]),
                      reads=[st_b], writes=[st_b])
                bld.s(lambda e, s=st_t: e.activation(s[:, 14:15], s[:, 13:14], Act.Sqrt, bias=cst[:, 0:1]),
                      reads=[st_b, cst_b], writes=[st_b])
                bld.v(lambda e, s=st_t: e.reciprocal(s[:, 15:16], s[:, 14:15]),
                      reads=[st_b], writes=[st_b])
                bld.v(lambda e, x=xti, o=xlni, s=st_t: e.tensor_scalar(o[:], x[:], s[:, 12:13], s[:, 15:16], Alu.subtract, Alu.mult),
                      reads=[xti_b, st_b], writes=[xlni_b])
                for hh in range(2):
                    emit_quantize(bld, nc, xlni[:, hh * 512:(hh + 1) * 512], xlni_b,
                                  xqi[:, hh * 512:(hh + 1) * 512], xqi_b, scr2[hh], 32)
                    bld.dma("qS" if (ti + hh) % 2 == 0 else "qA",
                            lambda e, hh=hh, a=xqi, b=xqTi: e.dma_start(
                                b[:, hh * 4:(hh + 1) * 4, :],
                                a[:, hh * 512:(hh + 1) * 512].rearrange("p (j q) -> p j q", q=128), transpose=True),
                            reads=[xqi_b], writes=[xqT_hb[ti % 2][hh]])

            if wprep is not None:
                wg_d_, wu_d_, wd_d_, wgc_, wuc_, wdc_ = wprep
                wload, wload_b = [xln[0], xt[2]], [xln_b[0], xt_b[2]]
                wqp, wqp_b = [xq[0], xq[1]], [xq_b[0], xq_b[1]]
                for half, (c0, c1) in enumerate(((0, 12), (12, IC))):
                    for wsrc, wdstT, whb in ((wg_d_, wgqT, wg_hb), (wu_d_, wuqT, wu_hb)):
                        for c in range(c0, c1):
                            wc, wc_b = wload[c % 2], wload_b[c % 2]
                            wq_, wq_b_ = wqp[c % 2], wqp_b[c % 2]
                            bld.dma("qG", lambda e, c=c, wsrc=wsrc, wc=wc: e.dma_start(
                                    wc[:], wsrc.ap()[c * 128:(c + 1) * 128, :]), writes=[wc_b])
                            emit_quantize(bld, nc, wc[:], wc_b, wq_[:], wq_b_, scr2[c % 2], 64, fin_v=True)
                            bld.dma("qS" if c % 2 == 0 else "qA",
                                    lambda e, c=c, a=wq_, b=wdstT: e.dma_start(
                                        b[:, :, c * 128:(c + 1) * 128],
                                        a[:].rearrange("p (j q) -> p j q", q=128), transpose=True),
                                    reads=[wq_b_], writes=[whb[half]])
                bld.dma("qS", lambda e: e.dma_start(wgc_.ap().rearrange("(j p) i -> p j i", p=128), wgqT[:]),
                        reads=[wg_hb[0], wg_hb[1]])
                bld.dma("qA", lambda e: e.dma_start(wuc_.ap().rearrange("(j p) i -> p j i", p=128), wuqT[:]),
                        reads=[wu_hb[0], wu_hb[1]])
                for c in range(DC):
                    for sl, (wi0, wiw) in enumerate(((0, 1024), (1024, 1024), (2048, 768))):
                        wc, wc_b = wload[sl % 2], wload_b[sl % 2]
                        wq_, wq_b_ = wqp[sl % 2], wqp_b[sl % 2]
                        bld.dma("qG", lambda e, c=c, wi0=wi0, wiw=wiw, wc=wc: e.dma_start(
                                wc[:, 0:wiw], wd_d_.ap()[c * 128:(c + 1) * 128, wi0:wi0 + wiw]), writes=[wc_b])
                        emit_quantize(bld, nc, wc[:, 0:wiw], wc_b, wq_[:, 0:wiw], wq_b_, scr2[sl % 2], wiw // 16, fin_v=True)
                        bld.dma("qS" if sl % 2 == 0 else "qA",
                                lambda e, c=c, wi0=wi0, wiw=wiw, a=wq_: e.dma_start(
                                    wdqT[:, wi0 // 128:(wi0 + wiw) // 128, c * 128:(c + 1) * 128],
                                    a[:, 0:wiw].rearrange("p (j q) -> p j q", q=128), transpose=True),
                                reads=[wq_b_], writes=[wd_b])
                bld.dma("qS", lambda e: e.dma_start(wdc_.ap().rearrange("(j p) i -> p j i", p=128), wdqT[:]),
                        reads=[wd_b])

            xload(0)
            xload(1)
            xside(0)
            for ti in range(NTT):
                xti, xti_b = xt[ti % 3], xt_b[ti % 3]
                xlni, xlni_b = xln[0], xln_b[0]
                xqTi, xqTi_hb = xqT[ti % 2], xqT_hb[ti % 2]
                hqi, hqi_b = hq[ti % 2], hq_b[ti % 2]
                hqTi, hqTi_b = hqT[ti % 2], hqT_b[ti % 2]
                if ti + 1 < NTT:
                    xside(ti + 1)
                for grp in range(2):
                    cis = [grp * 3, grp * 3 + 1, grp * 3 + 2]
                    for j in range(DC):
                        for k, ci in enumerate(cis):
                            i0, iw = ichunks[ci]
                            bld.t(lambda e, j=j, i0=i0, iw=iw, k=k, a=xqTi, w=wgqT: e.matmul(
                                    ps_g[k][:, 0:iw], a[:, j, :], w[:, j, i0:i0 + iw],
                                    start=(j == 0), stop=(j == DC - 1)),
                                  reads=[xqTi_hb[j // 4], wg_hb[grp]], writes=[ps_g_b[k]], track=(j == DC - 1))
                            bld.t(lambda e, j=j, i0=i0, iw=iw, k=k, a=xqTi, w=wuqT: e.matmul(
                                    ps_u[k][:, 0:iw], a[:, j, :], w[:, j, i0:i0 + iw],
                                    start=(j == 0), stop=(j == DC - 1)),
                                  reads=[xqT_hb[ti % 2][j // 4], wu_hb[grp]], writes=[ps_u_b[k]], track=(j == DC - 1))
                    pieces = []
                    for k, ci in enumerate(cis):
                        i0, iw = ichunks[ci]
                        off = (ci % 2) * 512
                        hfc_b = hf_b[ci % 2]
                        bld.s(lambda e, iw=iw, k=k, off=off: e.activation(hft[:, off:off + iw], ps_g[k][:, 0:iw], Act.Silu),
                              reads=[ps_g_b[k]], writes=[hfc_b])
                        bld.v(lambda e, iw=iw, k=k, off=off: e.tensor_tensor(hft[:, off:off + iw], hft[:, off:off + iw], ps_u[k][:, 0:iw], Alu.mult),
                              reads=[hfc_b, ps_u_b[k]], writes=[hfc_b])
                        pieces.append((ci, i0, iw, off))
                        if len(pieces) == 2 or ci == cis[-1]:
                            if len(pieces) == 2 and pieces[0][3] == 0:
                                (ciA, i0A, iwA, _), (_, _, iwB, _) = pieces
                                emit_quantize(bld, nc, hft[:, 0:iwA + iwB], [hf_b[0], hf_b[1]],
                                              hqi[:, i0A:i0A + iwA + iwB], hqi_b, scr2[grp], (iwA + iwB) // 16)
                                pieces = []
                            elif len(pieces) == 2:
                                (ciA, i0A, iwA, offA) = pieces.pop(0)
                                emit_quantize(bld, nc, hft[:, offA:offA + iwA], hf_b[ciA % 2],
                                              hqi[:, i0A:i0A + iwA], hqi_b, scr2[1 - grp], iwA // 16)
                            if pieces and ci == cis[-1]:
                                (ciA, i0A, iwA, offA) = pieces.pop(0)
                                emit_quantize(bld, nc, hft[:, offA:offA + iwA], hf_b[ciA % 2],
                                              hqi[:, i0A:i0A + iwA], hqi_b, scr2[1 - grp], iwA // 16)
                    if grp == 0 and pend:
                        emit_down(pend.pop(0))
                bld.dma("qA" if ti % 2 == 0 else "qS",
                        lambda e, a=hqi, b=hqTi: e.dma_start(
                            b[:], a[:].rearrange("p (j q) -> p j q", q=128), transpose=True),
                        reads=[hqi_b], writes=[hqTi_b])
                pend.append((ti, hqTi, hqTi_b, xti, xti_b, xlni, xlni_b))
                if len(pend) > 1:
                    emit_down(pend.pop(0))
                if ti + 2 < NTT:
                    xload(ti + 2)
            while pend:
                emit_down(pend.pop(0))
            bld.flush_phase(tag)

        with contextlib.ExitStack() as wstack:
            wgqT = sb("wgqT", [128, DC, DI], BF16, wstack, is_transpose_dest=True)
            wuqT = sb("wuqT", [128, DC, DI], BF16, wstack, is_transpose_dest=True)
            wdqT = sb("wdqT", [128, IC, D], BF16, wstack, is_transpose_dest=True)
            wgqT_hb = [Buf("wg0"), Buf("wg1")]
            wuqT_hb = [Buf("wu0"), Buf("wu1")]
            wdqT_b = Buf("wdqT")
            with contextlib.ExitStack() as fstack:
                ffn_phase("f1", x_in, xs1, wgqT, wuqT, wdqT, (wgqT_hb, wuqT_hb, wdqT_b), fstack,
                          wprep=(wg_d, wu_d, wd_d, wgc, wuc, wdc))

        with contextlib.ExitStack() as astack:
            idf = sb("idf", [128, 128], F32, astack)
            idf_b = Buf()
            maskt = sb("maskt", [TT, TT], BF16, astack)
            maskt_b = Buf()
            bld.g(lambda e: e.memset(idf[:], 0.0), writes=[idf_b])
            bld.g(lambda e: e.affine_select(out=idf[:], in_=idf[:], compare_op=Alu.not_equal,
                                            fill=1.0, base=0, pattern=[[-1, 128]], channel_multiplier=1),
                  reads=[idf_b], writes=[idf_b])
            bld.g(lambda e: e.memset(maskt[:], 1.0), writes=[maskt_b])
            bld.g(lambda e: e.affine_select(out=maskt[:], in_=maskt[:], compare_op=Alu.is_ge,
                                            fill=0.0, base=0, pattern=[[1, TT]], channel_multiplier=-1),
                  reads=[maskt_b], writes=[maskt_b])

            QT = sb("QT", [128, 4, S], BF16, astack)
            QT_b = Buf("QT")
            KT = sb("KT", [128, 4, S], BF16, astack)
            KT_b = Buf("KT")
            VH = sb("VH", [TT, NTT, H, 132], BF16, astack)
            VH_b = Buf("VH")
            so = sb("so", [TT, NTT, D], BF16, astack)
            so_b = Buf("so")
            onw = sb("onw_sb", [128, D], F32, astack)
            onw_b = Buf()
            ecol_t = sb("ecol_t", [128, NTT, 128], F32, astack)
            ecol_t_b = Buf()
            erow_t = sb("erow_t", [128, 128], F32, astack)
            erow_t_b = Buf()
            emb_t = sb("emb_t", [128, 128], F32, astack)
            emb_t_b = Buf()
            bld.dma("qS", lambda e: e.dma_start(onw[:], onw_d.ap()), writes=[onw_b])
            bld.g(lambda e: e.memset(VH[:, :, :, 128:129], 1.0), writes=[VH_b])
            with contextlib.ExitStack() as a12stack:
                ifsb = sb("ifsb", [TT, NTT, 16], F32, a12stack)
                ifsb_b = Buf()
                ir_row = sb("ir_row", [8, S], F32, a12stack)
                ir_b = Buf()
                fr_row = sb("fr_row", [8, S], F32, a12stack)
                fr_b = Buf()
                bi_s = sb("bi_s", [8, 2], F32, a12stack)
                bi_b = Buf()
                bf_s = sb("bf_s", [8, 2], F32, a12stack)
                bf_b = Buf()
                bld.dma("qS", lambda e: e.dma_start(bi_s[:, 0:1], bif_d.ap()[0:8, :]), writes=[bi_b])
                bld.dma("qS", lambda e: e.dma_start(bf_s[:, 0:1], bif_d.ap()[8:16, :]), writes=[bf_b])
                bld.v(lambda e: e.tensor_single_scalar(bi_s[:, 1:2], bi_s[:, 0:1], 1.0 / CAP, Alu.mult),
                      reads=[bi_b], writes=[bi_b])
                bld.v(lambda e: e.tensor_single_scalar(bf_s[:, 1:2], bf_s[:, 0:1], 1.0 / CAP, Alu.mult),
                      reads=[bf_b], writes=[bf_b])

                with contextlib.ExitStack() as pstack:
                    wqs = sb("wqs", [128, DC, 512], BF16, pstack)
                    wqs_b = Buf()
                    wks = sb("wks", [128, DC, 512], BF16, pstack)
                    wks_b = Buf()
                    wvs = sb("wvs", [128, DC, D], BF16, pstack)
                    wvs_b = Buf()
                    wos = sb("wos", [128, DC, D], BF16, pstack)
                    wos_b = Buf()
                    wifs = sb("wifs", [128, DC, 16], BF16, pstack)
                    wifs_b = Buf()
                    xT2t = [sb(f"xT2t{i}", [128, DC, TT], BF16, pstack, is_transpose_dest=True) for i in range(2)]
                    xT2t_b = [Buf() for _ in range(2)]
                    xt = [sb(f"a1xt{i}", [TT, D], F32, pstack) for i in range(2)]
                    xt_b = [Buf() for _ in range(2)]
                    xln = [sb(f"a1xln{i}", [TT, D], BF16, pstack) for i in range(2)]
                    xln_b = [Buf() for _ in range(2)]
                    st_t = sb("a1st", [TT, 16], F32, pstack)
                    st_b = Buf()
                    ps_p = [psum(f"a1ps{i}", [128, 512], F32, pstack) for i in range(2)]
                    ps_p_b = [Buf() for _ in range(2)]
                    ps_v = [psum(f"a1psv{i}", [128, 512], F32, pstack) for i in range(2)]
                    ps_v_b = [Buf() for _ in range(2)]
                    ps_if = psum("a1psif", [128, 16], F32, pstack)
                    ps_if_b = Buf()
                    ps_t = psum("a1pst", [16, 128], F32, pstack)
                    ps_t_b = Buf()
                    bld.dma("qG", lambda e: e.dma_start(wqs[:], wq_d.ap().rearrange("(j p) q -> p j q", p=128)), writes=[wqs_b])
                    bld.dma("qG", lambda e: e.dma_start(wks[:], wk_d.ap().rearrange("(j p) q -> p j q", p=128)), writes=[wks_b])
                    bld.dma("qG", lambda e: e.dma_start(wvs[:], wv_d.ap().rearrange("(j p) q -> p j q", p=128)), writes=[wvs_b])
                    bld.dma("qG", lambda e: e.dma_start(wos[:], wo_d.ap().rearrange("(j p) q -> p j q", p=128)), writes=[wos_b])
                    bld.dma("qG", lambda e: e.dma_start(wifs[:], wif_d.ap().rearrange("(j p) q -> p j q", p=128)), writes=[wifs_b])
                    def a1_xside(ti):
                        xti, xti_b = xt[ti % 2], xt_b[ti % 2]
                        xlni, xlni_b = xln[ti % 2], xln_b[ti % 2]
                        xTt, xTt_b = xT2t[ti % 2], xT2t_b[ti % 2]
                        bld.dma("qG", lambda e, ti=ti, x=xti: e.dma_start(x[:], xs1.ap()[ti * TT:(ti + 1) * TT, :]),
                                writes=[xti_b])
                        bld.v(lambda e, x=xti, s=st_t: e.bn_stats(s[:, 0:6], x[:, 0:512]), reads=[xti_b], writes=[st_b])
                        bld.v(lambda e, x=xti, s=st_t: e.bn_stats(s[:, 6:12], x[:, 512:1024]), reads=[xti_b], writes=[st_b])
                        bld.v(lambda e, s=st_t: e.bn_aggr(s[:, 12:14], s[:, 0:12]), reads=[st_b], writes=[st_b])
                        bld.s(lambda e, s=st_t: e.activation(s[:, 14:15], s[:, 13:14], Act.Sqrt, bias=cst[:, 0:1]),
                              reads=[st_b, cst_b], writes=[st_b])
                        bld.v(lambda e, s=st_t: e.reciprocal(s[:, 15:16], s[:, 14:15]), reads=[st_b], writes=[st_b])
                        bld.v(lambda e, x=xti, o=xlni, s=st_t: e.tensor_scalar(o[:], x[:], s[:, 12:13], s[:, 15:16], Alu.subtract, Alu.mult),
                              reads=[xti_b, st_b], writes=[xlni_b])
                        bld.dma("qS" if ti % 2 == 0 else "qA",
                                lambda e, a=xlni, b=xTt: e.dma_start(
                                    b[:], a[:].rearrange("p (j q) -> p j q", q=128), transpose=True),
                                reads=[xlni_b], writes=[xTt_b])

                    a1_xside(0)
                    for ti in range(NTT):
                        xTt, xTt_b = xT2t[ti % 2], xT2t_b[ti % 2]
                        if ti + 1 < NTT:
                            a1_xside(ti + 1)
                        for gq in range(4):
                            pp, pp_b = ps_p[gq % 2], ps_p_b[gq % 2]
                            for j in range(DC):
                                bld.t(lambda e, j=j, gq=gq, pp=pp, a=xTt: e.matmul(
                                        pp[:, 0:TT], wqs[:, j, gq * 128:(gq + 1) * 128], a[:, j, :],
                                        start=(j == 0), stop=(j == DC - 1)),
                                      reads=[wqs_b, xTt_b], writes=[pp_b], track=(j == DC - 1))
                            bld.v(lambda e, gq=gq, ti=ti, pp=pp: e.tensor_copy(QT[:, gq, ti * TT:(ti + 1) * TT], pp[:, 0:TT]),
                                  reads=[pp_b], writes=[QT_b])
                            pp, pp_b = ps_p[(gq + 1) % 2], ps_p_b[(gq + 1) % 2]
                            for j in range(DC):
                                bld.t(lambda e, j=j, gq=gq, pp=pp, a=xTt: e.matmul(
                                        pp[:, 0:TT], wks[:, j, gq * 128:(gq + 1) * 128], a[:, j, :],
                                        start=(j == 0), stop=(j == DC - 1)),
                                      reads=[wks_b, xTt_b], writes=[pp_b], track=(j == DC - 1))
                            bld.v(lambda e, gq=gq, ti=ti, pp=pp: e.tensor_copy(KT[:, gq, ti * TT:(ti + 1) * TT], pp[:, 0:TT]),
                                  reads=[pp_b], writes=[KT_b])
                        for j in range(DC):
                            for half in range(2):
                                bld.t(lambda e, j=j, half=half, pp=ps_v[half], a=xTt: e.matmul(
                                        pp[:], a[:, j, :], wvs[:, j, half * 512:(half + 1) * 512],
                                        start=(j == 0), stop=(j == DC - 1)),
                                      reads=[wvs_b, xTt_b], writes=[ps_v_b[half]], track=(j == DC - 1))
                                bld.t(lambda e, j=j, half=half, pp=ps_p[half], a=xTt: e.matmul(
                                        pp[:], a[:, j, :], wos[:, j, half * 512:(half + 1) * 512],
                                        start=(j == 0), stop=(j == DC - 1)),
                                      reads=[wos_b, xTt_b], writes=[ps_p_b[half]], track=(j == DC - 1))
                            bld.t(lambda e, j=j, a=xTt: e.matmul(
                                    ps_if[:], a[:, j, :], wifs[:, j, :],
                                    start=(j == 0), stop=(j == DC - 1)),
                                  reads=[wifs_b, xTt_b], writes=[ps_if_b], track=(j == DC - 1))
                        for half in range(2):
                            bld.v(lambda e, ti=ti, half=half, pp=ps_v[half]: e.tensor_copy(
                                    VH[:, ti, half * 4:(half + 1) * 4, 0:128], pp[:].rearrange("p (h v) -> p h v", v=128)),
                                  reads=[ps_v_b[half]], writes=[VH_b])
                            bld.s(lambda e, ti=ti, half=half, pp=ps_p[half]: e.activation(
                                    so[:, ti, half * 512:(half + 1) * 512], pp[:], Act.Sigmoid),
                                  reads=[ps_p_b[half]], writes=[so_b])
                        bld.v(lambda e, ti=ti: e.tensor_copy(ifsb[:, ti, :], ps_if[:]),
                              reads=[ps_if_b], writes=[ifsb_b])
                        bld.t(lambda e, ti=ti: e.transpose(ps_t[0:8, :], ifsb[:, ti, 0:8], idf[:]),
                              reads=[ifsb_b, idf_b], writes=[ps_t_b])
                        bld.v(lambda e, ti=ti: e.tensor_copy(ir_row[:, ti * TT:(ti + 1) * TT], ps_t[0:8, :]),
                              reads=[ps_t_b], writes=[ir_b])
                        bld.t(lambda e, ti=ti: e.transpose(ps_t[0:8, :], ifsb[:, ti, 8:16], idf[:]),
                              reads=[ifsb_b, idf_b], writes=[ps_t_b])
                        bld.v(lambda e, ti=ti: e.tensor_copy(fr_row[:, ti * TT:(ti + 1) * TT], ps_t[0:8, :]),
                              reads=[ps_t_b], writes=[fr_b])
                    bld.flush_phase("a1")
                gr_row = sb("gr_row", [8, S], F32, a12stack)
                gr_b = Buf()
                tmp_row = sb("tmp_row", [8, S], F32, a12stack)
                tmp_b = Buf()
                lfcr = sb("lfcr", [8, S], F32, a12stack)
                lfcr_b = Buf()
                Mr = sb("Mr", [8, S], F32, a12stack)
                Mr_b = Buf()
                M_loc = sb("M_loc", [128, TT], F32, a12stack)
                M_loc_b = Buf()
                nm_loc = sb("nm_loc", [128, TT], F32, a12stack)
                nm_loc_b = Buf()
                g_rep = sb("g_rep", [128, S], F32, a12stack)
                g_rep_b = Buf()
                PL = sb("PL", [128, 2], F32, a12stack)
                PL_b = Buf()
                ps_tr = psum("a2pst", [128, 128], F32, a12stack)
                ps_tr_b = Buf()
                bld.s(lambda e: e.activation(ir_row[:], ir_row[:], Act.Tanh, bias=bi_s[:, 1:2], scale=1.0 / CAP),
                      reads=[ir_b, bi_b], writes=[ir_b])
                bld.v(lambda e: e.tensor_single_scalar(ir_row[:], ir_row[:], CAP, Alu.mult),
                      reads=[ir_b], writes=[ir_b])
                bld.s(lambda e: e.activation(fr_row[:], fr_row[:], Act.Tanh, bias=bf_s[:, 1:2], scale=1.0 / CAP),
                      reads=[fr_b, bf_b], writes=[fr_b])
                bld.v(lambda e: e.tensor_single_scalar(fr_row[:], fr_row[:], CAP, Alu.mult),
                      reads=[fr_b], writes=[fr_b])
                bld.s(lambda e: e.activation(tmp_row[:], fr_row[:], Act.Abs),
                      reads=[fr_b], writes=[tmp_b])
                bld.s(lambda e: e.activation(tmp_row[:], tmp_row[:], Act.Exp, scale=-1.0),
                      reads=[tmp_b], writes=[tmp_b])
                bld.s(lambda e: e.activation(tmp_row[:], tmp_row[:], Act.Ln, bias=1.0),
                      reads=[tmp_b], writes=[tmp_b])
                bld.v(lambda e: e.scalar_tensor_tensor(tmp_row[:], fr_row[:], 0.0, tmp_row[:], Alu.min, Alu.subtract),
                      reads=[fr_b, tmp_b], writes=[tmp_b])
                bld.v(lambda e: e.tensor_tensor_scan(lfcr[:], tmp_row[:], tmp_row[:], 0.0, Alu.add, Alu.bypass),
                      reads=[tmp_b], writes=[lfcr_b])
                bld.v(lambda e: e.tensor_sub(gr_row[:], ir_row[:], lfcr[:]),
                      reads=[ir_b, lfcr_b], writes=[gr_b])
                bld.v(lambda e: e.tensor_tensor_scan(Mr[:], gr_row[:], gr_row[:], -1e30, Alu.max, Alu.bypass),
                      reads=[gr_b], writes=[Mr_b])
                bld.v(lambda e: e.scalar_tensor_tensor(tmp_row[:], lfcr[:], -1.0, Mr[:], Alu.mult, Alu.subtract),
                      reads=[lfcr_b, Mr_b], writes=[tmp_b])
                for i in range(NTT):
                    bld.dma("qS", lambda e, i=i: e.dma_start(M_loc[i * 8:(i + 1) * 8, :], Mr[:, i * TT:(i + 1) * TT]),
                            reads=[Mr_b], writes=[M_loc_b])
                    bld.dma("qA", lambda e, i=i: e.dma_start(nm_loc[i * 8:(i + 1) * 8, :], tmp_row[:, i * TT:(i + 1) * TT]),
                            reads=[tmp_b], writes=[nm_loc_b])
                    bld.dma("qG", lambda e, i=i: e.dma_start(g_rep[i * 8:(i + 1) * 8, :], gr_row[:]),
                            reads=[gr_b], writes=[g_rep_b])
                bld.v(lambda e: e.tensor_copy(PL[:, 0:1], M_loc[:, TT - 1:TT]), reads=[M_loc_b], writes=[PL_b])
                bld.v(lambda e: e.tensor_single_scalar(PL[:, 1:2], PL[:, 0:1], LN8, Alu.add), reads=[PL_b], writes=[PL_b])
                bld.v(lambda e: e.tensor_scalar(g_rep[:], g_rep[:], PL[:, 1:2], 0.0, Alu.subtract, Alu.min),
                      reads=[g_rep_b, PL_b], writes=[g_rep_b])
                bld.s(lambda e: e.activation(g_rep[:], g_rep[:], Act.Exp), reads=[g_rep_b], writes=[g_rep_b])
                for tau in range(NTT):
                    bld.t(lambda e, tau=tau: e.transpose(ps_tr[:], g_rep[:, tau * TT:(tau + 1) * TT], idf[:]),
                          reads=[g_rep_b, idf_b], writes=[ps_tr_b])
                    bld.v(lambda e, tau=tau: e.tensor_copy(ecol_t[:, tau, :], ps_tr[:]),
                          reads=[ps_tr_b], writes=[ecol_t_b])
                bld.v(lambda e: e.tensor_scalar(M_loc[:], M_loc[:], PL[:, 0:1], -85.0, Alu.subtract, Alu.max),
                      reads=[M_loc_b, PL_b], writes=[M_loc_b])
                bld.s(lambda e: e.activation(M_loc[:], M_loc[:], Act.Exp, scale=-1.0), reads=[M_loc_b], writes=[M_loc_b])
                bld.t(lambda e: e.transpose(ps_tr[:], M_loc[:], idf[:]), reads=[M_loc_b, idf_b], writes=[ps_tr_b])
                bld.v(lambda e: e.tensor_copy(erow_t[:], ps_tr[:]), reads=[ps_tr_b], writes=[erow_t_b])
                bld.s(lambda e: e.activation(nm_loc[:], nm_loc[:], Act.Exp), reads=[nm_loc_b], writes=[nm_loc_b])
                bld.t(lambda e: e.transpose(ps_tr[:], nm_loc[:], idf[:]), reads=[nm_loc_b, idf_b], writes=[ps_tr_b])
                bld.v(lambda e: e.tensor_copy(emb_t[:], ps_tr[:]), reads=[ps_tr_b], writes=[emb_t_b])
                bld.flush_phase("a2")

            with contextlib.ExitStack() as astack3:
                wdns = sb("wdns", [128, DC, D], BF16, astack3)
                wdns_b = Buf()
                bld.dma("qG", lambda e: e.dma_start(wdns[:], wdn_d.ap().rearrange("(j p) q -> p j q", p=128)),
                        writes=[wdns_b])
                ct = [sb(f"ct{i}", [TT, 512], BF16, astack3) for i in range(2)]
                ct_b = [Buf() for _ in range(2)]
                acat = sb("acat", [TT, 4, H, 132], F32, astack3)
                acat_b = Buf()
                nsm = sb("nsm", [TT, 8 * 8], F32, astack3)
                nsm_b = Buf()
                sqt = sb("sqt", [TT, H, 128], F32, astack3)
                sqt_b = Buf()
                hub = sb("hub", [TT, H, 128], F32, astack3)
                hub_b = Buf()
                wsr = sb("wsr", [TT, D], F32, astack3)
                wsr_b = Buf()
                hg = [sb(f"hg{i}", [TT, D], BF16, astack3) for i in range(2)]
                hg_b = [Buf() for _ in range(2)]
                hgT = [sb(f"hgT{i}", [128, DC, TT], BF16, astack3, is_transpose_dest=True) for i in range(2)]
                hgT_b = [Buf() for _ in range(2)]
                x1t = [sb(f"x1t{i}", [TT, D], F32, astack3) for i in range(2)]
                x1t_b = [Buf() for _ in range(2)]
                x2t = [sb(f"x2t{i}", [TT, D], F32, astack3) for i in range(2)]
                x2t_b = [Buf() for _ in range(2)]
                ps_sc = [psum(f"pssc{i}", [TT, 512], F32, astack3) for i in range(2)]
                ps_sc_b = [Buf() for _ in range(2)]
                ps_a = [psum(f"psa{i}", [TT, 132], F32, astack3) for i in range(4)]
                ps_a_b = [Buf() for _ in range(4)]
                ps_o = [psum(f"pso{i}", [TT, 512], F32, astack3) for i in range(2)]
                ps_o_b = [Buf() for _ in range(2)]

                for i4 in range(4):
                    for h in range(H):
                        gq, sub = h // 2, (h % 2) * 64

                        def emit_scores(tau, i4=i4, gq=gq, sub=sub):
                            cb = tau % 2
                            psc, psc_b = ps_sc[cb], ps_sc_b[cb]
                            bld.t(lambda e, i4=i4, tau=tau, gq=gq, sub=sub, psc=psc: e.matmul(
                                    psc[:], KT[sub:sub + 64, gq, tau * TT:(tau + 1) * TT],
                                    QT[sub:sub + 64, gq, i4 * 512:(i4 + 1) * 512], start=True, stop=True),
                                  reads=[KT_b, QT_b], writes=[psc_b])

                        emit_scores(0)
                        for tau in range(4 * i4 + 4):
                            cb = tau % 2
                            psc, psc_b = ps_sc[cb], ps_sc_b[cb]
                            if tau + 1 < 4 * i4 + 4:
                                emit_scores(tau + 1)
                            for il in range(4):
                                i = 4 * i4 + il
                                if tau > i:
                                    continue
                                col = i * 8 + h
                                if tau == i:
                                    bld.v(lambda e, tau=tau, col=col, il=il, cbuf=ct[cb], psc=psc: e.scalar_tensor_tensor(
                                            cbuf[:, il * TT:(il + 1) * TT], psc[:, il * TT:(il + 1) * TT],
                                            ecol_t[:, tau, col:col + 1], maskt[:], Alu.mult, Alu.mult),
                                          reads=[psc_b, ecol_t_b, maskt_b], writes=[ct_b[cb]])
                                elif il % 2 == 0:
                                    bld.s(lambda e, tau=tau, col=col, il=il, cbuf=ct[cb], psc=psc: e.activation(
                                            cbuf[:, il * TT:(il + 1) * TT], psc[:, il * TT:(il + 1) * TT],
                                            Act.Copy, scale=ecol_t[:, tau, col:col + 1]),
                                          reads=[psc_b, ecol_t_b], writes=[ct_b[cb]])
                                else:
                                    bld.v(lambda e, tau=tau, col=col, il=il, cbuf=ct[cb], psc=psc: e.tensor_scalar(
                                            cbuf[:, il * TT:(il + 1) * TT], psc[:, il * TT:(il + 1) * TT],
                                            ecol_t[:, tau, col:col + 1], None, Alu.mult),
                                          reads=[psc_b, ecol_t_b], writes=[ct_b[cb]])
                                bld.t(lambda e, i=i, tau=tau, h=h, il=il, cbuf=ct[cb], pa=ps_a[il]: e.matmul(
                                        pa[:, 0:129], cbuf[:, il * TT:(il + 1) * TT], VH[:, tau, h, 0:129],
                                        start=(tau == 0), stop=(tau == i)),
                                      reads=[ct_b[cb], VH_b], writes=[ps_a_b[il]], track=(tau == i))
                        for il in range(4):
                            col = (4 * i4 + il) * 8 + h
                            bld.v(lambda e, h=h, il=il, col=col, pa=ps_a[il]: e.tensor_scalar(
                                    acat[:, il, h, 0:129], pa[:, 0:129], erow_t[:, col:col + 1], None, Alu.mult),
                                  reads=[ps_a_b[il], erow_t_b], writes=[acat_b])
                    for il in range(4):
                        i = 4 * i4 + il
                        bld.dma("qG", lambda e, i=i, x=x1t[i % 2]: e.dma_start(x[:], xs1.ap()[i * TT:(i + 1) * TT, :]),
                                writes=[x1t_b[i % 2]])
                        c0 = i * 8
                        rsv = acat[:, il, :, 128:129]
                        bld.v(lambda e, rsv=rsv: e.tensor_scalar(nsm[:, 0:8], rsv, -1.0, None, Alu.mult),
                              reads=[acat_b], writes=[nsm_b])
                        bld.v(lambda e, rsv=rsv: e.tensor_tensor(nsm[:, 0:8].rearrange("p (h o) -> p h o", o=1), rsv,
                                                        nsm[:, 0:8].rearrange("p (h o) -> p h o", o=1), Alu.max),
                              reads=[acat_b, nsm_b], writes=[nsm_b])
                        bld.v(lambda e, c0=c0: e.tensor_tensor(nsm[:, 0:8], nsm[:, 0:8], emb_t[:, c0:c0 + 8], Alu.max),
                              reads=[nsm_b, emb_t_b], writes=[nsm_b])
                        bld.v(lambda e: e.tensor_single_scalar(nsm[:, 0:8], nsm[:, 0:8], ATT_EPS, Alu.add),
                              reads=[nsm_b], writes=[nsm_b])
                        bld.v(lambda e: e.reciprocal(nsm[:, 8:16], nsm[:, 0:8]), reads=[nsm_b], writes=[nsm_b])
                        av = acat[:, il, :, 0:128]
                        bld.v(lambda e, av=av: e.tensor_reduce(nsm[:, 16:24].rearrange("p (h o) -> p h o", o=1), av, Ax.X, Alu.add),
                              reads=[acat_b], writes=[nsm_b])
                        bld.s(lambda e, av=av: e.activation(sqt[:], av, Act.Square), reads=[acat_b], writes=[sqt_b])
                        bld.v(lambda e: e.tensor_reduce(nsm[:, 24:32].rearrange("p (h o) -> p h o", o=1), sqt[:], Ax.X, Alu.add),
                              reads=[sqt_b], writes=[nsm_b])
                        bld.v(lambda e: e.tensor_single_scalar(nsm[:, 16:24], nsm[:, 16:24], 1.0 / 128.0, Alu.mult),
                              reads=[nsm_b], writes=[nsm_b])
                        bld.v(lambda e: e.tensor_single_scalar(nsm[:, 24:32], nsm[:, 24:32], 1.0 / 128.0, Alu.mult),
                              reads=[nsm_b], writes=[nsm_b])
                        bld.v(lambda e: e.tensor_tensor(nsm[:, 32:40], nsm[:, 16:24], nsm[:, 16:24], Alu.mult),
                              reads=[nsm_b], writes=[nsm_b])
                        bld.v(lambda e: e.tensor_tensor(nsm[:, 24:32], nsm[:, 24:32], nsm[:, 32:40], Alu.subtract),
                              reads=[nsm_b], writes=[nsm_b])
                        bld.v(lambda e: e.tensor_tensor(nsm[:, 32:40], nsm[:, 8:16], nsm[:, 8:16], Alu.mult),
                              reads=[nsm_b], writes=[nsm_b])
                        bld.v(lambda e: e.tensor_tensor(nsm[:, 24:32], nsm[:, 24:32], nsm[:, 32:40], Alu.mult),
                              reads=[nsm_b], writes=[nsm_b])
                        bld.s(lambda e: e.activation(nsm[:, 32:40], nsm[:, 24:32], Act.Sqrt, bias=cst[:, 1:2]),
                              reads=[nsm_b, cst_b], writes=[nsm_b])
                        bld.v(lambda e: e.reciprocal(nsm[:, 40:48], nsm[:, 32:40]), reads=[nsm_b], writes=[nsm_b])
                        bld.v(lambda e: e.tensor_tensor(nsm[:, 40:48], nsm[:, 40:48], nsm[:, 8:16], Alu.mult),
                              reads=[nsm_b], writes=[nsm_b])
                        mu3 = nsm[:, 16:24].rearrange("p (h o) -> p h o", o=1)
                        bld.v(lambda e, av=av: e.tensor_tensor(hub[:], av, bcast(av, mu3), Alu.subtract),
                              reads=[acat_b, nsm_b], writes=[hub_b])
                        w3 = wsr[:].rearrange("p (h v) -> p h v", v=128)
                        rr3 = nsm[:, 40:48].rearrange("p (h o) -> p h o", o=1)
                        bld.g(lambda e, i=i: e.tensor_tensor(wsr[:], onw[:], so[:, i, :], Alu.mult),
                              reads=[onw_b, so_b], writes=[wsr_b])
                        bld.g(lambda e: e.tensor_tensor(w3, w3, bcast(w3, rr3), Alu.mult),
                              reads=[wsr_b, nsm_b], writes=[wsr_b])
                        hgi, hgi_b = hg[i % 2], hg_b[i % 2]
                        bld.v(lambda e, o=hgi: e.tensor_tensor(o[:].rearrange("p (h v) -> p h v", v=128), hub[:], w3, Alu.mult),
                              reads=[hub_b, wsr_b], writes=[hgi_b])
                        hgTi, hgTi_b = hgT[i % 2], hgT_b[i % 2]
                        bld.dma("qS" if i % 2 == 0 else "qA",
                                lambda e, a=hgi, b=hgTi: e.dma_start(
                                    b[:], a[:].rearrange("p (j q) -> p j q", q=128), transpose=True),
                                reads=[hgi_b], writes=[hgTi_b])
                        for j in range(DC):
                            for half in range(2):
                                po, po_b = ps_o[half], ps_o_b[half]
                                bld.t(lambda e, j=j, half=half, a=hgTi, po=po: e.matmul(
                                        po[:], a[:, j, :], wdns[:, j, half * 512:(half + 1) * 512],
                                        start=(j == 0), stop=(j == DC - 1)),
                                      reads=[hgTi_b, wdns_b], writes=[po_b], track=(j == DC - 1))
                        x2i, x2i_b = x2t[i % 2], x2t_b[i % 2]
                        for half in range(2):
                            po, po_b = ps_o[half], ps_o_b[half]
                            bld.v(lambda e, half=half, x=x1t[i % 2], o=x2i, po=po: e.tensor_tensor(
                                    o[:, half * 512:(half + 1) * 512], po[:], x[:, half * 512:(half + 1) * 512], Alu.add),
                                  reads=[po_b, x1t_b[i % 2]], writes=[x2i_b])
                        bld.dma("qS", lambda e, i=i, o=x2i: e.dma_start(xs2.ap()[i * TT:(i + 1) * TT, :], o[:]),
                                reads=[x2i_b])
                bld.flush_phase("a3")

        with contextlib.ExitStack() as w2stack:
            wgqT2 = sb("wgqT2", [128, DC, DI], BF16, w2stack)
            wuqT2 = sb("wuqT2", [128, DC, DI], BF16, w2stack)
            wdqT2 = sb("wdqT2", [128, IC, D], BF16, w2stack)
            wgqT2_hb = [Buf(), Buf()]
            wuqT2_hb = [Buf(), Buf()]
            wdqT2_b = Buf()
            bld.dma("qS", lambda e: e.dma_start(wgqT2[:], wgc.ap().rearrange("(j p) i -> p j i", p=128)),
                    writes=[wgqT2_hb[0], wgqT2_hb[1]])
            bld.dma("qA", lambda e: e.dma_start(wuqT2[:], wuc.ap().rearrange("(j p) i -> p j i", p=128)),
                    writes=[wuqT2_hb[0], wuqT2_hb[1]])
            bld.dma("qS", lambda e: e.dma_start(wdqT2[:], wdc.ap().rearrange("(j p) i -> p j i", p=128)), writes=[wdqT2_b])
            with contextlib.ExitStack() as fstack2:
                ffn_phase("f2", xs2, out_d, wgqT2, wuqT2, wdqT2, (wgqT2_hb, wuqT2_hb, wdqT2_b), fstack2)

    return nc


def kernel(L0_attn_x, L0_attn_Wq, L0_attn_Wk, L0_attn_Wv, L0_attn_Wo, L0_attn_Wi,
           L0_attn_bi, L0_attn_Wf, L0_attn_bf, L0_attn_outnorm_w, L0_attn_Wdown,
           L0_ffn_w_gate, L0_ffn_w_up, L0_ffn_w_down):
    f32 = lambda a: np.ascontiguousarray(np.asarray(a), dtype=np.float32)
    x = f32(L0_attn_x)
    wif = np.ascontiguousarray(np.concatenate([f32(L0_attn_Wi), f32(L0_attn_Wf)], axis=1))
    bif = np.ascontiguousarray(np.concatenate([f32(L0_attn_bi), f32(L0_attn_bf)]).reshape(16, 1))
    onw = np.ascontiguousarray(np.broadcast_to(f32(L0_attn_outnorm_w).reshape(1, D), (128, D)))
    base = {
        "wq": f32(L0_attn_Wq), "wk": f32(L0_attn_Wk), "wv": f32(L0_attn_Wv),
        "wo": f32(L0_attn_Wo), "wif": wif, "bif": bif, "onw": onw,
        "wdn": f32(L0_attn_Wdown), "wg": f32(L0_ffn_w_gate), "wu": f32(L0_ffn_w_up),
        "wd": f32(L0_ffn_w_down),
    }
    in_maps = [dict(base, x=np.ascontiguousarray(x[c])) for c in range(8)]
    nc = build_core_program()
    res = run_bass_kernel_spmd(nc, in_maps, core_ids=list(range(8)))
    return np.stack([np.asarray(r["out"], dtype=np.float32) for r in res.results], axis=0)
